# version 05
# baseline (speedup 1.0000x reference)
import contextlib
import numpy as np

import concourse.bass as bass
import concourse.mybir as mybir
from concourse.bass_utils import run_bass_kernel_spmd

F32 = mybir.dt.float32
FP16 = mybir.dt.float16
AF = mybir.ActivationFunctionType
ALU = mybir.AluOpType
AX = mybir.AxisListType

RND = 12582912.0
PI = float(np.float32(np.pi))
HALF_PI = float(np.float32(np.pi / 2))
TWO_PI = float(np.float32(2 * np.pi))
INV_2PI = float(np.float32(1.0 / (2 * np.pi)))
CW_C1 = 6.28125
CW_C2 = float(np.float32(2 * np.pi - 6.28125))

T = 2048
HF = 1024
D = 2048
H = 8192
NCH = 16
HCH = 64
TCOL = 4
TB = 1024
LN_EPS = 1e-5
NW = float(H * D)


def hsl(i):
    return slice(i * HF, (i + 1) * HF)


class Eng:
    def __init__(self, owner, name, step=1):
        self.o = owner
        self.name = name
        self.step = step
        self.cnt = 0
        self.real = None
        self.sem = None

    def op(self, emit_fn, waits=(), label=None):
        self.cnt += self.step
        if self.o.mode == "count":
            if label is not None:
                key = (self.name, label)
                assert key not in self.o.tok, f"dup label {key}"
                self.o.tok[key] = self.cnt
        else:
            if label is not None:
                exp = self.o.tok[(self.name, label)]
                assert exp == self.cnt, (
                    f"token drift at {self.name}:{label}: table {exp}, "
                    f"emit {self.cnt}")
            for s_eng, lab in waits:
                self.real.wait_ge(s_eng.sem, self.o.tok[(s_eng.name, lab)])
            ins = emit_fn(self.real)
            ins.then_inc(self.sem, self.step)
        return self.cnt


class _SecE:
    def __init__(self, b, engobj):
        self.b = b
        self.engobj = engobj

    def wait(self, s_eng, lab):
        self.engobj.wait_ge(s_eng.sem, self.b.tok[(s_eng.name, lab)])

    def wait_val(self, s_eng, val):
        self.engobj.wait_ge(s_eng.sem, val)


class _FakeE:
    def wait(self, *a, **k):
        pass

    def wait_val(self, *a, **k):
        pass


class Builder:
    def __init__(self, taps=()):
        self.taps = taps
        self.mode = "count"
        self.tok = {}
        nc = self.nc = bass.Bass("TRN2", target_bir_lowering=False,
                                 dynamic_dma_scratch_size=16384)
        self.stack = contextlib.ExitStack()

        din = lambda n, s, dt=F32: nc.dram_tensor(n, s, dt, kind="ExternalInput").ap()
        self.XT = din("XT", [D, T])
        self.WinT = din("WinT", [48, D, 128])
        self.WoT = din("WoT", [16, D, 128])
        self.GwT = din("GwT", [64, D, 128])
        self.VwT = din("VwT", [64, D, 128])
        self.OwT = din("OwT", [64, D, 128])
        self.b_inT = din("b_inT", [128, 48])
        self.b_outT = din("b_outT", [128, 16])
        self.gate_bT = din("gate_bT", [128, 64])
        self.val_bT = din("val_bT", [128, 64])
        self.out_bT = din("out_bT", [128, 16])
        self.ramp = din("ramp", [1, T])
        self.invfF = din("invfF", [128, 8])
        self.onesR = din("onesR", [1, 128])
        self.onesC = din("onesC", [128, 1])
        self.eye = din("eye", [128, 128])
        self.perm = din("perm", [128, 128])
        self.bsel = din("bsel", [128, 2])
        self.zrowD = din("zrowD", [1, T])
        self.gW8 = din("gW8", [8, D, 128])
        self.vW8 = din("vW8", [8, D, 128])
        self.oW8 = din("oW8", [8, D, 128])

        self.OUT = nc.dram_tensor("OUT", [D, T], F32, kind="ExternalOutput").ap()
        dint = lambda n, s, dt=F32: nc.dram_tensor(n, s, dt, kind="Internal").ap()
        self.x2pre_sp = dint("x2pre_sp", [D, T])
        self.x2_sp = dint("x2_sp", [D, T])
        self.h_sp = dint("h_sp", [HCH, 128, T])
        self.owq_cache = dint("owq_cache", [16, 128, H], FP16)
        self.o3_sp = dint("o3_sp", [D, T])
        self.tab_sp = dint("tab_sp", [8, 2, 2, 128, HF])
        self.ccin = dint("ccin", [4, T])
        self.ccout = dint("ccout", [4, T])
        self.bounce = dint("bounce", [8, T])
        self.tap_outs = {}
        for tp in taps:
            shp = {"A": [128, 16], "gvals": [1, 8], "x2": [D, T],
                   "x2pre": [D, T], "h": [HCH * 128, T],
                   "kv": [128, NCH * T], "srow": [1, T],
                   "arena": [128, NCH * T], "k0": [128, T],
                   "v0": [128, T], "rope0": [128, T],
                   "nsq": [128, T], "sq00": [128, HF],
                   "nsq0": [128, T],
                   **{f"gc{n}": [128, T] for n in range(16)},
                   **{f"hc{n}": [128, HF] for n in range(16)},
                   "rampend": [128, T], "ka3": [128, T],
                   "kr3": [128, T]}[tp]
            dtt = FP16 if tp in ("kv", "arena") else F32
            self.tap_outs[tp] = nc.dram_tensor("tap_" + tp, shp, dtt,
                                               kind="ExternalOutput").ap()

        sb = lambda n, s, dt: self.stack.enter_context(nc.sbuf_tensor(n, s, dt))
        self.arena = sb("arena", [128, NCH * T], FP16)
        self.bigbuf = sb("bigbuf", [128, NCH * T], FP16)
        self.bin_t = sb("bin_t", [128, 48], F32)
        self.bout_t = sb("bout_t", [128, 16], F32)
        self.gb_t = sb("gb_t", [128, 64], F32)
        self.vb_t = sb("vb_t", [128, 64], F32)
        self.ob_t = sb("ob_t", [128, 16], F32)
        self.invf_t = sb("invf_t", [128, 8], F32)
        self.onesR_t = sb("onesR_t", [1, 128], F32)
        self.onesC_t = sb("onesC_t", [128, 1], F32)
        self.eye_t = sb("eye_t", [128, 128], F32)
        self.perm_t = sb("perm_t", [128, 128], F32)
        self.bsel_t = sb("bsel_t", [128, 2], F32)
        self.acol = sb("acol", [128, 16], F32)
        self.acol2 = sb("acol2", [128, 16], F32)
        self.rbounce = sb("rbounce", [16, 128], F32)
        self.acolB = sb("acolB", [128, 16], F32)
        self.gacc = sb("gacc", [128, 4], F32)
        self.gcells = sb("gcells", [1, 4], F32)
        self.gvals = sb("gvals", [1, 8], F32)
        self.gB8 = sb("gB8", [128, 8], F32)
        self.rndP = sb("rndP", [128, 1], F32)
        self.rndN = sb("rndN", [128, 1], F32)
        for n in ["sT16", "mT16", "rT16", "siT16", "hT16", "dT16", "t16a", "t16b"]:
            setattr(self, n, sb(n, [128, 16], F32))
        self.rowbuf = sb("rowbuf", [1, 2 * T], F32)
        self.rampR = self.rowbuf[0:1, 0:T]
        self.nrowR = self.rowbuf[0:1, 0:T]
        self.ln1rows = [self.rowbuf[0:1, 0:T], self.rowbuf[0:1, T:2*T]]
        self.sHrow = self.rowbuf[0:1, 0:T]
        self.dq3row = self.rowbuf[0:1, T:2*T]

        self.psA = self.stack.enter_context(nc.psum_tensor("psA", [128, T], F32))
        self.psB = self.stack.enter_context(nc.psum_tensor("psB", [128, T], F32))
        self.PS = [self.psA, self.psB]

        self.eT = Eng(self, "eT")
        self.eV = Eng(self, "eV")
        self.eA = Eng(self, "eA")
        self.eP = Eng(self, "eP")
        names = ["d_tabW", "d_tabR",
                 "d_cst", "d_xt", "d_pre0", "d_pre1", "d_wk", "d_wv",
                 "d_wq0", "d_wq1", "d_wo0", "d_wo1", "d_xres0", "d_xres1",
                 "d_x2preW0", "d_x2preW1", "d_x2preR0", "d_x2preR1",
                 "d_x2W0", "d_x2W1", "d_x2R0", "d_x2R1",
                 "d_gw0", "d_gw1", "d_vw0", "d_vw1", "d_owp0", "d_owp1", "d_owqW",
                 "d_hW0", "d_hW1", "d_hR0", "d_hR1",
                 "d_owqR0", "d_owqR1", "d_owqR2", "d_owqR3",
                 "d_o3W0", "d_o3W1", "d_o3R0", "d_o3R1",
                 "d_out", "d_cc", "d_ccR", "d_bnc", "d_tap"]
        self.DQ = {n: Eng(self, n, 16) for n in names}
        self.era1 = None
        self.t1 = {}
        self.t2 = {}

    def open_era1(self):
        self.era1 = contextlib.ExitStack()
        e = self.era1.enter_context
        nc = self.nc
        for n in ["gA", "gB", "gC", "gD"]:
            self.t1[n] = e(nc.sbuf_tensor(n, [128, T], F32))
        for n in ["hA", "hB", "hC", "hD", "hE"]:
            self.t1[n] = e(nc.sbuf_tensor(n, [128, HF], F32))
        self.t1["wkv16"] = e(nc.sbuf_tensor("wkv16", [128, 2 * D], FP16))

    def open_era2(self):
        self.era2 = contextlib.ExitStack()
        e = self.era2.enter_context
        nc = self.nc
        t2 = self.t2
        for i in range(2):
            t2[f"owq{i}"] = e(nc.sbuf_tensor(f"owq{i}", [128, 2048], FP16))
        t2["sHB"] = e(nc.sbuf_tensor("sHB", [128, TB], F32))
        t2["dq3B"] = e(nc.sbuf_tensor("dq3B", [128, TB], F32))
        for n in ["scr4", "hrd0", "hrd1", "x2r0", "x2r1", "o3t0", "o3t1",
                  "s2sum", "s2sq", "m2B", "r2B"]:
            t2[n] = e(nc.sbuf_tensor(n, [128, TB], F32))

    def T1(self, n):
        return self.t1[n]

    def T2(self, n):
        return self.t2[n]

    def ph_p0(self):
        b = self
        eT, eV, eA, eP, DQ = b.eT, b.eV, b.eA, b.eP, b.DQ

        def sync(E):
            consts = [(b.bin_t[:], b.b_inT), (b.bout_t[:], b.b_outT),
                      (b.gb_t[:], b.gate_bT), (b.vb_t[:], b.val_bT),
                      (b.ob_t[:], b.out_bT), (b.rampR, b.ramp),
                      (b.invf_t[:], b.invfF), (b.onesR_t[:], b.onesR),
                      (b.onesC_t[:], b.onesC), (b.eye_t[:], b.eye),
                      (b.perm_t[:], b.perm), (b.bsel_t[:], b.bsel)]
            for dst, src in consts:
                DQ["d_cst"].op(lambda e, d=dst, s=src: e.dma_start(out=d, in_=s))
            for k in range(48):
                pc, ih = k // 2, k % 2
                src = [b.gW8, b.vW8, b.oW8][pc // 8][pc % 8][ih*HF:(ih+1)*HF, :]
                par = k % 2
                buf = ["hA", "hB"][par]
                wts = [(eA, f"p0.abs.{k-2}")] if k >= 2 else []
                DQ[f"d_pre{par}"].op(
                    lambda e, s=src, bu=buf: e.dma_start(
                        out=b.T1(bu)[:].rearrange("p (dc h) -> p dc h", h=128),
                        in_=s.rearrange("(dc p) h -> p dc h", p=128)),
                    waits=wts, label=f"pre.{k}")

        def gp(E):
            DQ["d_xt"].op(lambda e: e.dma_start(
                out=b.arena[:].rearrange("p (dc t) -> p dc t", t=T),
                in_=b.XT.rearrange("(dc p) t -> p dc t", p=128)), label="xt")

        def pe(E):
            E.wait_val(DQ["d_cst"], 16 * 12)
            for tc in range(TCOL):
                eT.op(lambda e, tc=tc: e.matmul(
                    b.psA[:, tc*512:(tc+1)*512], b.onesR_t[:],
                    b.rampR[0:1, tc*512:(tc+1)*512], start=True, stop=True),
                    label=("p0.rampmm" if tc == 3 else None))

        def act(E):
            E.wait(eT, "p0.rampmm")
            eA.op(lambda e: e.activation(b.T1("gD")[:], b.psA[:], AF.Identity),
                  label="p0.rampB")
            for k in range(48):
                par = k % 2
                buf = ["hA", "hB"][par]
                wts = [(DQ[f"d_pre{par}"], f"pre.{k}")]
                if k >= 1:
                    wts.append((eV, f"p0.red.{k-1}"))
                eA.op(lambda e, bu=buf: e.activation(b.T1("hC")[:], b.T1(bu)[:],
                                                     AF.Abs),
                      waits=wts, label=f"p0.abs.{k}")

        def vec(E):
            eV.op(lambda e: e.memset(b.gacc[:], 0.0), label="p0.gacc0")
            eV.op(lambda e: e.memset(b.rndP[:], RND), label="p0.rndp")
            eV.op(lambda e: e.memset(b.rndN[:], -RND), label="p0.rndn")
            for k in range(48):
                m = k // 16
                wts = [(eA, f"p0.abs.{k}")]
                if k >= 1:
                    wts.append((eV, f"p0.acc.{k-1}"))
                eV.op(lambda e: e.tensor_reduce(b.t16a[:, 0:1], b.T1("hC")[:],
                                                AX.X, ALU.add),
                      waits=wts, label=f"p0.red.{k}")
                eV.op(lambda e, m=m: e.tensor_tensor(b.gacc[:, m:m+1],
                                                     b.gacc[:, m:m+1],
                                                     b.t16a[:, 0:1], ALU.add),
                      waits=[(eV, f"p0.red.{k}"), (eV, "p0.gacc0")],
                      label=f"p0.acc.{k}")

        return {"sync": sync, "gpsimd": gp, "tensor": pe, "scalar": act,
                "vector": vec}

    def _theta_chain(self, pref, c, extra0, sq_guard, ih):
        b = self
        eV, eA = b.eV, b.eA
        if True:
            s = hsl(ih)
            wts = [(eA, "p0.rampB")]
            if ih == 0:
                wts += list(extra0)
            else:
                wts += [(eA, f"{pref}.cos.{c}.0"),
                        (eV, f"{pref}.rope.{c}.0")]
                if pref == "p1":
                    wts.append((eV, f"p1.nsq.{c}.0"))
            eV.op(lambda e, s=s: e.tensor_scalar(
                b.T1("hA")[:], b.T1("gD")[:, s], b.invf_t[:, c:c+1], None,
                ALU.mult), waits=wts, label=f"{pref}.th0.{c}.{ih}")
            eV.op(lambda e: e.tensor_scalar(b.T1("hB")[:], b.T1("hA")[:],
                                            INV_2PI, RND, ALU.mult, ALU.add),
                  waits=[(eV, f"{pref}.th0.{c}.{ih}")], label=f"{pref}.r.{c}.{ih}")
            wts = [(eV, f"{pref}.r.{c}.{ih}")]
            if sq_guard is not None and ih == 0:
                wts.append(sq_guard)
            eV.op(lambda e: e.tensor_scalar(b.T1("hC")[:], b.T1("hB")[:],
                                            -RND, CW_C2, ALU.add, ALU.mult),
                  waits=wts, label=f"{pref}.nc2.{c}.{ih}")
            eV.op(lambda e: e.tensor_scalar(b.T1("hB")[:], b.T1("hB")[:],
                                            -RND, CW_C1, ALU.add, ALU.mult),
                  waits=[(eV, f"{pref}.nc2.{c}.{ih}")], label=f"{pref}.nc1.{c}.{ih}")
            eV.op(lambda e: e.tensor_tensor(b.T1("hA")[:], b.T1("hA")[:],
                                            b.T1("hB")[:], ALU.subtract),
                  waits=[(eV, f"{pref}.nc1.{c}.{ih}")], label=f"{pref}.tha.{c}.{ih}")
            eV.op(lambda e: e.tensor_tensor(b.T1("hA")[:], b.T1("hA")[:],
                                            b.T1("hC")[:], ALU.subtract),
                  waits=[(eV, f"{pref}.tha.{c}.{ih}")], label=f"{pref}.thr0.{c}.{ih}")
            eV.op(lambda e: e.tensor_scalar(b.T1("hA")[:], b.T1("hA")[:],
                                            PI, -PI, ALU.min, ALU.max),
                  waits=[(eV, f"{pref}.thr0.{c}.{ih}")], label=f"{pref}.thr.{c}.{ih}")
            eV.op(lambda e: e.tensor_scalar(b.T1("hA")[:], b.T1("hA")[:],
                                            HALF_PI, None, ALU.add),
                  waits=[(eA, f"{pref}.sin.{c}.{ih}")], label=f"{pref}.ps0.{c}.{ih}")
            eV.op(lambda e: e.tensor_scalar(b.T1("hB")[:], b.T1("hA")[:],
                                            PI, None, ALU.is_gt),
                  waits=[(eV, f"{pref}.ps0.{c}.{ih}")], label=f"{pref}.msk.{c}.{ih}")
            eV.op(lambda e: e.scalar_tensor_tensor(
                b.T1("hA")[:], b.T1("hB")[:], -TWO_PI, b.T1("hA")[:],
                ALU.mult, ALU.add),
                waits=[(eV, f"{pref}.msk.{c}.{ih}")], label=f"{pref}.psr0.{c}.{ih}")
            eV.op(lambda e: e.tensor_scalar(b.T1("hA")[:], b.T1("hA")[:],
                                            PI, -PI, ALU.min, ALU.max),
                waits=[(eV, f"{pref}.psr0.{c}.{ih}")], label=f"{pref}.psr.{c}.{ih}")

    def _tables_act(self, pref, c, ih):
        b = self
        eV, eA = b.eV, b.eA
        pc, pih = (c, 0) if ih == 1 else (c - 1, 1)
        wts = [(eV, f"{pref}.thr.{c}.{ih}")]
        if pref == "p1" and (c, ih) != (0, 0):
            wts.append((b.DQ["d_tabW"], f"tabWs.{pc}.{pih}"))
        eA.op(lambda e: e.activation(b.T1("hE")[:], b.T1("hA")[:], AF.Sin),
              waits=wts, label=f"{pref}.sin.{c}.{ih}")
        wts = [(eV, f"{pref}.psr.{c}.{ih}")]
        if pref == "p1" and (c, ih) != (0, 0):
            wts.append((b.DQ["d_tabW"], f"tabWc.{pc}.{pih}"))
        eA.op(lambda e: e.activation(b.T1("hD")[:], b.T1("hA")[:], AF.Sin),
              waits=wts, label=f"{pref}.cos.{c}.{ih}")

    def _rope_combine(self, pref, c, G, ih):
        b = self
        eV, eA, eT = b.eV, b.eA, b.eT
        if True:
            s = hsl(ih)
            wts = [(eT, f"{pref}.perm.{c}"), (eA, f"{pref}.cos.{c}.{ih}"),
                   (eV, f"{pref}.psr.{c}.{ih}")]
            if ih == 1:
                wts.append((eV, f"{pref}.rope.{c}.0"))
            eV.op(lambda e, s=s, G=G: e.tensor_tensor(
                b.T1("hB")[:], G[:, s], b.T1("hE")[:], ALU.mult),
                waits=wts, label=f"{pref}.krs.{c}.{ih}")
            kcw = [(eV, f"{pref}.krs.{c}.{ih}")]
            if pref == "p1" and c == 0 and ih == 0 and "k0" in b.taps:
                kcw.append((b.DQ["d_tap"], "tap.k0"))
            if pref == "p1" and c == 3 and ih == 0 and "ka3" in b.taps:
                kcw.append((b.DQ["d_tap"], "tap.ka3"))
            eV.op(lambda e, s=s: e.tensor_tensor(
                b.T1("gA")[:, s], b.T1("gA")[:, s], b.T1("hD")[:], ALU.mult),
                waits=kcw, label=f"{pref}.kcs.{c}.{ih}")
            eV.op(lambda e, s=s: e.tensor_tensor(
                b.T1("gA")[:, s], b.T1("gA")[:, s], b.T1("hB")[:], ALU.add),
                waits=[(eV, f"{pref}.kcs.{c}.{ih}")], label=f"{pref}.rope.{c}.{ih}")

    def ph_p1(self):
        b = self
        eT, eV, eA, eP, DQ, PS = b.eT, b.eV, b.eA, b.eP, b.DQ, b.PS

        def sync(E):
            chain = []
            def tp(name, ap, wts):
                DQ["d_tap"].op(lambda e, ap=ap: e.dma_start(
                    out=b.tap_outs[name], in_=ap),
                    waits=wts + ([(DQ["d_tap"], chain[-1])] if chain else []),
                    label=f"tap.{name}")
                chain.append(f"tap.{name}")
            if "arena" in b.taps:
                tp("arena", b.arena[:], [(DQ["d_xt"], "xt")])
            if "k0" in b.taps:
                tp("k0", b.T1("gA")[:], [(eA, "p1.kev.0.1")])
            if "v0" in b.taps:
                tp("v0", b.T1("gB")[:], [(eA, "p1.vev.0.1")])
            prev = [None]
            for c in range(8):
                for ih in range(2):
                    for sc, buf, lab in ((0, "hE", "tabWs"), (1, "hD", "tabWc")):
                        wts = [(eA, f"p1.{'sin' if sc == 0 else 'cos'}.{c}.{ih}")]
                        if prev[0] is not None:
                            wts.append((DQ["d_tabW"], prev[0]))
                        DQ["d_tabW"].op(lambda e, c=c, ih=ih, sc=sc, buf=buf:
                                        e.dma_start(
                            out=b.tab_sp[c, ih, sc],
                            in_=b.T1(buf)[:]),
                            waits=wts, label=f"{lab}.{c}.{ih}")
                        prev[0] = f"{lab}.{c}.{ih}"
            if "rope0" in b.taps:
                tp("rope0", b.T1("gA")[:], [(eV, "p1.rope.0.1")])
            if "sq00" in b.taps:
                tp("sq00", b.T1("hC")[:], [(eA, "p1.sq.0.0")])
            if "nsq0" in b.taps:
                tp("nsq0", b.T1("gC")[:], [(eV, "p1.nsq.0.1")])
            for n in range(16):
                if n == 3:
                    if "ka3" in b.taps:
                        tp("ka3", b.T1("gA")[:], [(eA, "p1.kev.3.1")])
                    if "kr3" in b.taps:
                        tp("kr3", b.T1("gA")[:], [(eV, "p1.rope.3.1")])
                if f"hc{n}" in b.taps:
                    tp(f"hc{n}", b.T1("hC")[:], [(eA, f"p1.sq.{n}.0")])
                if f"gc{n}" in b.taps:
                    tp(f"gc{n}", b.T1("gC")[:], [(eV, f"p1.nsq.{n}.1")])
            if "rampend" in b.taps:
                tp("rampend", b.T1("gD")[:], [(eV, "p1.nsq.15.1")])

        def gp(E):
            for c in range(16):
                wts = []
                if c > 0:
                    wts += [(eT, f"p1.kmm.{c-1}"), (eT, f"p1.vmm.{c-1}")]
                DQ["d_wk"].op(lambda e, c=c: e.dma_start(
                    out=b.T1("wkv16")[:, 0:D].rearrange("p (dc e) -> p dc e", e=128),
                    in_=b.WinT[16 + c].rearrange("(dc p) e -> p dc e", p=128)),
                    waits=wts, label=f"wk.{c}")
                DQ["d_wv"].op(lambda e, c=c: e.dma_start(
                    out=b.T1("wkv16")[:, D:2*D].rearrange("p (dc e) -> p dc e", e=128),
                    in_=b.WinT[32 + c].rearrange("(dc p) e -> p dc e", p=128)),
                    waits=([(eT, f"p1.kmm.{c-1}"), (eT, f"p1.vmm.{c-1}")]
                           if c > 0 else ()), label=f"wv.{c}")

        def pe(E):
            E.wait(DQ["d_xt"], "xt")
            for c in range(16):
                G1, G2 = PS[c % 2], PS[(c + 1) % 2]
                wts = [(DQ["d_wk"], f"wk.{c}")]
                if c == 0:
                    wts.append((eA, "p0.rampB"))
                if c >= 1:
                    wts.append((eA, f"p1.vev.{c-1}.1"))
                if c >= 2 and (c - 2) < 8:
                    wts.append((eV, f"p1.rope.{c-2}.1"))
                for dch in range(NCH):
                    for tc in range(TCOL):
                        last = dch == NCH - 1 and tc == TCOL - 1
                        eT.op(lambda e, G1=G1, dch=dch, tc=tc: e.matmul(
                            G1[:, tc*512:(tc+1)*512],
                            b.T1("wkv16")[:, dch*128:(dch+1)*128],
                            b.arena[:, dch*T + tc*512: dch*T + (tc+1)*512],
                            start=(dch == 0), stop=(dch == NCH - 1)),
                            waits=(wts if (dch == 0 and tc == 0) else ()),
                            label=(f"p1.kmm.{c}" if last else None))
                wts = [(DQ["d_wv"], f"wv.{c}")]
                if c >= 1:
                    wts.append((eA, f"p1.kev.{c-1}.1"))
                if c >= 1 and (c - 1) < 8:
                    wts.append((eV, f"p1.rope.{c-1}.1"))
                for dch in range(NCH):
                    for tc in range(TCOL):
                        last = dch == NCH - 1 and tc == TCOL - 1
                        eT.op(lambda e, G2=G2, dch=dch, tc=tc: e.matmul(
                            G2[:, tc*512:(tc+1)*512],
                            b.T1("wkv16")[:, D + dch*128:D + (dch+1)*128],
                            b.arena[:, dch*T + tc*512: dch*T + (tc+1)*512],
                            start=(dch == 0), stop=(dch == NCH - 1)),
                            waits=(wts if (dch == 0 and tc == 0) else ()),
                            label=(f"p1.vmm.{c}" if last else None))
                if c < 8:
                    for tc in range(TCOL):
                        eT.op(lambda e, G1=G1, tc=tc: e.matmul(
                            G1[:, tc*512:(tc+1)*512], b.perm_t[:],
                            b.T1("gA")[:, tc*512:(tc+1)*512], start=True,
                            stop=True),
                            waits=([(eA, f"p1.kev.{c}.1")] if tc == 0 else ()),
                            label=(f"p1.perm.{c}" if tc == TCOL - 1 else None))

        def act(E):
            def emit_sq(c, ih):
                wts = ([(eV, f"p1.rope.{c}.{ih}")] if c < 8
                       else [(eA, f"p1.kev.{c}.{ih}")])
                if ih == 1 and f"hc{c}" in b.taps:
                    wts.append((b.DQ["d_tap"], f"tap.hc{c}"))
                if c == 0 and ih == 0:
                    wts.append((eV, "p0.red.47"))
                elif ih == 0:
                    wts.append((eV, f"p1.nsq.{c-1}.1"))
                else:
                    wts.append((eV, f"p1.nsq.{c}.0"))
                eA.op(lambda e: e.activation(
                    b.T1("hC")[:], b.T1("gA")[:, hsl(ih)], AF.Square),
                    waits=wts, label=f"p1.sq.{c}.{ih}")

            for c in range(16):
                if c < 8:
                    self._tables_act("p1", c, 0)
                for ih in range(2):
                    wts = [(eT, f"p1.kmm.{c}")]
                    if c >= 1:
                        wts.append((eV, f"p1.kv.{c-1}.1"))
                    if c == 1 and "rope0" in b.taps:
                        wts.append((b.DQ["d_tap"], "tap.rope0"))
                    if c == 4 and "kr3" in b.taps:
                        wts.append((b.DQ["d_tap"], "tap.kr3"))
                    eA.op(lambda e, c=c, ih=ih: e.activation(
                        b.T1("gA")[:, hsl(ih)], PS[c % 2][:, hsl(ih)], AF.Identity,
                        bias=b.bin_t[:, 16+c:17+c], scale=1.0),
                        waits=wts, label=f"p1.kev.{c}.{ih}")
                for ih in range(2):
                    wts = [(eT, f"p1.vmm.{c}")]
                    if c >= 1:
                        wts.append((eV, f"p1.kv.{c-1}.1"))
                    if c == 1 and ih == 0 and "v0" in b.taps:
                        wts.append((b.DQ["d_tap"], "tap.v0"))
                    eA.op(lambda e, c=c, ih=ih: e.activation(
                        b.T1("gB")[:, hsl(ih)], PS[(c+1) % 2][:, hsl(ih)],
                        AF.Identity, bias=b.bin_t[:, 32+c:33+c], scale=1.0),
                        waits=wts, label=f"p1.vev.{c}.{ih}")
                emit_sq(c, 0)
                if c < 8:
                    self._tables_act("p1", c, 1)
                emit_sq(c, 1)

        def vec(E):
            def emit_nsq(c, ih):
                wts = [(eA, f"p1.sq.{c}.{ih}")]
                if c == 0 and ih == 0 and "sq00" in b.taps:
                    wts.append((b.DQ["d_tap"], "tap.sq00"))
                if c == 1 and ih == 0 and "nsq0" in b.taps:
                    wts.append((b.DQ["d_tap"], "tap.nsq0"))
                if ih == 0 and f"hc{c}" in b.taps:
                    wts.append((b.DQ["d_tap"], f"tap.hc{c}"))
                if c >= 1 and ih == 0 and f"gc{c-1}" in b.taps:
                    wts.append((b.DQ["d_tap"], f"tap.gc{c-1}"))
                if c == 0:
                    if ih == 1:
                        wts.append((eV, "p1.nsq.0.0"))
                    eV.op(lambda e, ih=ih: e.tensor_copy(
                        b.T1("gC")[:, hsl(ih)], b.T1("hC")[:]),
                        waits=wts, label=f"p1.nsq.0.{ih}")
                else:
                    wts.append((eV, f"p1.nsq.{c-1}.1" if ih == 0
                                else f"p1.nsq.{c}.0"))
                    eV.op(lambda e, ih=ih: e.tensor_tensor(
                        b.T1("gC")[:, hsl(ih)], b.T1("gC")[:, hsl(ih)],
                        b.T1("hC")[:], ALU.add),
                        waits=wts, label=f"p1.nsq.{c}.{ih}")

            def emit_kv(c, ih):
                wts = [(eA, f"p1.vev.{c}.{ih}"), (eA, f"p1.sq.{c}.{ih}")]
                eV.op(lambda e, c=c, ih=ih: e.tensor_tensor(
                    b.bigbuf[:, c*T + ih*HF: c*T + (ih+1)*HF],
                    b.T1("gA")[:, hsl(ih)], b.T1("gB")[:, hsl(ih)], ALU.mult),
                    waits=wts, label=f"p1.kv.{c}.{ih}")

            for c in range(16):
                if c < 8:
                    if c == 0:
                        extra0 = [(eV, "p0.red.47"), (eA, "p0.abs.47")]
                        sq_guard = None
                    else:
                        extra0 = [(eV, f"p1.rope.{c-1}.1"),
                                  (eA, f"p1.cos.{c-1}.1")]
                        sq_guard = (eA, f"p1.sq.{c-1}.1")
                    self._theta_chain("p1", c, extra0, sq_guard, 0)
                    self._rope_combine("p1", c, PS[c % 2], 0)
                    emit_nsq(c, 0)
                    emit_kv(c, 0)
                    self._theta_chain("p1", c, extra0, sq_guard, 1)
                    self._rope_combine("p1", c, PS[c % 2], 1)
                    emit_nsq(c, 1)
                    emit_kv(c, 1)
                else:
                    for ih in range(2):
                        emit_nsq(c, ih)
                        emit_kv(c, ih)

        return {"gpsimd": gp, "tensor": pe, "scalar": act, "vector": vec,
                "sync": sync}

    def ph_p1b(self):
        b = self
        eT, eV, eA, eP, DQ, PS = b.eT, b.eV, b.eA, b.eP, b.DQ, b.PS

        def pe(E):
            for tc in range(TCOL):
                eT.op(lambda e, tc=tc: e.matmul(
                    b.psA[0:1, tc*512:(tc+1)*512], b.onesC_t[:],
                    b.T1("gC")[:, tc*512:(tc+1)*512], start=True, stop=True),
                    waits=([(eV, "p1.nsq.15.1"), (eA, "p1.vev.15.1"),
                            (eV, "p1.kv.15.1")] if tc == 0 else ()),
                    label=("p1b.nsqmm" if tc == TCOL - 1 else None))
            eT.op(lambda e: e.matmul(b.psB[0:1, 0:4], b.onesC_t[:], b.gacc[:, 0:4],
                                     start=True, stop=True),
                  waits=[(eV, "p0.acc.47"), (eA, "p1.kev.15.1")], label="p1b.gmm")
            for tc in range(TCOL):
                eT.op(lambda e, tc=tc: e.matmul(
                    b.psB[:, tc*512:(tc+1)*512], b.onesR_t[:],
                    b.nrowR[0:1, tc*512:(tc+1)*512], start=True, stop=True),
                    waits=([(eV, "p1b.srow"), (eA, "p1b.gvac")] if tc == 0 else ()),
                    label=("p1b.sbmm" if tc == TCOL - 1 else None))

        def act(E):
            eA.op(lambda e: e.activation(b.nrowR, b.psA[0:1, :], AF.Sqrt),
                  waits=[(eT, "p1b.nsqmm")], label="p1b.nsqvac")
            eA.op(lambda e: e.activation(b.gcells[:], b.psB[0:1, 0:4], AF.Identity),
                  waits=[(eT, "p1b.gmm")], label="p1b.gvac")
            eA.op(lambda e: e.activation(b.T1("gA")[:], b.psB[:], AF.Identity),
                  waits=[(eT, "p1b.sbmm"), (eV, "p1.kv.15.1")], label="p1b.sB")

        def vec(E):
            eV.op(lambda e: e.tensor_scalar(b.nrowR, b.nrowR, 1e-12, None, ALU.max),
                  waits=[(eA, "p1b.nsqvac")], label="p1b.nmax")
            eV.op(lambda e: e.reciprocal(b.nrowR, b.nrowR),
                  waits=[(eV, "p1b.nmax")], label="p1b.srow")
            for c in range(16):
                for ih in range(2):
                    wts = [(eP, f"p1b.kvs.{c}.{ih}")]
                    if ih == 0 and c >= 1:
                        wts.append((eV, f"p1b.acol.{c-1}"))
                    eV.op(lambda e, ih=ih: e.tensor_reduce(
                        b.t16b[:, ih:ih+1], b.T1("hA")[:], AX.X, ALU.add),
                        waits=wts, label=f"p1b.red.{c}.{ih}")
                eV.op(lambda e, c=c: e.tensor_tensor(
                    b.acol[:, c:c+1], b.t16b[:, 0:1], b.t16b[:, 1:2], ALU.add),
                    waits=[(eV, f"p1b.red.{c}.1")], label=f"p1b.acol.{c}")
            eV.op(lambda e: e.tensor_scalar(b.acolB[:], b.acol[:],
                                            b.bsel_t[:, 1:2], None, ALU.mult),
                  waits=[(eV, "p1b.acol.15")], label="p1b.a1")
            eV.op(lambda e: e.tensor_scalar(b.acol2[:], b.acol[:],
                                            b.bsel_t[:, 0:1], None, ALU.mult),
                  waits=[(eV, "p1b.a1")], label="p1b.a0")

        def gp(E):
            for c in range(16):
                for ih in range(2):
                    wts = [(eA, "p1b.sB"), (eV, f"p1.kv.{c}.{ih}")]
                    if c == 0 and ih == 0:
                        wts += [(eV, "p1.rope.7.1"), (eA, "p1.cos.7.1")]
                    elif ih == 0:
                        wts.append((eV, f"p1b.red.{c-1}.1"))
                    else:
                        wts.append((eV, f"p1b.red.{c}.0"))
                    eP.op(lambda e, c=c, ih=ih: e.tensor_tensor(
                        b.T1("hA")[:], b.bigbuf[:, c*T + ih*HF: c*T + (ih+1)*HF],
                        b.T1("gA")[:, hsl(ih)], ALU.mult),
                        waits=wts, label=f"p1b.kvs.{c}.{ih}")

        def sync(E):
            DQ["d_cc"].op(lambda e: e.dma_start(
                out=b.ccin[0:1, :].rearrange("r (p f) -> (r p) f", p=128),
                in_=b.acol2[:]), waits=[(eV, "p1b.a0")], label="cc.a0")
            DQ["d_cc"].op(lambda e: e.dma_start(
                out=b.ccin[1:2, :].rearrange("r (p f) -> (r p) f", p=128),
                in_=b.acolB[:]), waits=[(eV, "p1b.a1")], label="cc.a1")
            DQ["d_cc"].op(lambda e: e.dma_start(out=b.ccin[2:3, 0:4],
                                                in_=b.gcells[:]),
                          waits=[(eA, "p1b.gvac")], label="cc.g")
            DQ["d_cc"].op(lambda e: e.dma_start(out=b.ccin[2:3, 4:T],
                                                in_=b.zrowD[0:1, 4:T]),
                          label="cc.gz")
            DQ["d_cc"].op(lambda e: e.dma_start(out=b.ccin[3:4, :], in_=b.zrowD),
                          label="cc.z")
            if "nsq" in b.taps:
                nw = [(eV, "p1.nsq.15.1")]
                for prev in ([f"gc{n}" for n in range(15, -1, -1)]
                             + [f"hc{n}" for n in range(15, -1, -1)]
                             + ["nsq0", "sq00", "rope0", "v0", "k0", "arena"]):
                    if prev in b.taps:
                        nw.append((DQ["d_tap"], f"tap.{prev}"))
                        break
                DQ["d_tap"].op(lambda e: e.dma_start(out=b.tap_outs["nsq"],
                                                     in_=b.T1("gC")[:]),
                               waits=nw, label="tap.nsq")
            if "srow" in b.taps:
                sw = [(eV, "p1b.srow")]
                if "nsq" in b.taps:
                    sw.append((DQ["d_tap"], "tap.nsq"))
                DQ["d_tap"].op(lambda e: e.dma_start(out=b.tap_outs["srow"],
                                                     in_=b.nrowR),
                               waits=sw, label="tap.srow")
            if "kv" in b.taps:
                wts = [(eV, "p1.kv.15.1")]
                if "srow" in b.taps:
                    wts.append((DQ["d_tap"], "tap.srow"))
                DQ["d_tap"].op(lambda e: e.dma_start(out=b.tap_outs["kv"],
                                                     in_=b.bigbuf[:]),
                               waits=wts, label="tap.kv")

        return {"tensor": pe, "scalar": act, "vector": vec, "gpsimd": gp,
                "sync": sync}

    def ph_pcc(self):
        b = self
        eT, eV, eA, eP, DQ = b.eT, b.eV, b.eA, b.eP, b.DQ

        def gp(E):
            E.wait_val(DQ["d_cc"], 16 * 5)
            eP.op(lambda e: e.collective_compute(
                "AllReduce", ALU.add, [list(range(8))], [b.ccin.opt()],
                [b.ccout.opt()]), label="cc.done")

        def sync(E):
            E.wait(eP, "cc.done")
            DQ["d_ccR"].op(lambda e: e.dma_start(
                out=b.acol2[:],
                in_=b.ccout[0:1, :].rearrange("r (p f) -> (r p) f", p=128)),
                waits=[(eV, "p1b.a0")], label="ccR.a0")
            DQ["d_ccR"].op(lambda e: e.dma_start(
                out=b.acolB[:],
                in_=b.ccout[1:2, :].rearrange("r (p f) -> (r p) f", p=128)),
                label="ccR.a1")
            DQ["d_ccR"].op(lambda e: e.dma_start(out=b.gcells[:],
                                                 in_=b.ccout[2:3, 0:4]),
                           waits=[(DQ["d_ccR"], "ccR.a1")], label="ccR.g")

        def vec(E):
            eV.op(lambda e: e.tensor_scalar(b.acol2[:], b.acol2[:],
                                            b.bsel_t[:, 0:1], None, ALU.mult),
                  waits=[(DQ["d_ccR"], "ccR.a1")],
                  label="cc.s0")
            eV.op(lambda e: e.tensor_scalar(b.acolB[:], b.acolB[:],
                                            b.bsel_t[:, 1:2], None, ALU.mult),
                  waits=[(eV, "cc.s0")], label="cc.s1")
            eV.op(lambda e: e.tensor_tensor(b.acol2[:], b.acol2[:], b.acolB[:],
                                            ALU.add),
                  waits=[(eV, "cc.s1")], label="cc.asel")
            eV.op(lambda e: e.tensor_scalar(b.gvals[0:1, 0:3], b.gcells[0:1, 0:3],
                                            1.0 / NW, 1e-5, ALU.mult, ALU.max),
                  waits=[(DQ["d_ccR"], "ccR.g")], label="cc.g3")
            eV.op(lambda e: e.reciprocal(b.gvals[0:1, 3:6], b.gvals[0:1, 0:3]),
                  waits=[(eV, "cc.g3")], label="cc.ginv")
            eV.op(lambda e: e.memset(b.gvals[0:1, 6:8], 0.0), label="cc.gpad")

        def pe(E):
            eT.op(lambda e: e.matmul(b.psA[:, 512:520], b.onesR_t[:],
                                     b.gvals[0:1, 0:8], start=True, stop=True),
                  waits=[(eV, "cc.ginv"), (eV, "cc.gpad")], label="cc.gbmm")

        def act(E):
            eA.op(lambda e: e.activation(b.acol2[:], b.acol2[:], AF.Identity),
                  waits=[(eV, "cc.asel")], label="cc.acol2")
            eA.op(lambda e: e.activation(b.gB8[:], b.psA[:, 512:520], AF.Identity),
                  waits=[(eT, "cc.gbmm")], label="cc.gB8")

        return {"gpsimd": gp, "sync": sync, "vector": vec, "tensor": pe,
                "scalar": act}

    def ph_p2(self):
        b = self
        eT, eV, eA, eP, DQ, PS = b.eT, b.eV, b.eA, b.eP, b.DQ, b.PS

        def gp(E):
            def emit_owqw(j2):
                j, ihp = j2 // 2, j2 % 2
                fc, j4 = j // 4, j % 4
                buf = ["gB", "gC"][j % 2]
                wts = [(eV, f"p2.owq.{j2}")]
                if j2 >= 1:
                    wts.append((DQ["d_owqW"], f"owqW.{j2-1}"))
                DQ["d_owqW"].op(lambda e, fc=fc, j4=j4, ihp=ihp, bu=buf:
                                e.dma_start(
                    out=b.owq_cache[fc][:, j4*2048 + ihp*HF:
                                        j4*2048 + (ihp+1)*HF],
                    in_=b.T1(bu)[:, hsl(ihp)]),
                    waits=wts, label=f"owqW.{j2}")

            for c in range(16):
                par = c % 2
                wts = [(eT, f"p2.qmm.{c-2}")] if c >= 2 else [
                    (eT, "p1.kmm.15"), (eT, "p1.vmm.15")]
                DQ[f"d_wq{par}"].op(lambda e, c=c, par=par: e.dma_start(
                    out=b.T1("wkv16")[:, par*D:(par+1)*D]
                    .rearrange("p (dc e) -> p dc e", e=128),
                    in_=b.WinT[c].rearrange("(dc p) e -> p dc e", p=128)),
                    waits=wts, label=f"wq.{c}")
                if c >= 1:
                    for j2 in range(8 * (c - 1), 8 * c):
                        emit_owqw(j2)
            for j2 in range(8 * 15, 128):
                emit_owqw(j2)

        def sync(E):
            prevR = [None]

            def emit_tabR(c, ih):
                for sc, buf, lab, rdlab in ((0, "hE", "tabRs", "krs"),
                                            (1, "hD", "tabRc", "kcs")):
                    wts = []
                    if prevR[0] is not None:
                        wts.append((DQ["d_tabR"], prevR[0]))
                    if c == 0 and ih == 0:
                        if sc == 0:
                            wts += [(DQ["d_tabW"], "tabWc.7.1"),
                                    (eV, "p1.rope.7.1")]
                    else:
                        pc, pih = (c, 0) if ih == 1 else (c - 1, 1)
                        wts.append((eV, f"p2.{rdlab}.{pc}.{pih}"))
                    DQ["d_tabR"].op(lambda e, c=c, ih=ih, sc=sc, buf=buf:
                                    e.dma_start(
                        out=b.T1(buf)[:], in_=b.tab_sp[c, ih, sc]),
                        waits=wts, label=f"{lab}.{c}.{ih}")
                    prevR[0] = f"{lab}.{c}.{ih}"

            for j in range(64):
                par = j % 2
                buf = ["gB", "gC"][par]
                if j % 4 == 0 and j // 4 < 8:
                    emit_tabR(j // 4, 0)
                    emit_tabR(j // 4, 1)
                if j == 0:
                    wts = [(eV, "p1.kv.15.1")]
                elif j == 1:
                    wts = [(eT, "p1b.nsqmm")]
                    if "nsq" in b.taps:
                        wts.append((DQ["d_tap"], "tap.nsq"))
                else:
                    wts = [(DQ["d_owqW"], f"owqW.{2*(j-2)+1}")]
                DQ[f"d_owp{par}"].op(lambda e, j=j, bu=buf: e.dma_start(
                    out=b.T1(bu)[:].rearrange("p (dc f) -> p dc f", f=128),
                    in_=b.OwT[j].rearrange("(dc p) f -> p dc f", p=128)),
                    waits=wts, label=f"owp.{j}")


        def pe(E):
            for c in range(16):
                par = c % 2
                G1 = PS[c % 2]
                wts = [(DQ[f"d_wq{par}"], f"wq.{c}"), (eA, "cc.acol2"),
                       (eA, "cc.gB8")]
                if c == 0:
                    wts += [(eP, "p1b.kvs.15.1"), (eA, "p1b.sB")]
                if c >= 2:
                    wts.append((eA, f"p2.qev.{c-2}.1"))
                if c >= 2 and (c - 2) < 8:
                    wts.append((eV, f"p2.rope.{c-2}.1"))
                for dch in range(NCH):
                    for tc in range(TCOL):
                        last = dch == NCH - 1 and tc == TCOL - 1
                        eT.op(lambda e, G1=G1, dch=dch, tc=tc, par=par: e.matmul(
                            G1[:, tc*512:(tc+1)*512],
                            b.T1("wkv16")[:, par*D + dch*128: par*D + (dch+1)*128],
                            b.arena[:, dch*T + tc*512: dch*T + (tc+1)*512],
                            start=(dch == 0), stop=(dch == NCH - 1)),
                            waits=(wts if (dch == 0 and tc == 0) else ()),
                            label=(f"p2.qmm.{c}" if last else None))
                if c < 8:
                    for tc in range(TCOL):
                        eT.op(lambda e, G1=G1, tc=tc: e.matmul(
                            G1[:, tc*512:(tc+1)*512], b.perm_t[:],
                            b.T1("gA")[:, tc*512:(tc+1)*512], start=True,
                            stop=True),
                            waits=([(eA, f"p2.qev.{c}.1")] if tc == 0 else ()),
                            label=(f"p2.perm.{c}" if tc == TCOL - 1 else None))

        def act(E):
            for c in range(16):
                for ih in range(2):
                    wts = [(eT, f"p2.qmm.{c}")]
                    if c >= 1:
                        wts.append((eV, f"p2.y.{c-1}.1"))
                    eA.op(lambda e, c=c, ih=ih: e.activation(
                        b.T1("gA")[:, hsl(ih)], PS[c % 2][:, hsl(ih)], AF.Identity,
                        bias=b.bin_t[:, c:c+1], scale=1.0),
                        waits=wts, label=f"p2.qev.{c}.{ih}")
                for jj in range(4):
                    j = 4 * c + jj
                    par = j % 2
                    buf = ["gB", "gC"][par]
                    for ihp in range(2):
                        j2 = 2 * j + ihp
                        sl2 = hsl(ihp)
                        wts = [(DQ[f"d_owp{par}"], f"owp.{j}"), (eA, "cc.gB8"),
                           (eV, "p0.rndp"), (eV, "p0.rndn")]
                        eA.op(lambda e, bu=buf, sl2=sl2: e.activation(
                            b.T1(bu)[:, sl2], b.T1(bu)[:, sl2], AF.Identity,
                            bias=b.rndP[:], scale=b.gB8[:, 5:6]),
                            waits=wts, label=f"p2.owr1.{j2}")
                        eA.op(lambda e, bu=buf, sl2=sl2: e.activation(
                            b.T1(bu)[:, sl2], b.T1(bu)[:, sl2], AF.Identity,
                            bias=b.rndN[:], scale=1.0),
                            waits=[(eA, f"p2.owr1.{j2}")],
                            label=f"p2.owr2.{j2}")

        def combine_ld(c, ih, G):
            s = hsl(ih)
            wts = [(eT, f"p2.perm.{c}"), (DQ["d_tabR"], f"tabRs.{c}.{ih}")]
            if ih == 1:
                wts.append((eV, f"p2.rope.{c}.0"))
            elif c >= 1:
                wts.append((eV, f"p2.rope.{c-1}.1"))
            else:
                wts += [(eV, "p1.rope.7.1"), (eP, "p1b.kvs.15.1")]
            eV.op(lambda e, s=s, G=G: e.tensor_tensor(
                b.T1("hB")[:], G[:, s], b.T1("hE")[:], ALU.mult),
                waits=wts, label=f"p2.krs.{c}.{ih}")
            eV.op(lambda e, s=s: e.tensor_tensor(
                b.T1("gA")[:, s], b.T1("gA")[:, s], b.T1("hD")[:], ALU.mult),
                waits=[(eV, f"p2.krs.{c}.{ih}"),
                       (DQ["d_tabR"], f"tabRc.{c}.{ih}")],
                label=f"p2.kcs.{c}.{ih}")
            eV.op(lambda e, s=s: e.tensor_tensor(
                b.T1("gA")[:, s], b.T1("gA")[:, s], b.T1("hB")[:], ALU.add),
                waits=[(eV, f"p2.kcs.{c}.{ih}")], label=f"p2.rope.{c}.{ih}")

        def vec(E):
            for c in range(16):
                if c < 8:
                    combine_ld(c, 0, PS[c % 2])
                    combine_ld(c, 1, PS[c % 2])
                for ih in range(2):
                    wts = ([(eV, f"p2.rope.{c}.{ih}")] if c < 8
                           else [(eA, f"p2.qev.{c}.{ih}")])
                    wts.append((eA, "cc.acol2"))
                    if c == 0 and ih == 0:
                        wts.append((eP, "p1b.kvs.15.1"))
                        if "kv" in b.taps:
                            wts.append((DQ["d_tap"], "tap.kv"))
                    eV.op(lambda e, c=c, ih=ih: e.tensor_scalar(
                        b.bigbuf[:, c*T + ih*HF: c*T + (ih+1)*HF],
                        b.T1("gA")[:, hsl(ih)], b.acol2[:, c:c+1], None, ALU.mult),
                        waits=wts, label=f"p2.y.{c}.{ih}")
                for jj in range(4):
                    j = 4 * c + jj
                    par = j % 2
                    buf = ["gB", "gC"][par]
                    for ihp in range(2):
                        j2 = 2 * j + ihp
                        sl2 = hsl(ihp)
                        wts = [(eA, f"p2.owr2.{j2}")]
                        eV.op(lambda e, bu=buf, sl2=sl2: e.tensor_scalar(
                            b.T1(bu)[:, sl2], b.T1(bu)[:, sl2], 1.0, -1.0,
                            ALU.min, ALU.max),
                            waits=wts, label=f"p2.owq.{j2}")

        return {"gpsimd": gp, "sync": sync, "tensor": pe,
                "scalar": act, "vector": vec}

    def ph_ao(self):
        b = self
        eT, eV, eA, eP, DQ, PS = b.eT, b.eV, b.eA, b.eP, b.DQ, b.PS

        def gp(E):
            for fc in range(16):
                par = fc % 2
                wts = [(eT, f"ao.mm.{fc-2}")] if fc >= 2 else [(eT, "p2.qmm.15")]
                DQ[f"d_wo{par}"].op(lambda e, fc=fc, par=par: e.dma_start(
                    out=b.T1("wkv16")[:, par*D:(par+1)*D]
                    .rearrange("p (dc e) -> p dc e", e=128),
                    in_=b.WoT[fc].rearrange("(dc p) e -> p dc e", p=128)),
                    waits=wts, label=f"wo.{fc}")

        def sync(E):
            for k in range(32):
                fc, ih = k // 2, k % 2
                par = k % 2
                buf = ["hC", "hD"][par]
                wts = [(eV, f"ao.x2pre.{k-2}")] if k >= 2 else [
                    (eV, "p2.y.15.1")]
                DQ[f"d_xres{par}"].op(lambda e, fc=fc, ih=ih, bu=buf: e.dma_start(
                    out=b.T1(bu)[:],
                    in_=b.XT[fc*128:(fc+1)*128, ih*HF:(ih+1)*HF]),
                    waits=wts, label=f"xres.{k}")

        def pe(E):
            for fc in range(16):
                par = fc % 2
                G1 = PS[fc % 2]
                wts = [(DQ[f"d_wo{par}"], f"wo.{fc}")]
                if fc == 0:
                    wts += [(eV, "p2.y.15.1"), (eV, "p2.rope.7.1"),
                            (eA, "p2.qev.15.1")]
                if fc == 1:
                    wts += [(eA, "p2.qev.15.1")]
                if fc >= 2:
                    wts.append((eV, f"ao.x2pre.{2*(fc-2)+1}"))
                for dch in range(NCH):
                    for tc in range(TCOL):
                        last = dch == NCH - 1 and tc == TCOL - 1
                        eT.op(lambda e, G1=G1, dch=dch, tc=tc, par=par: e.matmul(
                            G1[:, tc*512:(tc+1)*512],
                            b.T1("wkv16")[:, par*D + dch*128: par*D + (dch+1)*128],
                            b.bigbuf[:, dch*T + tc*512: dch*T + (tc+1)*512],
                            start=(dch == 0), stop=(dch == NCH - 1)),
                            waits=(wts if (dch == 0 and tc == 0) else ()),
                            label=(f"ao.mm.{fc}" if last else None))

        def act(E):
            for k in range(32):
                fc, ih = k // 2, k % 2
                par = k % 2
                wts = [(eV, f"ao.x2pre.{k}")]
                if k >= 1:
                    wts.append((eV, f"ao.ssq.{k-1}"))
                eA.op(lambda e, p=par: e.activation(b.T1("hE")[:],
                                                    b.T1(["hA", "hB"][p])[:],
                                                    AF.Square),
                      waits=wts, label=f"ao.sq.{k}")
                DQ[f"d_x2preW{par}"].op(lambda e, fc=fc, ih=ih, par=par: e.dma_start(
                    out=b.x2pre_sp[fc*128:(fc+1)*128, ih*HF:(ih+1)*HF],
                    in_=b.T1(["hA", "hB"][par])[:]),
                    waits=[(eV, f"ao.ssum.{k}"), (eV, f"ao.mx.{k}"),
                           (eV, f"ao.mn.{k}")],
                    label=f"x2preW.{k}")

        def vec(E):
            for k in range(32):
                fc, ih = k // 2, k % 2
                par = k % 2
                xp = ["hA", "hB"][par]
                xr = ["hC", "hD"][par]
                wts = [(eT, f"ao.mm.{fc}"), (DQ[f"d_xres{par}"], f"xres.{k}")]
                if k >= 2:
                    wts.append((DQ[f"d_x2preW{par}"], f"x2preW.{k-2}"))
                if k == 0:
                    wts += [(eV, "p2.y.15.1"), (eV, "p2.owq.127"),
                            (eA, "p2.owr2.127")]
                if k == 1:
                    wts.append((eA, "p1.cos.7.1"))
                eV.op(lambda e, xp=xp, xr=xr, fc=fc, ih=ih: e.scalar_tensor_tensor(
                    b.T1(xp)[:], PS[fc % 2][:, hsl(ih)], b.bout_t[:, fc:fc+1],
                    b.T1(xr)[:], ALU.add, ALU.add),
                    waits=wts, label=f"ao.x2pre0.{k}")
                eV.op(lambda e, xp=xp, xr=xr: e.tensor_tensor(
                    b.T1(xp)[:], b.T1(xp)[:], b.T1(xr)[:], ALU.add),
                    waits=[(eV, f"ao.x2pre0.{k}")], label=f"ao.x2pre.{k}")
                sgm = hsl(ih)
                if k <= 1:
                    xb = ["hA", "hB"][par]
                    pre = [(eV, f"ao.x2pre.{k}")]
                    prev = ([] if k == 0 else [(eV, f"ao.ssum.{k-1}")])
                    init_wts = {
                        "ssum": pre + prev + ([(eV, "p2.y.15.1")] if k == 0 else []),
                        "mx": pre + ([(eV, "p2.owq.127"),
                                      (DQ["d_owqW"], "owqW.127")] if k == 0
                                     else [(eV, "ao.mx.0")]),
                        "mn": pre + ([(eV, "p1.th0.7.1")] if k == 0
                                     else [(eV, "ao.mn.0")]),
                    }
                    eV.op(lambda e, xb=xb, sgm=sgm: e.tensor_copy(
                        b.T1("gA")[:, sgm], b.T1(xb)[:]),
                        waits=init_wts["ssum"], label=f"ao.ssum.{k}")
                    eV.op(lambda e, xb=xb, sgm=sgm: e.tensor_copy(
                        b.T1("gC")[:, sgm], b.T1(xb)[:]),
                        waits=init_wts["mx"], label=f"ao.mx.{k}")
                    eV.op(lambda e, xb=xb, sgm=sgm: e.tensor_copy(
                        b.T1("gD")[:, sgm], b.T1(xb)[:]),
                        waits=init_wts["mn"], label=f"ao.mn.{k}")
                    sq_wts = [(eA, f"ao.sq.{k}")]
                    if k == 0:
                        sq_wts += [(eV, "p2.owq.126"),
                                   (DQ["d_owqW"], "owqW.127")]
                    else:
                        sq_wts.append((eV, "ao.ssq.0"))
                    eV.op(lambda e, sgm=sgm: e.tensor_copy(
                        b.T1("gB")[:, sgm], b.T1("hE")[:]),
                        waits=sq_wts, label=f"ao.ssq.{k}")
                else:
                    eV.op(lambda e, xp=xp, sgm=sgm: e.tensor_tensor(
                        b.T1("gA")[:, sgm], b.T1("gA")[:, sgm], b.T1(xp)[:],
                        ALU.add),
                        waits=[(eV, f"ao.x2pre.{k}"), (eV, f"ao.ssum.{k-1}")],
                        label=f"ao.ssum.{k}")
                    eV.op(lambda e, xp=xp, sgm=sgm: e.tensor_tensor(
                        b.T1("gC")[:, sgm], b.T1("gC")[:, sgm], b.T1(xp)[:],
                        ALU.max),
                        waits=[(eV, f"ao.mx.{k-1}")], label=f"ao.mx.{k}")
                    eV.op(lambda e, xp=xp, sgm=sgm: e.tensor_tensor(
                        b.T1("gD")[:, sgm], b.T1("gD")[:, sgm], b.T1(xp)[:],
                        ALU.min),
                        waits=[(eV, f"ao.mn.{k-1}")], label=f"ao.mn.{k}")
                    eV.op(lambda e, sgm=sgm: e.tensor_tensor(
                        b.T1("gB")[:, sgm], b.T1("gB")[:, sgm], b.T1("hE")[:],
                        ALU.add),
                        waits=[(eA, f"ao.sq.{k}"), (eV, f"ao.ssq.{k-1}")],
                        label=f"ao.ssq.{k}")

        return {"gpsimd": gp, "sync": sync, "tensor": pe,
                "scalar": act, "vector": vec}

    def ph_ln1(self):
        b = self
        eT, eV, eA, eP, DQ, PS = b.eT, b.eV, b.eA, b.eP, b.DQ, b.PS
        srcs = ["gA", "gB", "gC", "gD"]
        red_ops = [ALU.add, ALU.add, ALU.max, ALU.min]
        lastdeps = ["ao.ssum.31", "ao.ssq.31", "ao.mx.31", "ao.mn.31"]

        def pe(E):
            for k in range(64):
                i, j = k // 16, k % 16
                wts = [(eV, lastdeps[i])]
                if k == 0:
                    wts += [(eV, "ao.x2pre.31"), (eA, "ao.sq.31")]
                if k >= 2:
                    wts.append((eV, f"ln1.red.{k-2}"))
                eT.op(lambda e, i=i, j=j, k=k: e.transpose(
                    PS[k % 2][:, 0:128], b.T1(srcs[i])[:, j*128:(j+1)*128],
                    b.eye_t[:]),
                    waits=wts, label=f"ln1.tr.{k}")
            packs = [("mT16", "ln1.m"), ("rT16", "ln1.rstd"), ("sT16", "ln1.sx"),
                     ("siT16", "ln1.six")]
            for i, (tn, dep) in enumerate(packs):
                wts = [(eV, dep)]
                if i < 2:
                    wts.append((eV, f"ln1.red.{62+i}"))
                else:
                    wts.append((eA, f"ln1.rev.{i-2}"))
                eT.op(lambda e, tn=tn, i=i: e.transpose(
                    PS[i % 2][0:16, 128:256], getattr(b, tn)[:], b.eye_t[:]),
                    waits=wts, label=f"ln1.rtr.{i}")
            labs = ["mB", "rB", "sxB", "siB"]
            for i, lab in enumerate(labs):
                row = b.ln1rows[i % 2]
                for tc in range(TCOL):
                    wts = []
                    if tc == 0:
                        wts = [(DQ["d_bnc"], f"bncR.{i}"), (eA, "ln1.rev.3"),
                               (eV, f"ln1.red.{62 + (i % 2)}")]
                        if i >= 2:
                            wts.append((eA, f"ln1.ev.{labs[i-2]}"))
                    eT.op(lambda e, row=row, i=i, tc=tc: e.matmul(
                        PS[i % 2][:, tc*512:(tc+1)*512], b.onesR_t[:],
                        row[0:1, tc*512:(tc+1)*512], start=True, stop=True),
                        waits=wts,
                        label=(f"ln1.bc.{lab}" if tc == TCOL - 1 else None))

        def vec(E):
            dsts = [b.sT16, b.t16a, b.t16b, b.hT16]
            for k in range(64):
                i, j = k // 16, k % 16
                wts = [(eT, f"ln1.tr.{k}")]
                if k == 0:
                    wts.append((eV, "ao.mn.31"))
                eV.op(lambda e, i=i, j=j, k=k: e.tensor_reduce(
                    dsts[i][:, j:j+1], PS[k % 2][0:128, 0:128], AX.X, red_ops[i]),
                    waits=wts, label=f"ln1.red.{k}")
            eV.op(lambda e: e.tensor_scalar(b.mT16[:], b.sT16[:], 1.0 / D, None,
                                            ALU.mult),
                  waits=[(eV, "ln1.red.15")], label="ln1.m")
            eV.op(lambda e: e.tensor_scalar(b.t16a[:], b.t16a[:], 1.0 / D, None,
                                            ALU.mult),
                  waits=[(eV, "ln1.red.31")], label="ln1.ex2")
            eV.op(lambda e: e.tensor_tensor(b.dT16[:], b.mT16[:], b.mT16[:],
                                            ALU.mult),
                  waits=[(eV, "ln1.m")], label="ln1.msq")
            eV.op(lambda e: e.tensor_tensor(b.t16a[:], b.t16a[:], b.dT16[:],
                                            ALU.subtract),
                  waits=[(eV, "ln1.msq"), (eV, "ln1.ex2")], label="ln1.var")
            eV.op(lambda e: e.tensor_scalar(b.t16a[:], b.t16a[:], LN_EPS, None,
                                            ALU.add),
                  waits=[(eV, "ln1.var")], label="ln1.vare")
            eV.op(lambda e: e.reciprocal(b.rT16[:], b.t16a[:]),
                  waits=[(eA, "ln1.sqrt")], label="ln1.rstd")
            eV.op(lambda e: e.tensor_tensor(b.t16b[:], b.t16b[:], b.mT16[:],
                                            ALU.subtract),
                  waits=[(eV, "ln1.red.47"), (eV, "ln1.m")], label="ln1.g1")
            eV.op(lambda e: e.tensor_tensor(b.hT16[:], b.mT16[:], b.hT16[:],
                                            ALU.subtract),
                  waits=[(eV, "ln1.red.63")], label="ln1.g2")
            eV.op(lambda e: e.tensor_tensor(b.t16b[:], b.t16b[:], b.hT16[:],
                                            ALU.max),
                  waits=[(eV, "ln1.g1"), (eV, "ln1.g2")], label="ln1.g3")
            eV.op(lambda e: e.tensor_tensor(b.t16b[:], b.t16b[:], b.rT16[:],
                                            ALU.mult),
                  waits=[(eV, "ln1.g3"), (eV, "ln1.rstd")], label="ln1.g4")
            eV.op(lambda e: e.tensor_scalar(b.t16b[:], b.t16b[:], 1e-5, None,
                                            ALU.max),
                  waits=[(eV, "ln1.g4")], label="ln1.gc")
            eV.op(lambda e: e.reciprocal(b.sT16[:], b.t16b[:]),
                  waits=[(eV, "ln1.gc")], label="ln1.sx0")
            eV.op(lambda e: e.tensor_scalar(b.sT16[:], b.sT16[:], 127.0, None,
                                            ALU.mult),
                  waits=[(eV, "ln1.sx0")], label="ln1.sx")
            eV.op(lambda e: e.tensor_scalar(b.siT16[:], b.t16b[:], 1.0 / 127.0,
                                            None, ALU.mult),
                  waits=[(eV, "ln1.gc")], label="ln1.six")

        def act(E):
            eA.op(lambda e: e.activation(b.t16a[:], b.t16a[:], AF.Sqrt),
                  waits=[(eV, "ln1.vare")], label="ln1.sqrt")
            for i in range(4):
                wts = [(eT, f"ln1.rtr.{i}")]
                if i >= 1:
                    wts.append((DQ["d_bnc"], f"bncW.{i-1}"))
                eA.op(lambda e, i=i: e.activation(b.rbounce[:],
                                                  PS[i % 2][0:16, 128:256],
                                                  AF.Identity),
                      waits=wts, label=f"ln1.rev.{i}")
            for i, lab in enumerate(["mB", "rB", "sxB", "siB"]):
                eA.op(lambda e, i=i: e.activation(b.T1(srcs[i])[:], PS[i % 2][:],
                                                  AF.Identity),
                      waits=[(eT, f"ln1.bc.{lab}")], label=f"ln1.ev.{lab}")

        def sync(E):
            for i in range(4):
                row = b.ln1rows[i % 2]
                wts = [(eA, f"ln1.rev.{i}")]
                if i >= 1:
                    wts.append((DQ["d_bnc"], f"bncR.{i-1}"))
                DQ["d_bnc"].op(lambda e, i=i: e.dma_start(
                    out=b.bounce[i:i+1, :].rearrange("r (p f) -> (r p) f", p=16),
                    in_=b.rbounce[:]), waits=wts, label=f"bncW.{i}")
                wts = [(DQ["d_bnc"], f"bncW.{i}")]
                if i == 0:
                    wts.append((eT, "p1b.sbmm"))
                    if "srow" in b.taps:
                        wts.append((DQ["d_tap"], "tap.srow"))
                if i >= 2:
                    wts.append((eT, f"ln1.bc.{['mB', 'rB'][i-2]}"))
                DQ["d_bnc"].op(lambda e, i=i, row=row: e.dma_start(
                    out=row, in_=b.bounce[i:i+1, :]),
                    waits=wts, label=f"bncR.{i}")

        return {"tensor": pe, "vector": vec, "scalar": act, "sync": sync}

    def ph_ln1b(self):
        b = self
        eT, eV, eA, eP, DQ, PS = b.eT, b.eV, b.eA, b.eP, b.DQ, b.PS

        def sync(E):
            for k in range(32):
                fc, ih = k // 2, k % 2
                par = k % 2
                buf = ["hA", "hB"][par]
                wts = [(DQ["d_x2preW0"], "x2preW.30"),
                       (DQ["d_x2preW1"], "x2preW.31")]
                if k >= 2:
                    wts.append((eV, f"ln1b.s.{k-2}"))
                else:
                    wts.append((eV, "ao.x2pre.31"))
                DQ[f"d_x2preR{par}"].op(lambda e, fc=fc, ih=ih, bu=buf: e.dma_start(
                    out=b.T1(bu)[:],
                    in_=b.x2pre_sp[fc*128:(fc+1)*128, ih*HF:(ih+1)*HF]),
                    waits=wts, label=f"x2preR.{k}")

        def scalarD(E):
            for k in range(32):
                fc, ih = k // 2, k % 2
                par = k % 2
                DQ[f"d_x2W{par}"].op(lambda e, fc=fc, ih=ih, par=par: e.dma_start(
                    out=b.x2_sp[fc*128:(fc+1)*128, ih*HF:(ih+1)*HF],
                    in_=b.T1(["hC", "hD"][par])[:]),
                    waits=[(eV, f"ln1b.q1.{k}")], label=f"x2W.{k}")

        def vec(E):
            for k in range(32):
                fc, ih = k // 2, k % 2
                par = k % 2
                xp = ["hA", "hB"][par]
                xq = ["hC", "hD"][par]
                sgm = hsl(ih)
                wts = [(DQ[f"d_x2preR{par}"], f"x2preR.{k}"), (eA, "ln1.ev.mB"),
                       (eA, "ln1.ev.rB")]
                if k >= 2:
                    wts.append((DQ[f"d_x2W{par}"], f"x2W.{k-2}"))
                else:
                    wts.append((eV, "ao.x2pre.31"))
                eV.op(lambda e, xp=xp, xq=xq, sgm=sgm: e.tensor_tensor(
                    b.T1(xq)[:], b.T1(xp)[:], b.T1("gA")[:, sgm], ALU.subtract),
                    waits=wts, label=f"ln1b.s.{k}")
                eV.op(lambda e, xq=xq, sgm=sgm: e.tensor_tensor(
                    b.T1(xq)[:], b.T1(xq)[:], b.T1("gB")[:, sgm], ALU.mult),
                    waits=[(eV, f"ln1b.s.{k}")], label=f"ln1b.x2.{k}")
                wts = [(eV, f"ln1b.x2.{k}"), (eA, "ln1.ev.sxB")]
                if k >= 1:
                    wts.append((eV, f"ln1b.dq.{k-1}"))
                if k == 0:
                    wts += [(eT, "ao.mm.15"), (eA, "ao.sq.31")]
                eV.op(lambda e, xq=xq, sgm=sgm: e.tensor_tensor(
                    b.T1("hE")[:], b.T1(xq)[:], b.T1("gC")[:, sgm], ALU.mult),
                    waits=wts, label=f"ln1b.q1.{k}")
                eV.op(lambda e: e.tensor_scalar(b.T1("hE")[:], b.T1("hE")[:],
                                                RND, -RND, ALU.add, ALU.add),
                      waits=[(eV, f"ln1b.q1.{k}")], label=f"ln1b.rnd.{k}")
                eV.op(lambda e: e.tensor_scalar(b.T1("hE")[:], b.T1("hE")[:],
                                                127.0, -128.0, ALU.min, ALU.max),
                      waits=[(eV, f"ln1b.rnd.{k}")], label=f"ln1b.clip.{k}")
                eV.op(lambda e, fc=fc, ih=ih, sgm=sgm: e.tensor_tensor(
                    b.bigbuf[:, fc*T + ih*HF: fc*T + (ih+1)*HF], b.T1("hE")[:],
                    b.T1("gD")[:, sgm], ALU.mult),
                    waits=[(eV, f"ln1b.clip.{k}"), (eA, "ln1.ev.siB")],
                    label=f"ln1b.dq.{k}")

        return {"sync": sync, "scalarD": scalarD, "vector": vec}

    def ph_f1(self):
        b = self
        eT, eV, eA, eP, DQ, PS = b.eT, b.eV, b.eA, b.eP, b.DQ, b.PS

        def sync(E):
            for hc in range(HCH):
                for ih in range(2):
                    k = 2 * hc + ih
                    par = k % 2
                    gbuf = ["hA", "hB"][par]
                    vbuf = ["hC", "hD"][par]
                    wts = [(eV, f"f1.wqg.{k-2}")] if k >= 2 else [
                        (DQ[f"d_x2preR{par}"], f"x2preR.{30+par}"),
                        (eV, "ln1b.s.31")]
                    DQ[f"d_gw{par}"].op(lambda e, hc=hc, ih=ih, bu=gbuf: e.dma_start(
                        out=b.T1(bu)[:].rearrange("p (dc h) -> p dc h", h=128),
                        in_=b.GwT[hc][ih*HF:(ih+1)*HF, :]
                        .rearrange("(dc p) h -> p dc h", p=128)),
                        waits=wts, label=f"gw.{k}")
                    wts = [(eV, f"f1.wqv.{k-2}")] if k >= 2 else [
                        (DQ[f"d_x2W{par}"], f"x2W.{30+par}"),
                        (eV, "ln1b.x2.31")]
                    DQ[f"d_vw{par}"].op(lambda e, hc=hc, ih=ih, bu=vbuf: e.dma_start(
                        out=b.T1(bu)[:].rearrange("p (dc h) -> p dc h", h=128),
                        in_=b.VwT[hc][ih*HF:(ih+1)*HF, :]
                        .rearrange("(dc p) h -> p dc h", p=128)),
                        waits=wts, label=f"vw.{k}")

        def act(E):
            for hc in range(HCH):
                for ih in range(2):
                    k = 2 * hc + ih
                    par = k % 2
                    gbuf = ["hA", "hB"][par]
                    vbuf = ["hC", "hD"][par]
                    wts = [(DQ[f"d_gw{par}"], f"gw.{k}"), (eA, "cc.gB8"),
                           (eV, "p0.rndp"), (eV, "p0.rndn")]
                    if k >= 2:
                        wts.append((eV, f"f1.wqg.{k-2}"))
                    eA.op(lambda e, bu=gbuf: e.activation(
                        b.T1(bu)[:], b.T1(bu)[:], AF.Identity, bias=b.rndP[:],
                        scale=b.gB8[:, 3:4]),
                        waits=wts, label=f"f1.gr1.{k}")
                    eA.op(lambda e, bu=gbuf: e.activation(
                        b.T1(bu)[:], b.T1(bu)[:], AF.Identity, bias=b.rndN[:],
                        scale=1.0),
                        waits=[(eA, f"f1.gr1.{k}")], label=f"f1.gr2.{k}")
                    wts = [(DQ[f"d_vw{par}"], f"vw.{k}")]
                    if k >= 2:
                        wts.append((eV, f"f1.wqv.{k-2}"))
                    eA.op(lambda e, bu=vbuf: e.activation(
                        b.T1(bu)[:], b.T1(bu)[:], AF.Identity, bias=b.rndP[:],
                        scale=b.gB8[:, 4:5]),
                        waits=wts, label=f"f1.vr1.{k}")
                    eA.op(lambda e, bu=vbuf: e.activation(
                        b.T1(bu)[:], b.T1(bu)[:], AF.Identity, bias=b.rndN[:],
                        scale=1.0),
                        waits=[(eA, f"f1.vr1.{k}")], label=f"f1.vr2.{k}")
                for ih in range(2):
                    k = 2 * hc + ih
                    if k >= 2:
                        kp = k - 2
                        DQ[f"d_hW{kp % 2}"].op(lambda e, kp=kp: e.dma_start(
                            out=b.h_sp[kp // 2][:, (kp % 2)*HF:(kp % 2 + 1)*HF],
                            in_=b.T1("gC")[:, (kp % 2)*HF:(kp % 2 + 1)*HF]),
                            waits=[(eV, f"f1.amax.{kp}")], label=f"hW.{kp}")
                    wts = [(eT, f"f1.gmm.{hc}")]
                    if k >= 2:
                        wts += [(eV, f"f1.amax.{k-2}"),
                                (DQ[f"d_hW{ih}"], f"hW.{k-2}")]
                    else:
                        wts.append((eV, "ln1b.dq.31"))
                    eA.op(lambda e, hc=hc, ih=ih: e.activation(
                        b.T1("gC")[:, hsl(ih)], PS[hc % 2][:, hsl(ih)], AF.Silu,
                        bias=b.gb_t[:, hc:hc+1], scale=b.gB8[:, 0:1]),
                        waits=wts, label=f"f1.gev.{k}")
                    wts = [(eT, f"f1.vmm.{hc}")]
                    if k >= 1:
                        wts.append((eV, f"f1.h.{k-1}"))
                    else:
                        wts.append((eV, "ln1b.dq.31"))
                    eA.op(lambda e, hc=hc, ih=ih: e.activation(
                        b.T1("hE")[:], PS[(hc+1) % 2][:, hsl(ih)], AF.Identity,
                        bias=b.vb_t[:, hc:hc+1], scale=b.gB8[:, 1:2]),
                        waits=wts, label=f"f1.vev.{k}")
                    if k == 127:
                        for kp in (126, 127):
                            DQ[f"d_hW{kp % 2}"].op(lambda e, kp=kp: e.dma_start(
                                out=b.h_sp[kp // 2][:, (kp % 2)*HF:
                                                    (kp % 2 + 1)*HF],
                                in_=b.T1("gC")[:, (kp % 2)*HF:(kp % 2 + 1)*HF]),
                                waits=[(eV, f"f1.amax.{kp}")], label=f"hW.{kp}")

        def vec(E):
            for hc in range(HCH):
                for ih in range(2):
                    k = 2 * hc + ih
                    par = k % 2
                    wts = [(eA, f"f1.gr2.{k}")]
                    if k <= 1:
                        wts.append((eT, "p2.qmm.15"))
                    if k >= 4:
                        wts.append((eT, f"f1.gmm.{hc-2}"))
                    eV.op(lambda e, hc=hc, ih=ih, p=par: e.tensor_scalar(
                        b.arena[:, (hc % 2)*D + ih*HF: (hc % 2)*D + (ih+1)*HF],
                        b.T1(["hA", "hB"][p])[:], 1.0, -1.0, ALU.min, ALU.max),
                        waits=wts, label=f"f1.wqg.{k}")
                    wts = [(eA, f"f1.vr2.{k}")]
                    if k <= 1:
                        wts.append((eT, "p2.qmm.15"))
                    if k >= 4:
                        wts.append((eT, f"f1.vmm.{hc-2}"))
                    eV.op(lambda e, hc=hc, ih=ih, p=par: e.tensor_scalar(
                        b.arena[:, (2 + hc % 2)*D + ih*HF:
                                (2 + hc % 2)*D + (ih+1)*HF],
                        b.T1(["hC", "hD"][p])[:], 1.0, -1.0, ALU.min, ALU.max),
                        waits=wts, label=f"f1.wqv.{k}")
                for ih in range(2):
                    k = 2 * hc + ih
                    wts = [(eA, f"f1.gev.{k}"), (eA, f"f1.vev.{k}")]
                    eV.op(lambda e, ih=ih: e.tensor_tensor(
                        b.T1("gC")[:, hsl(ih)], b.T1("gC")[:, hsl(ih)],
                        b.T1("hE")[:], ALU.mult),
                        waits=wts, label=f"f1.h.{k}")
                    if k == 0:
                        eV.op(lambda e: e.scalar_tensor_tensor(
                            b.T1("gD")[:, 0:HF], b.T1("gC")[:, 0:HF], -1.0,
                            b.T1("gC")[:, 0:HF], ALU.mult, ALU.max),
                            waits=[(eV, "f1.h.0"), (eV, "ln1b.dq.31")],
                            label="f1.amax.0")
                    elif k == 1:
                        eV.op(lambda e: e.scalar_tensor_tensor(
                            b.T1("gD")[:, HF:T], b.T1("gC")[:, HF:T], -1.0,
                            b.T1("gC")[:, HF:T], ALU.mult, ALU.max),
                            waits=[(eV, "f1.h.1"), (eV, "f1.amax.0")],
                            label="f1.amax.1")
                    else:
                        eV.op(lambda e, ih=ih: e.scalar_tensor_tensor(
                            b.T1("gD")[:, hsl(ih)], b.T1("gC")[:, hsl(ih)], -1.0,
                            b.T1("gD")[:, hsl(ih)], ALU.mult, ALU.max),
                            waits=[(eV, f"f1.h.{k}"), (eV, f"f1.amax.{k-1}")],
                            label=f"f1.amaxa.{k}")
                        eV.op(lambda e, ih=ih: e.tensor_tensor(
                            b.T1("gD")[:, hsl(ih)], b.T1("gD")[:, hsl(ih)],
                            b.T1("gC")[:, hsl(ih)], ALU.max),
                            waits=[(eV, f"f1.amaxa.{k}")],
                            label=f"f1.amax.{k}")

        def pe(E):
            for hc in range(HCH):
                par = hc % 2
                G1, G2 = PS[hc % 2], PS[(hc + 1) % 2]
                wts = [(eV, f"f1.wqg.{2*hc+1}"), (eV, "ln1b.dq.31")]
                if hc >= 1:
                    wts.append((eA, f"f1.vev.{2*hc-1}"))
                else:
                    wts += [(eA, "ln1.ev.sxB"), (eA, "ln1.ev.siB")]
                for dch in range(NCH):
                    for tc in range(TCOL):
                        last = dch == NCH - 1 and tc == TCOL - 1
                        eT.op(lambda e, G1=G1, dch=dch, tc=tc, p=par: e.matmul(
                            G1[:, tc*512:(tc+1)*512],
                            b.arena[:, p*D + dch*128: p*D + (dch+1)*128],
                            b.bigbuf[:, dch*T + tc*512: dch*T + (tc+1)*512],
                            start=(dch == 0), stop=(dch == NCH - 1)),
                            waits=(wts if (dch == 0 and tc == 0) else ()),
                            label=(f"f1.gmm.{hc}" if last else None))
                wts = [(eV, f"f1.wqv.{2*hc+1}")]
                if hc >= 1:
                    wts.append((eA, f"f1.gev.{2*hc}"))
                else:
                    wts += [(eA, "ln1.ev.mB"), (eA, "ln1.ev.rB")]
                for dch in range(NCH):
                    for tc in range(TCOL):
                        last = dch == NCH - 1 and tc == TCOL - 1
                        eT.op(lambda e, G2=G2, dch=dch, tc=tc, p=par: e.matmul(
                            G2[:, tc*512:(tc+1)*512],
                            b.arena[:, (2+p)*D + dch*128: (2+p)*D + (dch+1)*128],
                            b.bigbuf[:, dch*T + tc*512: dch*T + (tc+1)*512],
                            start=(dch == 0), stop=(dch == NCH - 1)),
                            waits=(wts if (dch == 0 and tc == 0) else ()),
                            label=(f"f1.vmm.{hc}" if last else None))

        return {"sync": sync, "scalar": act, "vector": vec,
                "tensor": pe}

    def ph_f1b(self):
        b = self
        eT, eV, eA, eP, DQ, PS = b.eT, b.eV, b.eA, b.eP, b.DQ, b.PS

        def pe(E):
            for j in range(16):
                wts = [(eV, "f1.amax.127")]
                if j == 0:
                    wts += [(eA, "f1.gev.127"), (eA, "f1.vev.127")]
                if j >= 2:
                    wts.append((eV, f"f1b.red.{j-2}"))
                eT.op(lambda e, j=j: e.transpose(
                    PS[j % 2][:, 0:128], b.T1("gD")[:, j*128:(j+1)*128],
                    b.eye_t[:]),
                    waits=wts, label=f"f1b.tr.{j}")
            eT.op(lambda e: e.transpose(PS[0][0:16, 128:256], b.hT16[:],
                                        b.eye_t[:]),
                  waits=[(eV, "f1b.sh"), (eV, "f1b.red.14")], label="f1b.rtr.0")
            eT.op(lambda e: e.transpose(PS[1][0:16, 128:256], b.dT16[:],
                                        b.eye_t[:]),
                  waits=[(eV, "f1b.dq3"), (eV, "f1b.red.15")], label="f1b.rtr.1")

        def vec(E):
            for j in range(16):
                wts = [(eT, f"f1b.tr.{j}")]
                if j == 0:
                    wts.append((eV, "f1.amax.127"))
                eV.op(lambda e, j=j: e.tensor_reduce(
                    b.t16a[:, j:j+1], PS[j % 2][0:128, 0:128], AX.X, ALU.max),
                    waits=wts, label=f"f1b.red.{j}")
            eV.op(lambda e: e.tensor_scalar(b.t16a[:], b.t16a[:], 1e-5, None,
                                            ALU.max),
                  waits=[(eV, "f1b.red.15")], label="f1b.gc")
            eV.op(lambda e: e.reciprocal(b.hT16[:], b.t16a[:]),
                  waits=[(eV, "f1b.gc")], label="f1b.ginv")
            eV.op(lambda e: e.tensor_scalar(b.hT16[:], b.hT16[:], 127.0, None,
                                            ALU.mult),
                  waits=[(eV, "f1b.ginv")], label="f1b.sh")
            eV.op(lambda e: e.tensor_scalar(b.dT16[:], b.t16a[:], b.gB8[:, 2:3],
                                            1.0 / 127.0, ALU.mult, ALU.mult),
                  waits=[(eV, "f1b.gc"), (eA, "cc.gB8")], label="f1b.dq3")

        def act(E):
            eA.op(lambda e: e.activation(b.rbounce[:], PS[0][0:16, 128:256],
                                         AF.Identity),
                  waits=[(eT, "f1b.rtr.0"), (DQ["d_bnc"], "bncR.3")],
                  label="f1b.rev.0")
            eA.op(lambda e: e.activation(b.rbounce[:], PS[1][0:16, 128:256],
                                         AF.Identity),
                  waits=[(eT, "f1b.rtr.1"), (DQ["d_bnc"], "bncW.4")],
                  label="f1b.rev.1")

        def sync(E):
            DQ["d_bnc"].op(lambda e: e.dma_start(
                out=b.bounce[4:5, :].rearrange("r (p f) -> (r p) f", p=16),
                in_=b.rbounce[:]),
                waits=[(eA, "f1b.rev.0"), (DQ["d_bnc"], "bncR.3")],
                label="bncW.4")
            DQ["d_bnc"].op(lambda e: e.dma_start(out=b.sHrow,
                                                 in_=b.bounce[4:5, :]),
                           waits=[(DQ["d_bnc"], "bncW.4"),
                                  (eT, "ln1.bc.sxB")], label="bncR.4")
            DQ["d_bnc"].op(lambda e: e.dma_start(
                out=b.bounce[5:6, :].rearrange("r (p f) -> (r p) f", p=16),
                in_=b.rbounce[:]),
                waits=[(eA, "f1b.rev.1"), (DQ["d_bnc"], "bncR.4")],
                label="bncW.5")
            DQ["d_bnc"].op(lambda e: e.dma_start(out=b.dq3row,
                                                 in_=b.bounce[5:6, :]),
                           waits=[(DQ["d_bnc"], "bncW.5"),
                                  (eT, "ln1.bc.siB")], label="bncR.5")

        return {"tensor": pe, "vector": vec, "scalar": act, "sync": sync}

    def ph_f2(self, tb_sel):
        b = self
        eT, eV, eA, eP, DQ, PS = b.eT, b.eV, b.eA, b.eP, b.DQ, b.PS

        def prev_kf(tb, fc):
            return (tb, fc - 2) if fc >= 2 else (tb - 1, 14 + fc)

        def pe(E):
            for tb in [tb_sel]:
                for t2 in range(2):
                    wts = []
                    if t2 == 0:
                        wts = [(DQ["d_bnc"], "bncR.4"), (eV, "f1b.red.14"),
                               (eA, "f1b.rev.0")]
                        if tb == 0:
                            wts += [(eT, "f1.gmm.63"), (eA, "f1.gev.127")]
                        else:
                            wts += [(eA, "f2.sHB.0"), (eV, "f2.hq1.63"),
                                    (eA, "f2.embc.0"), (eA, "f2.erbc.0")]
                    eT.op(lambda e, tb=tb, t2=t2: e.matmul(
                        b.psA[:, t2*512:(t2+1)*512], b.onesR_t[:],
                        b.sHrow[0:1, tb*TB + t2*512: tb*TB + (t2+1)*512],
                        start=True, stop=True),
                        waits=wts, label=(f"f2.shmm.{tb}" if t2 == 1 else None))
                for t2 in range(2):
                    wts = []
                    if t2 == 0:
                        wts = [(DQ["d_bnc"], "bncR.5"), (eV, "f1b.red.15"),
                               (eA, "f1b.rev.1")]
                        if tb == 0:
                            wts += [(eT, "f1.vmm.63"), (eA, "f1.vev.127")]
                        else:
                            wts += [(eA, "f2.dq3B.0"), (eV, "f2.o1.0.15")]
                    eT.op(lambda e, tb=tb, t2=t2: e.matmul(
                        b.psB[:, t2*512:(t2+1)*512], b.onesR_t[:],
                        b.dq3row[0:1, tb*TB + t2*512: tb*TB + (t2+1)*512],
                        start=True, stop=True),
                        waits=wts, label=(f"f2.dqmm.{tb}" if t2 == 1 else None))
                for fc in range(16):
                    par = fc % 2
                    G = PS[fc % 2]
                    k = tb * 16 + fc
                    wts = [(eV, f"f2.hint.{tb}.63")]
                    if fc == 0:
                        wts.append((eA, f"f2.sHB.{tb}"))
                    if fc == 1:
                        wts.append((eA, f"f2.dq3B.{tb}"))
                    if k >= 2:
                        pt, pf = prev_kf(tb, fc)
                        wts.append((eV, f"f2.o1.{pt}.{pf}"))
                    if tb == 1 and fc <= 1:
                        wts.append((eA, "f2.erbc.0"))
                    for hc in range(HCH):
                        src = b.bigbuf if hc < 32 else b.arena
                        off = (hc % 32) * TB
                        q = hc // 16
                        qb = b.T2(f"owq{(4 * fc + q) % 2}")
                        for t2 in range(2):
                            last = hc == HCH - 1 and t2 == 1
                            qlast = (hc % 16 == 15) and t2 == 1
                            w2 = list(wts) if (hc == 0 and t2 == 0) else []
                            if hc % 16 == 0 and t2 == 0:
                                w2.append((DQ[f"d_owqR{(4*fc+q) % 2}"],
                                           f"owqR.{tb}.{fc}.{q}"))
                            lab = None
                            if last:
                                lab = f"f2.mm.{tb}.{fc}"
                            elif qlast:
                                lab = f"f2.mmq.{tb}.{fc}.{q}"
                            eT.op(lambda e, G=G, hc=hc, t2=t2, src=src, off=off,
                                  qb=qb: e.matmul(
                                      G[:, t2*512:(t2+1)*512],
                                      qb[:, (hc % 16)*128:(hc % 16 + 1)*128],
                                      src[:, off + t2*512: off + (t2+1)*512],
                                      start=(hc == 0), stop=(hc == HCH - 1)),
                                  waits=w2, label=lab)
                for i in range(2):
                    src = [b.T2("s2sum"), b.T2("s2sq")][i]
                    dep = [f"f2.ssum.{tb}.15", f"f2.ssq.{tb}.15"][i]
                    for t2 in range(2):
                        wts = []
                        if t2 == 0:
                            wts = [(eV, dep), (eV, f"f2.o1.{tb}.15")]
                            if tb == 0 and i == 0:
                                wts.append((eA, "f2.sHB.0"))
                            if tb == 1 and i == 0:
                                wts.append((eA, "f2.rsum.0"))
                            if tb == 1 and i == 1:
                                wts.append((eA, "f2.rsq.0"))
                        dst = (b.psA[0:1, 1024 + t2*512: 1024 + (t2+1)*512]
                               if i == 0 else
                               b.psB[0:1, 1024 + t2*512: 1024 + (t2+1)*512])
                        eT.op(lambda e, dst=dst, src=src, t2=t2: e.matmul(
                            dst, b.onesC_t[:], src[:, t2*512:(t2+1)*512],
                            start=True, stop=True),
                            waits=wts,
                            label=(f"f2.rmm{i}.{tb}" if t2 == 1 else None))
                for i, (row, dep) in enumerate([(b.T2("m2B")[0:1, 0:TB], f"f2.var.{tb}"),
                                                (b.T2("r2B")[0:1, 0:TB], f"f2.rstd.{tb}")]):
                    for t2 in range(2):
                        wts = []
                        if t2 == 0:
                            wts = [(eV, dep), (eV, f"f2.o1.{tb}.15"),
                                   (eV, f"f2.o1.{tb}.14")]
                            if i == 1:
                                wts.append((eA, f"f2.embc.{tb}"))
                        eT.op(lambda e, row=row, t2=t2: e.matmul(
                            b.psA[:, t2*512:(t2+1)*512], b.onesR_t[:],
                            row[0:1, t2*512:(t2+1)*512], start=True, stop=True),
                            waits=wts,
                            label=(f"f2.bc{i}.{tb}" if t2 == 1 else None))

        def act(E):
            for tb in [tb_sel]:
                eA.op(lambda e: e.activation(b.T2("sHB")[:], b.psA[:, 0:TB],
                                             AF.Identity),
                      waits=[(eT, f"f2.shmm.{tb}")], label=f"f2.sHB.{tb}")
                eA.op(lambda e: e.activation(b.T2("dq3B")[:], b.psB[:, 0:TB],
                                             AF.Identity),
                      waits=[(eT, f"f2.dqmm.{tb}")], label=f"f2.dq3B.{tb}")
                for fc in range(16):
                    par = fc % 2
                    wts = [(eV, f"f2.res.{tb}.{fc}")]
                    if fc >= 1:
                        wts.append((eV, f"f2.ssq.{tb}.{fc-1}"))
                    elif tb == 1:
                        wts.append((eV, "f2.ssq.0.15"))
                    eA.op(lambda e, p=par: e.activation(
                        b.T2("scr4")[:], b.T2(f"o3t{p}")[:], AF.Square),
                        waits=wts, label=f"f2.sq.{tb}.{fc}")
                    DQ[f"d_o3W{par}"].op(lambda e, fc=fc, tb=tb, p=par:
                                         e.dma_start(
                        out=b.o3_sp[fc*128:(fc+1)*128, tb*TB:(tb+1)*TB],
                        in_=b.T2(f"o3t{p}")[:]),
                        waits=[(eV, f"f2.res.{tb}.{fc}")],
                        label=f"o3W.{tb}.{fc}")
                eA.op(lambda e: e.activation(b.T2("m2B")[0:1, 0:TB], b.psA[0:1, 1024:2048],
                                             AF.Identity),
                      waits=[(eT, f"f2.rmm0.{tb}")], label=f"f2.rsum.{tb}")
                eA.op(lambda e: e.activation(b.T2("r2B")[0:1, 0:TB], b.psB[0:1, 1024:2048],
                                             AF.Identity),
                      waits=[(eT, f"f2.rmm1.{tb}")], label=f"f2.rsq.{tb}")
                eA.op(lambda e: e.activation(b.T2("r2B")[0:1, 0:TB], b.T2("r2B")[0:1, 0:TB], AF.Sqrt),
                      waits=[(eV, f"f2.vare.{tb}")], label=f"f2.sqrt.{tb}")
                wts = [(eT, f"f2.bc0.{tb}")]
                if tb == 1:
                    wts.append((eV, "f3.s.0.15"))
                eA.op(lambda e: e.activation(b.T2("m2B")[:], b.psA[:, 0:TB],
                                             AF.Identity),
                      waits=wts, label=f"f2.embc.{tb}")
                wts = [(eT, f"f2.bc1.{tb}")]
                if tb == 1:
                    wts.append((eV, "f3.o.0.15"))
                eA.op(lambda e: e.activation(b.T2("r2B")[:], b.psA[:, 0:TB],
                                             AF.Identity),
                      waits=wts, label=f"f2.erbc.{tb}")

        def sync(E):
            for tb in [tb_sel]:
                for hc in range(HCH):
                    par = hc % 2
                    k = tb * HCH + hc
                    wts = []
                    if tb == 0:
                        wts += [(DQ["d_hW0"], "hW.126"),
                                (DQ["d_hW1"], "hW.127")]
                    if k >= 2:
                        wts.append((eV, f"f2.hq1.{k-2}"))
                    if tb == 1 and hc <= 1:
                        wts.append((eV, f"f3.o.0.{14+hc}"))
                    DQ[f"d_hR{par}"].op(lambda e, hc=hc, tb=tb, p=par: e.dma_start(
                        out=b.T2(f"hrd{p}")[:],
                        in_=b.h_sp[hc][:, tb*TB:(tb+1)*TB]),
                        waits=wts, label=f"hR.{k}")
                for fc in range(16):
                    k = tb * 16 + fc
                    for q in range(4):
                        g = tb * 64 + 4 * fc + q
                        wts = []
                        if tb == 0:
                            wts.append((DQ["d_owqW"], "owqW.127"))
                        if g >= 2:
                            ptb, r = divmod(g - 2, 64)
                            pfc, pq = divmod(r, 4)
                            wts.append((eT, f"f2.mm.{ptb}.{pfc}" if pq == 3
                                        else f"f2.mmq.{ptb}.{pfc}.{pq}"))
                        DQ[f"d_owqR{(4*fc+q) % 2}"].op(
                            lambda e, fc=fc, q=q: e.dma_start(
                                out=b.T2(f"owq{(4*fc+q) % 2}")[:],
                                in_=b.owq_cache[fc][:, q*2048:(q+1)*2048]),
                            waits=wts, label=f"owqR.{tb}.{fc}.{q}")
                    par = fc % 2
                    wts = []
                    if tb == 0:
                        wts += [(DQ["d_x2W0"], "x2W.30"),
                                (DQ["d_x2W1"], "x2W.31")]
                    if k >= 2:
                        pt, pf = prev_kf(tb, fc)
                        wts.append((eV, f"f2.res.{pt}.{pf}"))
                    DQ[f"d_x2R{par}"].op(lambda e, fc=fc, tb=tb, p=par: e.dma_start(
                        out=b.T2(f"x2r{p}")[:],
                        in_=b.x2_sp[fc*128:(fc+1)*128, tb*TB:(tb+1)*TB]),
                        waits=wts, label=f"x2R.{tb}.{fc}")

        def vec(E):
            for tb in [tb_sel]:
                for hc in range(HCH):
                    par = hc % 2
                    k = tb * HCH + hc
                    wts = [(DQ[f"d_hR{par}"], f"hR.{k}"), (eA, f"f2.sHB.{tb}")]
                    if k >= 1:
                        pt, ph_ = (tb, hc - 1) if hc >= 1 else (tb - 1, 63)
                        wts.append((eV, f"f2.hint.{pt}.{ph_}"))
                    if k == 0:
                        wts += [(eT, "f1.gmm.63"), (eT, "f1.vmm.63"),
                                (eV, "f1.wqg.127"), (eV, "f1.wqv.127")]
                    if tb == 1 and hc == 0:
                        wts += [(eT, "f2.mm.0.15"), (eV, "f3.o.0.15")]
                    eV.op(lambda e, p=par: e.tensor_tensor(
                        b.T2("scr4")[:], b.T2(f"hrd{p}")[:], b.T2("sHB")[:],
                        ALU.mult),
                        waits=wts, label=f"f2.hq1.{k}")
                    eV.op(lambda e: e.tensor_scalar(
                        b.T2("scr4")[:], b.T2("scr4")[:], RND, -RND,
                        ALU.add, ALU.add),
                        waits=[(eV, f"f2.hq1.{k}")], label=f"f2.hrnd.{k}")
                    dst = b.bigbuf if hc < 32 else b.arena
                    off = (hc % 32) * TB
                    eV.op(lambda e, d=dst, o=off: e.tensor_scalar(
                        d[:, o:o+TB], b.T2("scr4")[:], 127.0, -128.0,
                        ALU.min, ALU.max),
                        waits=[(eV, f"f2.hrnd.{k}")], label=f"f2.hint.{tb}.{hc}")
                for fc in range(16):
                    par = fc % 2
                    k = tb * 16 + fc
                    wts = [(eT, f"f2.mm.{tb}.{fc}"), (eA, f"f2.dq3B.{tb}"),
                           (DQ[f"d_x2R{par}"], f"x2R.{tb}.{fc}")]
                    if k >= 2:
                        pt, pf = prev_kf(tb, fc)
                        wts.append((DQ[f"d_o3W{par}"], f"o3W.{pt}.{pf}"))
                    if tb == 1 and fc <= 1:
                        wts.append((DQ["d_out"], f"out.0.{14+fc}"))
                    eV.op(lambda e, p=par, tb=tb, fc=fc: e.tensor_tensor(
                        b.T2(f"o3t{p}")[:], PS[fc % 2][:, 0:TB], b.T2("dq3B")[:],
                        ALU.mult),
                        waits=wts, label=f"f2.o1.{tb}.{fc}")
                    eV.op(lambda e, p=par, fc=fc: e.tensor_scalar(
                        b.T2(f"o3t{p}")[:], b.T2(f"o3t{p}")[:],
                        b.ob_t[:, fc:fc+1], None, ALU.add),
                        waits=[(eV, f"f2.o1.{tb}.{fc}")],
                        label=f"f2.o2.{tb}.{fc}")
                    eV.op(lambda e, p=par: e.tensor_tensor(
                        b.T2(f"o3t{p}")[:], b.T2(f"o3t{p}")[:],
                        b.T2(f"x2r{p}")[:], ALU.add),
                        waits=[(eV, f"f2.o2.{tb}.{fc}")],
                        label=f"f2.res.{tb}.{fc}")
                    if fc == 0:
                        eV.op(lambda e, p=par: e.tensor_copy(
                            b.T2("s2sum")[:], b.T2(f"o3t{p}")[:]),
                            waits=[(eV, f"f2.res.{tb}.0")] + (
                                [(eT, "f2.rmm0.0")] if tb else []),
                            label=f"f2.ssum.{tb}.0")
                        eV.op(lambda e: e.tensor_copy(b.T2("s2sq")[:],
                                                      b.T2("scr4")[:]),
                              waits=[(eA, f"f2.sq.{tb}.0")] + (
                                  [(eT, "f2.rmm1.0")] if tb else []),
                              label=f"f2.ssq.{tb}.0")
                    else:
                        eV.op(lambda e, p=par: e.tensor_tensor(
                            b.T2("s2sum")[:], b.T2("s2sum")[:],
                            b.T2(f"o3t{p}")[:], ALU.add),
                            waits=[(eV, f"f2.res.{tb}.{fc}"),
                                   (eV, f"f2.ssum.{tb}.{fc-1}")],
                            label=f"f2.ssum.{tb}.{fc}")
                        eV.op(lambda e: e.tensor_tensor(
                            b.T2("s2sq")[:], b.T2("s2sq")[:], b.T2("scr4")[:],
                            ALU.add),
                            waits=[(eA, f"f2.sq.{tb}.{fc}"),
                                   (eV, f"f2.ssq.{tb}.{fc-1}")],
                            label=f"f2.ssq.{tb}.{fc}")
                eV.op(lambda e: e.tensor_scalar(b.T2("m2B")[0:1, 0:TB], b.T2("m2B")[0:1, 0:TB], 1.0 / D, None,
                                                ALU.mult),
                      waits=[(eA, f"f2.rsum.{tb}")] + (
                          [(eT, "f2.bc0.0"), (eT, "f2.bc1.0")] if tb else []),
                      label=f"f2.m2.{tb}")
                eV.op(lambda e: e.tensor_tensor(
                    b.T2("scr4")[0:1, 0:TB], b.T2("m2B")[0:1, 0:TB],
                    b.T2("m2B")[0:1, 0:TB], ALU.mult),
                      waits=[(eV, f"f2.m2.{tb}"), (eV, f"f2.ssq.{tb}.15"),
                             (eA, f"f2.sq.{tb}.15")],
                      label=f"f2.msq.{tb}")
                eV.op(lambda e: e.tensor_scalar(b.T2("r2B")[0:1, 0:TB], b.T2("r2B")[0:1, 0:TB], 1.0 / D, None,
                                                ALU.mult),
                      waits=[(eA, f"f2.rsq.{tb}")], label=f"f2.e2.{tb}")
                eV.op(lambda e: e.tensor_tensor(
                    b.T2("r2B")[0:1, 0:TB], b.T2("r2B")[0:1, 0:TB],
                    b.T2("scr4")[0:1, 0:TB], ALU.subtract),
                      waits=[(eV, f"f2.msq.{tb}"), (eV, f"f2.e2.{tb}")],
                      label=f"f2.var.{tb}")
                eV.op(lambda e: e.tensor_scalar(b.T2("r2B")[0:1, 0:TB], b.T2("r2B")[0:1, 0:TB], LN_EPS, None,
                                                ALU.add),
                      waits=[(eV, f"f2.var.{tb}")], label=f"f2.vare.{tb}")
                eV.op(lambda e: e.reciprocal(b.T2("r2B")[0:1, 0:TB], b.T2("r2B")[0:1, 0:TB]),
                      waits=[(eA, f"f2.sqrt.{tb}")], label=f"f2.rstd.{tb}")
                for fc in range(16):
                    par = fc % 2
                    k = tb * 16 + fc
                    wts = [(DQ[f"d_o3R{par}"], f"o3R.{tb}.{fc}"),
                           (eA, f"f2.embc.{tb}"), (eA, f"f2.erbc.{tb}")]
                    if k >= 2:
                        pt, pf = prev_kf(tb, fc)
                        wts.append((DQ["d_out"], f"out.{pt}.{pf}"))
                    eV.op(lambda e, p=par: e.tensor_tensor(
                        b.T2(f"o3t{p}")[:], b.T2(f"hrd{p}")[:], b.T2("m2B")[:],
                        ALU.subtract), waits=wts, label=f"f3.s.{tb}.{fc}")
                    eV.op(lambda e, p=par: e.tensor_tensor(
                        b.T2(f"o3t{p}")[:], b.T2(f"o3t{p}")[:], b.T2("r2B")[:],
                        ALU.mult),
                        waits=[(eV, f"f3.s.{tb}.{fc}")], label=f"f3.o.{tb}.{fc}")

        def sync2(E):
            for tb in [tb_sel]:
                for fc in range(16):
                    par = fc % 2
                    k = tb * 16 + fc
                    wts = [(DQ["d_o3W0"], f"o3W.{tb}.14"),
                           (DQ["d_o3W1"], f"o3W.{tb}.15"),
                           (eV, f"f2.hq1.{tb*HCH + HCH - 1}")]
                    if k >= 2:
                        pt, pf = prev_kf(tb, fc)
                        wts.append((eV, f"f3.s.{pt}.{pf}"))
                    DQ[f"d_o3R{par}"].op(lambda e, fc=fc, tb=tb, p=par: e.dma_start(
                        out=b.T2(f"hrd{p}")[:],
                        in_=b.o3_sp[fc*128:(fc+1)*128, tb*TB:(tb+1)*TB]),
                        waits=wts, label=f"o3R.{tb}.{fc}")

        def scalarD2(E):
            for tb in [tb_sel]:
                for fc in range(16):
                    par = fc % 2
                    wts = [(eV, f"f3.o.{tb}.{fc}")]
                    if tb * 16 + fc >= 1:
                        pt, pf = (tb, fc - 1) if fc >= 1 else (tb - 1, 15)
                        wts.append((DQ["d_out"], f"out.{pt}.{pf}"))
                    DQ["d_out"].op(lambda e, fc=fc, tb=tb, p=par: e.dma_start(
                        out=b.OUT[fc*128:(fc+1)*128, tb*TB:(tb+1)*TB],
                        in_=b.T2(f"o3t{p}")[:]),
                        waits=wts, label=f"out.{tb}.{fc}")

        return {"tensor": pe, "scalar": act, "sync": sync,
                "vector": vec, "sync2": sync2, "scalarD2": scalarD2}

    def ph_fin(self):
        b = self
        DQ = b.DQ

        def sync(E):
            for tp in b.taps:
                if tp == "A":
                    DQ["d_tap"].op(lambda e: e.dma_start(out=b.tap_outs["A"],
                                                         in_=b.acol2[:]),
                                   waits=[(b.eA, "cc.acol2")], label="tap.A")
                elif tp == "gvals":
                    DQ["d_tap"].op(lambda e: e.dma_start(out=b.tap_outs["gvals"],
                                                         in_=b.gvals[:]),
                                   waits=[(b.eV, "cc.ginv")], label="tap.gvals")
                elif tp == "x2":
                    DQ["d_tap"].op(lambda e: e.dma_start(out=b.tap_outs["x2"],
                                                         in_=b.x2_sp),
                                   waits=[(b.DQ["d_x2W1"], "x2W.31")],
                                   label="tap.x2")
                elif tp == "x2pre":
                    DQ["d_tap"].op(lambda e: e.dma_start(out=b.tap_outs["x2pre"],
                                                         in_=b.x2pre_sp),
                                   waits=[(b.DQ["d_x2preW1"], "x2preW.31")],
                                   label="tap.x2pre")
                elif tp == "h":
                    DQ["d_tap"].op(lambda e: e.dma_start(
                        out=b.tap_outs["h"],
                        in_=b.h_sp.rearrange("a p t -> (a p) t")),
                        waits=[(b.DQ["d_hW1"], "hW.127")], label="tap.h")
            E.wait_val(DQ["d_out"], 16 * 32)
            if b.taps:
                E.wait_val(DQ["d_tap"], 16 * len(b.taps))

        return {"sync": sync}

    def run_phases(self, block=None):
        order1 = [("p0", self.ph_p0), ("p1", self.ph_p1), ("p1b", self.ph_p1b),
                  ("pcc", self.ph_pcc), ("p2", self.ph_p2), ("ao", self.ph_ao),
                  ("ln1", self.ph_ln1), ("ln1b", self.ph_ln1b),
                  ("f1", self.ph_f1), ("f1b", self.ph_f1b)]
        order2 = [("f2t0", lambda: self.ph_f2(0)),
                  ("f2t1", lambda: self.ph_f2(1)), ("fin", self.ph_fin)]
        sect_engine = {"sync": "sync", "gpsimd": "gpsimd", "tensor": "tensor",
                       "scalar": "scalar", "vector": "vector",
                       "scalarD": "scalar", "sync2": "sync",
                       "scalarD2": "scalar", "gpsimd2": "gpsimd"}

        def emit(order):
            for pname, pfn in order:
                secs = pfn()
                for sec, fn in secs.items():
                    engname = sect_engine[sec]
                    if self.mode == "count":
                        fn(_FakeE())
                    else:
                        def runner(engobj, fn=fn):
                            for e in ([self.eT, self.eV, self.eA, self.eP]
                                      + list(self.DQ.values())):
                                e.real = engobj
                            fn(_SecE(self, engobj))
                        getattr(block, engname)(runner)

        emit(order1)
        if self.mode == "emit":
            self.era1.close()
            self.open_era2()
        else:
            for n in ["owq0", "owq1", "sHB", "dq3B", "scr4", "hrd0", "hrd1",
                      "x2r0", "x2r1", "o3t0", "o3t1", "s2sum", "s2sq",
                      "m2B", "r2B"]:
                self.t2[n] = self.arena
        emit(order2)


def build_nc(taps=()):
    b = Builder(taps=taps)
    b.open_era1()
    b.mode = "count"
    b.run_phases()
    counts = {e.name: e.cnt for e in [b.eT, b.eV, b.eA, b.eP] + list(b.DQ.values())}
    for e in [b.eT, b.eV, b.eA, b.eP] + list(b.DQ.values()):
        e.cnt = 0
    b.mode = "emit"
    for e in [b.eT, b.eV, b.eA, b.eP] + list(b.DQ.values()):
        cm = b.nc.semaphore(e.name)
        e.sem = cm.__enter__()
        b.stack.callback(cm.__exit__, None, None, None)
    with b.nc.Block() as block:
        b.run_phases(block)
    for e in [b.eT, b.eV, b.eA, b.eP] + list(b.DQ.values()):
        assert e.cnt == counts[e.name], (e.name, e.cnt, counts[e.name])
    b.era2.close()
    return b.nc



def _prep_shared(inp):
    f32 = np.float32
    W_in = np.asarray(inp["L0_attn_W_in"], f32)
    W_out = np.asarray(inp["L0_attn_W_out"], f32)
    gate_w = np.asarray(inp["L0_ffn_gate_w"], f32)
    val_w = np.asarray(inp["L0_ffn_val_w"], f32)
    out_w = np.asarray(inp["L0_ffn_out_w"], f32)

    def blocks_of_T(w):
        wt = np.ascontiguousarray(w.T)
        E = wt.shape[1]
        return np.ascontiguousarray(
            wt.reshape(D, E // 128, 128).transpose(1, 0, 2))

    d = {}
    d["WinT"] = blocks_of_T(W_in)
    d["WoT"] = blocks_of_T(W_out)
    d["GwT"] = blocks_of_T(gate_w)
    d["VwT"] = blocks_of_T(val_w)
    owt = np.ascontiguousarray(out_w.T)
    ow_blocks = np.empty((64, D, 128), f32)
    for fc in range(16):
        for jj in range(4):
            ow_blocks[4 * fc + jj] = owt[2048*jj:2048*(jj+1), 128*fc:128*(fc+1)]
    d["OwT"] = np.ascontiguousarray(ow_blocks)

    def bias_T(v, k):
        return np.ascontiguousarray(np.asarray(v, f32).reshape(k, 128).T)

    d["b_inT"] = bias_T(inp["L0_attn_b_in"], 48)
    d["b_outT"] = bias_T(inp["L0_attn_b_out"], 16)
    d["gate_bT"] = bias_T(inp["L0_ffn_gate_b"], 64)
    d["val_bT"] = bias_T(inp["L0_ffn_val_b"], 64)
    d["out_bT"] = bias_T(inp["L0_ffn_out_b"], 16)

    inv_freq = (1.0 / 10000.0 ** (np.arange(0, 1024, 2, dtype=f32)
                                  / 1024.0)).astype(f32)
    invfF = np.repeat(inv_freq, 2)
    d["invfF"] = np.ascontiguousarray(invfF.reshape(8, 128).T)
    d["onesR"] = np.ones((1, 128), f32)
    d["onesC"] = np.ones((128, 1), f32)
    d["eye"] = np.eye(128, dtype=f32)
    P = np.zeros((128, 128), f32)
    for j in range(64):
        P[2*j+1, 2*j] = -1.0
        P[2*j, 2*j+1] = 1.0
    d["perm"] = P
    d["zrowD"] = np.zeros((1, T), f32)
    return d


def _prep_core(inp, shared, core):
    f32 = np.float32
    X = np.asarray(inp["L0_attn_X"], f32)
    bsel = np.zeros((128, 2), f32)
    bdx = core // 4
    q = core % 4
    bsel[:, bdx] = 1.0
    dcore = dict(shared)
    dcore["XT"] = np.ascontiguousarray(X[bdx, q*T:(q+1)*T, :].T)
    dcore["ramp"] = (np.arange(T, dtype=f32) + q * T)[None, :]
    dcore["bsel"] = bsel
    gate_w = np.asarray(inp["L0_ffn_gate_w"], f32)
    val_w = np.asarray(inp["L0_ffn_val_w"], f32)
    out_w = np.asarray(inp["L0_ffn_out_w"], f32)

    def slice8(wT, c):
        ncols = wT.shape[1] // 8
        sl = wT[:, c*ncols:(c+1)*ncols]
        rows = sl.shape[0]
        npieces = rows * ncols // (D * 128)
        return np.ascontiguousarray(
            sl.reshape(rows // D, D, ncols // 128, 128)
            .transpose(0, 2, 1, 3).reshape(npieces, D, 128))

    dcore["gW8"] = slice8(np.ascontiguousarray(gate_w.T), core)
    dcore["vW8"] = slice8(np.ascontiguousarray(val_w.T), core)
    ow8 = np.ascontiguousarray(out_w.T)
    sl = ow8[:, core*256:(core+1)*256]
    dcore["oW8"] = np.ascontiguousarray(
        sl.reshape(4, D, 2, 128).transpose(0, 2, 1, 3).reshape(8, D, 128))
    return dcore


def kernel(**inputs):
    shared = _prep_shared(inputs)
    in_maps = [_prep_core(inputs, shared, c) for c in range(8)]
    nc = build_nc()
    res = run_bass_kernel_spmd(nc, in_maps, core_ids=list(range(8)))
    out = np.empty((2, 8192, D), np.float32)
    for c in range(8):
        bdx, q = c // 4, c % 4
        out[bdx, q*T:(q+1)*T, :] = res.results[c]["OUT"].T
    return out


# version 06
# speedup vs baseline: 1.0292x; 1.0292x over previous
import contextlib
import numpy as np

import concourse.bass as bass
import concourse.mybir as mybir
from concourse.bass_utils import run_bass_kernel_spmd

F32 = mybir.dt.float32
FP16 = mybir.dt.float16
AF = mybir.ActivationFunctionType
ALU = mybir.AluOpType
AX = mybir.AxisListType

RND = 12582912.0
PI = float(np.float32(np.pi))
HALF_PI = float(np.float32(np.pi / 2))
TWO_PI = float(np.float32(2 * np.pi))
INV_2PI = float(np.float32(1.0 / (2 * np.pi)))
CW_C1 = 6.28125
CW_C2 = float(np.float32(2 * np.pi - 6.28125))

T = 2048
HF = 1024
D = 2048
H = 8192
NCH = 16
HCH = 64
TCOL = 4
TB = 1024
LN_EPS = 1e-5
NW = float(H * D)


def hsl(i):
    return slice(i * HF, (i + 1) * HF)


class Eng:
    def __init__(self, owner, name, step=1):
        self.o = owner
        self.name = name
        self.step = step
        self.cnt = 0
        self.real = None
        self.sem = None

    def op(self, emit_fn, waits=(), label=None):
        self.cnt += self.step
        if self.o.mode == "count":
            if label is not None:
                key = (self.name, label)
                assert key not in self.o.tok, f"dup label {key}"
                self.o.tok[key] = self.cnt
        else:
            if label is not None:
                exp = self.o.tok[(self.name, label)]
                assert exp == self.cnt, (
                    f"token drift at {self.name}:{label}: table {exp}, "
                    f"emit {self.cnt}")
            for s_eng, lab in waits:
                self.real.wait_ge(s_eng.sem, self.o.tok[(s_eng.name, lab)])
            ins = emit_fn(self.real)
            ins.then_inc(self.sem, self.step)
        return self.cnt


class _SecE:
    def __init__(self, b, engobj):
        self.b = b
        self.engobj = engobj

    def wait(self, s_eng, lab):
        self.engobj.wait_ge(s_eng.sem, self.b.tok[(s_eng.name, lab)])

    def wait_val(self, s_eng, val):
        self.engobj.wait_ge(s_eng.sem, val)


class _FakeE:
    def wait(self, *a, **k):
        pass

    def wait_val(self, *a, **k):
        pass


class Builder:
    def __init__(self, taps=()):
        self.taps = taps
        self.mode = "count"
        self.tok = {}
        nc = self.nc = bass.Bass("TRN2", target_bir_lowering=False,
                                 dynamic_dma_scratch_size=16384)
        self.stack = contextlib.ExitStack()

        din = lambda n, s, dt=F32: nc.dram_tensor(n, s, dt, kind="ExternalInput").ap()
        self.XT = din("XT", [D, T])
        self.WinT = din("WinT", [48, D, 128])
        self.WoT = din("WoT", [16, D, 128])
        self.GwT = din("GwT", [64, D, 128])
        self.VwT = din("VwT", [64, D, 128])
        self.OwT = din("OwT", [64, D, 128])
        self.b_inT = din("b_inT", [128, 48])
        self.b_outT = din("b_outT", [128, 16])
        self.gate_bT = din("gate_bT", [128, 64])
        self.val_bT = din("val_bT", [128, 64])
        self.out_bT = din("out_bT", [128, 16])
        self.ramp = din("ramp", [1, T])
        self.invfF = din("invfF", [128, 8])
        self.onesR = din("onesR", [1, 128])
        self.onesC = din("onesC", [128, 1])
        self.eye = din("eye", [128, 128])
        self.perm = din("perm", [128, 128])
        self.bsel = din("bsel", [128, 2])
        self.zrowD = din("zrowD", [1, T])
        self.gW8 = din("gW8", [8, D, 128])
        self.vW8 = din("vW8", [8, D, 128])
        self.oW8 = din("oW8", [8, D, 128])

        self.OUT = nc.dram_tensor("OUT", [D, T], F32, kind="ExternalOutput").ap()
        dint = lambda n, s, dt=F32: nc.dram_tensor(n, s, dt, kind="Internal").ap()
        self.x2pre_sp = dint("x2pre_sp", [D, T])
        self.x2_sp = dint("x2_sp", [D, T])
        self.h_sp = dint("h_sp", [HCH, 128, T])
        self.owq_cache = dint("owq_cache", [16, 128, H], FP16)
        self.o3_sp = dint("o3_sp", [D, T])
        self.tab_sp = dint("tab_sp", [8, 2, 2, 128, HF])
        self.ccin = dint("ccin", [4, T])
        self.ccout = dint("ccout", [4, T])
        self.bounce = dint("bounce", [8, T])
        self.tap_outs = {}
        for tp in taps:
            shp = {"A": [128, 16], "gvals": [1, 8], "x2": [D, T],
                   "x2pre": [D, T], "h": [HCH * 128, T],
                   "kv": [128, NCH * T], "srow": [1, T],
                   "arena": [128, NCH * T], "k0": [128, T],
                   "v0": [128, T], "rope0": [128, T],
                   "nsq": [128, T], "sq00": [128, HF],
                   "nsq0": [128, T],
                   **{f"gc{n}": [128, T] for n in range(16)},
                   **{f"hc{n}": [128, HF] for n in range(16)},
                   "rampend": [128, T], "ka3": [128, T],
                   "kr3": [128, T]}[tp]
            dtt = FP16 if tp in ("kv", "arena") else F32
            self.tap_outs[tp] = nc.dram_tensor("tap_" + tp, shp, dtt,
                                               kind="ExternalOutput").ap()

        sb = lambda n, s, dt: self.stack.enter_context(nc.sbuf_tensor(n, s, dt))
        self.arena = sb("arena", [128, NCH * T], FP16)
        self.bigbuf = sb("bigbuf", [128, NCH * T], FP16)
        self.bin_t = sb("bin_t", [128, 48], F32)
        self.bout_t = sb("bout_t", [128, 16], F32)
        self.gb_t = sb("gb_t", [128, 64], F32)
        self.vb_t = sb("vb_t", [128, 64], F32)
        self.ob_t = sb("ob_t", [128, 16], F32)
        self.invf_t = sb("invf_t", [128, 8], F32)
        self.onesR_t = sb("onesR_t", [1, 128], F32)
        self.onesC_t = sb("onesC_t", [128, 1], F32)
        self.eye_t = sb("eye_t", [128, 128], F32)
        self.perm_t = sb("perm_t", [128, 128], F32)
        self.bsel_t = sb("bsel_t", [128, 2], F32)
        self.acol = sb("acol", [128, 16], F32)
        self.acol2 = sb("acol2", [128, 16], F32)
        self.rbounce = sb("rbounce", [16, 128], F32)
        self.acolB = sb("acolB", [128, 16], F32)
        self.gacc = sb("gacc", [128, 4], F32)
        self.gcells = sb("gcells", [1, 4], F32)
        self.gvals = sb("gvals", [1, 8], F32)
        self.gB8 = sb("gB8", [128, 8], F32)
        self.rndP = sb("rndP", [128, 1], F32)
        self.rndN = sb("rndN", [128, 1], F32)
        for n in ["sT16", "mT16", "rT16", "siT16", "hT16", "dT16", "t16a", "t16b"]:
            setattr(self, n, sb(n, [128, 16], F32))
        self.rowbuf = sb("rowbuf", [1, 2 * T], F32)
        self.rampR = self.rowbuf[0:1, 0:T]
        self.nrowR = self.rowbuf[0:1, 0:T]
        self.ln1rows = [self.rowbuf[0:1, 0:T], self.rowbuf[0:1, T:2*T]]
        self.sHrow = self.rowbuf[0:1, 0:T]
        self.dq3row = self.rowbuf[0:1, T:2*T]

        self.psA = self.stack.enter_context(nc.psum_tensor("psA", [128, T], F32))
        self.psB = self.stack.enter_context(nc.psum_tensor("psB", [128, T], F32))
        self.PS = [self.psA, self.psB]

        self.eT = Eng(self, "eT")
        self.eV = Eng(self, "eV")
        self.eA = Eng(self, "eA")
        self.eP = Eng(self, "eP")
        names = ["d_tabW", "d_tabR",
                 "d_cst", "d_xt", "d_pre0", "d_pre1", "d_wk", "d_wv",
                 "d_wq0", "d_wq1", "d_wo0", "d_wo1", "d_xres0", "d_xres1",
                 "d_x2preW0", "d_x2preW1", "d_x2preR0", "d_x2preR1",
                 "d_x2W0", "d_x2W1", "d_x2R0", "d_x2R1",
                 "d_gw0", "d_gw1", "d_vw0", "d_vw1", "d_owp0", "d_owp1", "d_owqW",
                 "d_hW0", "d_hW1", "d_hR0", "d_hR1",
                 "d_owqR0", "d_owqR1", "d_owqR2", "d_owqR3",
                 "d_o3W0", "d_o3W1", "d_o3R0", "d_o3R1",
                 "d_out", "d_cc", "d_ccR", "d_bnc", "d_tap"]
        self.DQ = {n: Eng(self, n, 16) for n in names}
        self.era1 = None
        self.t1 = {}
        self.t2 = {}

    def open_era1(self):
        self.era1 = contextlib.ExitStack()
        e = self.era1.enter_context
        nc = self.nc
        for n in ["gA", "gB", "gC", "gD"]:
            self.t1[n] = e(nc.sbuf_tensor(n, [128, T], F32))
        for n in ["hA", "hB", "hC", "hD", "hE"]:
            self.t1[n] = e(nc.sbuf_tensor(n, [128, HF], F32))
        self.t1["wkv16"] = e(nc.sbuf_tensor("wkv16", [128, 2 * D], FP16))

    def open_era2(self):
        self.era2 = contextlib.ExitStack()
        e = self.era2.enter_context
        nc = self.nc
        t2 = self.t2
        for i in range(2):
            t2[f"owq{i}"] = e(nc.sbuf_tensor(f"owq{i}", [128, 2048], FP16))
        t2["sHB"] = e(nc.sbuf_tensor("sHB", [128, TB], F32))
        t2["dq3B"] = e(nc.sbuf_tensor("dq3B", [128, TB], F32))
        for n in ["scr4", "hrd0", "hrd1", "x2r0", "x2r1", "o3t0", "o3t1",
                  "s2sum", "s2sq", "m2B", "r2B"]:
            t2[n] = e(nc.sbuf_tensor(n, [128, TB], F32))

    def T1(self, n):
        return self.t1[n]

    def T2(self, n):
        return self.t2[n]

    def ph_p0(self):
        b = self
        eT, eV, eA, eP, DQ = b.eT, b.eV, b.eA, b.eP, b.DQ

        def sync(E):
            consts = [(b.bin_t[:], b.b_inT), (b.bout_t[:], b.b_outT),
                      (b.gb_t[:], b.gate_bT), (b.vb_t[:], b.val_bT),
                      (b.ob_t[:], b.out_bT), (b.rampR, b.ramp),
                      (b.invf_t[:], b.invfF), (b.onesR_t[:], b.onesR),
                      (b.onesC_t[:], b.onesC), (b.eye_t[:], b.eye),
                      (b.perm_t[:], b.perm), (b.bsel_t[:], b.bsel)]
            for dst, src in consts:
                DQ["d_cst"].op(lambda e, d=dst, s=src: e.dma_start(out=d, in_=s))
            for k in range(48):
                pc, ih = k // 2, k % 2
                src = [b.gW8, b.vW8, b.oW8][pc // 8][pc % 8][ih*HF:(ih+1)*HF, :]
                par = k % 2
                buf = ["hA", "hB"][par]
                wts = [(eA, f"p0.abs.{k-2}")] if k >= 2 else []
                DQ[f"d_pre{par}"].op(
                    lambda e, s=src, bu=buf: e.dma_start(
                        out=b.T1(bu)[:].rearrange("p (dc h) -> p dc h", h=128),
                        in_=s.rearrange("(dc p) h -> p dc h", p=128)),
                    waits=wts, label=f"pre.{k}")

        def gp(E):
            DQ["d_xt"].op(lambda e: e.dma_start(
                out=b.arena[:].rearrange("p (dc t) -> p dc t", t=T),
                in_=b.XT.rearrange("(dc p) t -> p dc t", p=128)), label="xt")

        def pe(E):
            E.wait_val(DQ["d_cst"], 16 * 12)
            for tc in range(TCOL):
                eT.op(lambda e, tc=tc: e.matmul(
                    b.psA[:, tc*512:(tc+1)*512], b.onesR_t[:],
                    b.rampR[0:1, tc*512:(tc+1)*512], start=True, stop=True),
                    label=("p0.rampmm" if tc == 3 else None))

        def act(E):
            E.wait(eT, "p0.rampmm")
            eA.op(lambda e: e.activation(b.T1("gD")[:], b.psA[:], AF.Identity),
                  label="p0.rampB")
            for k in range(48):
                par = k % 2
                buf = ["hA", "hB"][par]
                wts = [(DQ[f"d_pre{par}"], f"pre.{k}")]
                if k >= 1:
                    wts.append((eV, f"p0.red.{k-1}"))
                eA.op(lambda e, bu=buf: e.activation(b.T1("hC")[:], b.T1(bu)[:],
                                                     AF.Abs),
                      waits=wts, label=f"p0.abs.{k}")

        def vec(E):
            eV.op(lambda e: e.memset(b.gacc[:], 0.0), label="p0.gacc0")
            eV.op(lambda e: e.memset(b.rndP[:], RND), label="p0.rndp")
            eV.op(lambda e: e.memset(b.rndN[:], -RND), label="p0.rndn")
            for k in range(48):
                m = k // 16
                wts = [(eA, f"p0.abs.{k}")]
                if k >= 1:
                    wts.append((eV, f"p0.acc.{k-1}"))
                eV.op(lambda e: e.tensor_reduce(b.t16a[:, 0:1], b.T1("hC")[:],
                                                AX.X, ALU.add),
                      waits=wts, label=f"p0.red.{k}")
                eV.op(lambda e, m=m: e.tensor_tensor(b.gacc[:, m:m+1],
                                                     b.gacc[:, m:m+1],
                                                     b.t16a[:, 0:1], ALU.add),
                      waits=[(eV, f"p0.red.{k}"), (eV, "p0.gacc0")],
                      label=f"p0.acc.{k}")

        return {"sync": sync, "gpsimd": gp, "tensor": pe, "scalar": act,
                "vector": vec}

    def _theta_chain(self, pref, c, extra0, sq_guard, ih):
        b = self
        eV, eA = b.eV, b.eA
        if True:
            s = hsl(ih)
            wts = [(eA, "p0.rampB")]
            if ih == 0:
                wts += list(extra0)
            else:
                wts += [(eA, f"{pref}.cos.{c}.0"),
                        (eV, f"{pref}.rope.{c}.0")]
                if pref == "p1":
                    wts.append((eV, f"p1.nsq.{c}.0"))
            eV.op(lambda e, s=s: e.tensor_scalar(
                b.T1("hA")[:], b.T1("gD")[:, s], b.invf_t[:, c:c+1], None,
                ALU.mult), waits=wts, label=f"{pref}.th0.{c}.{ih}")
            eV.op(lambda e: e.tensor_scalar(b.T1("hB")[:], b.T1("hA")[:],
                                            INV_2PI, RND, ALU.mult, ALU.add),
                  waits=[(eV, f"{pref}.th0.{c}.{ih}")], label=f"{pref}.r.{c}.{ih}")
            wts = [(eV, f"{pref}.r.{c}.{ih}")]
            if sq_guard is not None and ih == 0:
                wts.append(sq_guard)
            eV.op(lambda e: e.tensor_scalar(b.T1("hC")[:], b.T1("hB")[:],
                                            -RND, CW_C2, ALU.add, ALU.mult),
                  waits=wts, label=f"{pref}.nc2.{c}.{ih}")
            eV.op(lambda e: e.tensor_scalar(b.T1("hB")[:], b.T1("hB")[:],
                                            -RND, CW_C1, ALU.add, ALU.mult),
                  waits=[(eV, f"{pref}.nc2.{c}.{ih}")], label=f"{pref}.nc1.{c}.{ih}")
            eV.op(lambda e: e.tensor_tensor(b.T1("hA")[:], b.T1("hA")[:],
                                            b.T1("hB")[:], ALU.subtract),
                  waits=[(eV, f"{pref}.nc1.{c}.{ih}")], label=f"{pref}.tha.{c}.{ih}")
            eV.op(lambda e: e.tensor_tensor(b.T1("hA")[:], b.T1("hA")[:],
                                            b.T1("hC")[:], ALU.subtract),
                  waits=[(eV, f"{pref}.tha.{c}.{ih}")], label=f"{pref}.thr0.{c}.{ih}")
            eV.op(lambda e: e.tensor_scalar(b.T1("hA")[:], b.T1("hA")[:],
                                            PI, -PI, ALU.min, ALU.max),
                  waits=[(eV, f"{pref}.thr0.{c}.{ih}")], label=f"{pref}.thr.{c}.{ih}")
            eV.op(lambda e: e.tensor_scalar(b.T1("hA")[:], b.T1("hA")[:],
                                            HALF_PI, None, ALU.add),
                  waits=[(eA, f"{pref}.sin.{c}.{ih}")], label=f"{pref}.ps0.{c}.{ih}")
            eV.op(lambda e: e.tensor_scalar(b.T1("hB")[:], b.T1("hA")[:],
                                            PI, None, ALU.is_gt),
                  waits=[(eV, f"{pref}.ps0.{c}.{ih}")], label=f"{pref}.msk.{c}.{ih}")
            eV.op(lambda e: e.scalar_tensor_tensor(
                b.T1("hA")[:], b.T1("hB")[:], -TWO_PI, b.T1("hA")[:],
                ALU.mult, ALU.add),
                waits=[(eV, f"{pref}.msk.{c}.{ih}")], label=f"{pref}.psr0.{c}.{ih}")
            eV.op(lambda e: e.tensor_scalar(b.T1("hA")[:], b.T1("hA")[:],
                                            PI, -PI, ALU.min, ALU.max),
                waits=[(eV, f"{pref}.psr0.{c}.{ih}")], label=f"{pref}.psr.{c}.{ih}")

    def _tables_act(self, pref, c, ih):
        b = self
        eV, eA = b.eV, b.eA
        pc, pih = (c, 0) if ih == 1 else (c - 1, 1)
        wts = [(eV, f"{pref}.thr.{c}.{ih}")]
        if pref == "p1" and (c, ih) != (0, 0):
            wts.append((b.DQ["d_tabW"], f"tabWs.{pc}.{pih}"))
        eA.op(lambda e: e.activation(b.T1("hE")[:], b.T1("hA")[:], AF.Sin),
              waits=wts, label=f"{pref}.sin.{c}.{ih}")
        wts = [(eV, f"{pref}.psr.{c}.{ih}")]
        if pref == "p1" and (c, ih) != (0, 0):
            wts.append((b.DQ["d_tabW"], f"tabWc.{pc}.{pih}"))
        eA.op(lambda e: e.activation(b.T1("hD")[:], b.T1("hA")[:], AF.Sin),
              waits=wts, label=f"{pref}.cos.{c}.{ih}")

    def _rope_combine(self, pref, c, G, ih):
        b = self
        eV, eA, eT = b.eV, b.eA, b.eT
        if True:
            s = hsl(ih)
            wts = [(eT, f"{pref}.perm.{c}"), (eA, f"{pref}.cos.{c}.{ih}"),
                   (eV, f"{pref}.psr.{c}.{ih}")]
            if ih == 1:
                wts.append((eV, f"{pref}.rope.{c}.0"))
            eV.op(lambda e, s=s, G=G: e.tensor_tensor(
                b.T1("hB")[:], G[:, s], b.T1("hE")[:], ALU.mult),
                waits=wts, label=f"{pref}.krs.{c}.{ih}")
            kcw = [(eV, f"{pref}.krs.{c}.{ih}")]
            if pref == "p1" and c == 0 and ih == 0 and "k0" in b.taps:
                kcw.append((b.DQ["d_tap"], "tap.k0"))
            if pref == "p1" and c == 3 and ih == 0 and "ka3" in b.taps:
                kcw.append((b.DQ["d_tap"], "tap.ka3"))
            eV.op(lambda e, s=s: e.tensor_tensor(
                b.T1("gA")[:, s], b.T1("gA")[:, s], b.T1("hD")[:], ALU.mult),
                waits=kcw, label=f"{pref}.kcs.{c}.{ih}")
            eV.op(lambda e, s=s: e.tensor_tensor(
                b.T1("gA")[:, s], b.T1("gA")[:, s], b.T1("hB")[:], ALU.add),
                waits=[(eV, f"{pref}.kcs.{c}.{ih}")], label=f"{pref}.rope.{c}.{ih}")

    def ph_p1(self):
        b = self
        eT, eV, eA, eP, DQ, PS = b.eT, b.eV, b.eA, b.eP, b.DQ, b.PS

        def sync(E):
            chain = []
            def tp(name, ap, wts):
                DQ["d_tap"].op(lambda e, ap=ap: e.dma_start(
                    out=b.tap_outs[name], in_=ap),
                    waits=wts + ([(DQ["d_tap"], chain[-1])] if chain else []),
                    label=f"tap.{name}")
                chain.append(f"tap.{name}")
            if "arena" in b.taps:
                tp("arena", b.arena[:], [(DQ["d_xt"], "xt")])
            if "k0" in b.taps:
                tp("k0", b.T1("gA")[:], [(eA, "p1.kev.0.1")])
            if "v0" in b.taps:
                tp("v0", b.T1("gB")[:], [(eA, "p1.vev.0.1")])
            prev = [None]
            for c in range(8):
                for ih in range(2):
                    for sc, buf, lab in ((0, "hE", "tabWs"), (1, "hD", "tabWc")):
                        wts = [(eA, f"p1.{'sin' if sc == 0 else 'cos'}.{c}.{ih}")]
                        if prev[0] is not None:
                            wts.append((DQ["d_tabW"], prev[0]))
                        DQ["d_tabW"].op(lambda e, c=c, ih=ih, sc=sc, buf=buf:
                                        e.dma_start(
                            out=b.tab_sp[c, ih, sc],
                            in_=b.T1(buf)[:]),
                            waits=wts, label=f"{lab}.{c}.{ih}")
                        prev[0] = f"{lab}.{c}.{ih}"
            if "rope0" in b.taps:
                tp("rope0", b.T1("gA")[:], [(eV, "p1.rope.0.1")])
            if "sq00" in b.taps:
                tp("sq00", b.T1("hC")[:], [(eA, "p1.sq.0.0")])
            if "nsq0" in b.taps:
                tp("nsq0", b.T1("gC")[:], [(eV, "p1.nsq.0.1")])
            for n in range(16):
                if n == 3:
                    if "ka3" in b.taps:
                        tp("ka3", b.T1("gA")[:], [(eA, "p1.kev.3.1")])
                    if "kr3" in b.taps:
                        tp("kr3", b.T1("gA")[:], [(eV, "p1.rope.3.1")])
                if f"hc{n}" in b.taps:
                    tp(f"hc{n}", b.T1("hC")[:], [(eA, f"p1.sq.{n}.0")])
                if f"gc{n}" in b.taps:
                    tp(f"gc{n}", b.T1("gC")[:], [(eV, f"p1.nsq.{n}.1")])
            if "rampend" in b.taps:
                tp("rampend", b.T1("gD")[:], [(eV, "p1.nsq.15.1")])

        def gp(E):
            for c in range(16):
                wts = []
                if c > 0:
                    wts += [(eT, f"p1.kmm.{c-1}"), (eT, f"p1.vmm.{c-1}")]
                DQ["d_wk"].op(lambda e, c=c: e.dma_start(
                    out=b.T1("wkv16")[:, 0:D].rearrange("p (dc e) -> p dc e", e=128),
                    in_=b.WinT[16 + c].rearrange("(dc p) e -> p dc e", p=128)),
                    waits=wts, label=f"wk.{c}")
                DQ["d_wv"].op(lambda e, c=c: e.dma_start(
                    out=b.T1("wkv16")[:, D:2*D].rearrange("p (dc e) -> p dc e", e=128),
                    in_=b.WinT[32 + c].rearrange("(dc p) e -> p dc e", p=128)),
                    waits=([(eT, f"p1.kmm.{c-1}"), (eT, f"p1.vmm.{c-1}")]
                           if c > 0 else ()), label=f"wv.{c}")

        def pe(E):
            E.wait(DQ["d_xt"], "xt")
            for c in range(16):
                G1, G2 = PS[c % 2], PS[(c + 1) % 2]
                wts = [(DQ["d_wk"], f"wk.{c}")]
                if c == 0:
                    wts.append((eA, "p0.rampB"))
                if c >= 1:
                    wts.append((eA, f"p1.vev.{c-1}.1"))
                if c >= 2 and (c - 2) < 8:
                    wts.append((eV, f"p1.rope.{c-2}.1"))
                for dch in range(NCH):
                    for tc in range(TCOL):
                        last = dch == NCH - 1 and tc == TCOL - 1
                        eT.op(lambda e, G1=G1, dch=dch, tc=tc: e.matmul(
                            G1[:, tc*512:(tc+1)*512],
                            b.T1("wkv16")[:, dch*128:(dch+1)*128],
                            b.arena[:, dch*T + tc*512: dch*T + (tc+1)*512],
                            start=(dch == 0), stop=(dch == NCH - 1)),
                            waits=(wts if (dch == 0 and tc == 0) else ()),
                            label=(f"p1.kmm.{c}" if last else None))
                wts = [(DQ["d_wv"], f"wv.{c}")]
                if c >= 1:
                    wts.append((eA, f"p1.kev.{c-1}.1"))
                if c >= 1 and (c - 1) < 8:
                    wts.append((eV, f"p1.rope.{c-1}.1"))
                for dch in range(NCH):
                    for tc in range(TCOL):
                        last = dch == NCH - 1 and tc == TCOL - 1
                        eT.op(lambda e, G2=G2, dch=dch, tc=tc: e.matmul(
                            G2[:, tc*512:(tc+1)*512],
                            b.T1("wkv16")[:, D + dch*128:D + (dch+1)*128],
                            b.arena[:, dch*T + tc*512: dch*T + (tc+1)*512],
                            start=(dch == 0), stop=(dch == NCH - 1)),
                            waits=(wts if (dch == 0 and tc == 0) else ()),
                            label=(f"p1.vmm.{c}" if last else None))
                if c < 8:
                    for tc in range(TCOL):
                        eT.op(lambda e, G1=G1, tc=tc: e.matmul(
                            G1[:, tc*512:(tc+1)*512], b.perm_t[:],
                            b.T1("gA")[:, tc*512:(tc+1)*512], start=True,
                            stop=True),
                            waits=([(eA, f"p1.kev.{c}.1")] if tc == 0 else ()),
                            label=(f"p1.perm.{c}" if tc == TCOL - 1 else None))

        def act(E):
            def emit_sq(c, ih):
                wts = ([(eV, f"p1.rope.{c}.{ih}")] if c < 8
                       else [(eA, f"p1.kev.{c}.{ih}")])
                if ih == 1 and f"hc{c}" in b.taps:
                    wts.append((b.DQ["d_tap"], f"tap.hc{c}"))
                if c == 0 and ih == 0:
                    wts.append((eV, "p0.red.47"))
                elif ih == 0:
                    wts.append((eV, f"p1.nsq.{c-1}.1"))
                else:
                    wts.append((eV, f"p1.nsq.{c}.0"))
                eA.op(lambda e: e.activation(
                    b.T1("hC")[:], b.T1("gA")[:, hsl(ih)], AF.Square),
                    waits=wts, label=f"p1.sq.{c}.{ih}")

            for c in range(16):
                if c < 8:
                    self._tables_act("p1", c, 0)
                for ih in range(2):
                    wts = [(eT, f"p1.kmm.{c}")]
                    if c >= 1:
                        wts.append((eV, f"p1.kv.{c-1}.1"))
                    if c == 1 and "rope0" in b.taps:
                        wts.append((b.DQ["d_tap"], "tap.rope0"))
                    if c == 4 and "kr3" in b.taps:
                        wts.append((b.DQ["d_tap"], "tap.kr3"))
                    eA.op(lambda e, c=c, ih=ih: e.activation(
                        b.T1("gA")[:, hsl(ih)], PS[c % 2][:, hsl(ih)], AF.Identity,
                        bias=b.bin_t[:, 16+c:17+c], scale=1.0),
                        waits=wts, label=f"p1.kev.{c}.{ih}")
                for ih in range(2):
                    wts = [(eT, f"p1.vmm.{c}")]
                    if c >= 1:
                        wts.append((eV, f"p1.kv.{c-1}.1"))
                    if c == 1 and ih == 0 and "v0" in b.taps:
                        wts.append((b.DQ["d_tap"], "tap.v0"))
                    eA.op(lambda e, c=c, ih=ih: e.activation(
                        b.T1("gB")[:, hsl(ih)], PS[(c+1) % 2][:, hsl(ih)],
                        AF.Identity, bias=b.bin_t[:, 32+c:33+c], scale=1.0),
                        waits=wts, label=f"p1.vev.{c}.{ih}")
                emit_sq(c, 0)
                if c < 8:
                    self._tables_act("p1", c, 1)
                emit_sq(c, 1)

        def vec(E):
            def emit_nsq(c, ih):
                wts = [(eA, f"p1.sq.{c}.{ih}")]
                if c == 0 and ih == 0 and "sq00" in b.taps:
                    wts.append((b.DQ["d_tap"], "tap.sq00"))
                if c == 1 and ih == 0 and "nsq0" in b.taps:
                    wts.append((b.DQ["d_tap"], "tap.nsq0"))
                if ih == 0 and f"hc{c}" in b.taps:
                    wts.append((b.DQ["d_tap"], f"tap.hc{c}"))
                if c >= 1 and ih == 0 and f"gc{c-1}" in b.taps:
                    wts.append((b.DQ["d_tap"], f"tap.gc{c-1}"))
                if c == 0:
                    if ih == 1:
                        wts.append((eV, "p1.nsq.0.0"))
                    eV.op(lambda e, ih=ih: e.tensor_copy(
                        b.T1("gC")[:, hsl(ih)], b.T1("hC")[:]),
                        waits=wts, label=f"p1.nsq.0.{ih}")
                else:
                    wts.append((eV, f"p1.nsq.{c-1}.1" if ih == 0
                                else f"p1.nsq.{c}.0"))
                    eV.op(lambda e, ih=ih: e.tensor_tensor(
                        b.T1("gC")[:, hsl(ih)], b.T1("gC")[:, hsl(ih)],
                        b.T1("hC")[:], ALU.add),
                        waits=wts, label=f"p1.nsq.{c}.{ih}")

            def emit_kv(c, ih):
                wts = [(eA, f"p1.vev.{c}.{ih}"), (eA, f"p1.sq.{c}.{ih}")]
                eV.op(lambda e, c=c, ih=ih: e.tensor_tensor(
                    b.bigbuf[:, c*T + ih*HF: c*T + (ih+1)*HF],
                    b.T1("gA")[:, hsl(ih)], b.T1("gB")[:, hsl(ih)], ALU.mult),
                    waits=wts, label=f"p1.kv.{c}.{ih}")

            for c in range(16):
                if c < 8:
                    if c == 0:
                        extra0 = [(eV, "p0.red.47"), (eA, "p0.abs.47")]
                        sq_guard = None
                    else:
                        extra0 = [(eV, f"p1.rope.{c-1}.1"),
                                  (eA, f"p1.cos.{c-1}.1")]
                        sq_guard = (eA, f"p1.sq.{c-1}.1")
                    self._theta_chain("p1", c, extra0, sq_guard, 0)
                    self._rope_combine("p1", c, PS[c % 2], 0)
                    emit_nsq(c, 0)
                    emit_kv(c, 0)
                    self._theta_chain("p1", c, extra0, sq_guard, 1)
                    self._rope_combine("p1", c, PS[c % 2], 1)
                    emit_nsq(c, 1)
                    emit_kv(c, 1)
                else:
                    for ih in range(2):
                        emit_nsq(c, ih)
                        emit_kv(c, ih)

        return {"gpsimd": gp, "tensor": pe, "scalar": act, "vector": vec,
                "sync": sync}

    def ph_p1b(self):
        b = self
        eT, eV, eA, eP, DQ, PS = b.eT, b.eV, b.eA, b.eP, b.DQ, b.PS

        def pe(E):
            for tc in range(TCOL):
                eT.op(lambda e, tc=tc: e.matmul(
                    b.psA[0:1, tc*512:(tc+1)*512], b.onesC_t[:],
                    b.T1("gC")[:, tc*512:(tc+1)*512], start=True, stop=True),
                    waits=([(eV, "p1.nsq.15.1"), (eA, "p1.vev.15.1"),
                            (eV, "p1.kv.15.1")] if tc == 0 else ()),
                    label=("p1b.nsqmm" if tc == TCOL - 1 else None))
            eT.op(lambda e: e.matmul(b.psB[0:1, 0:4], b.onesC_t[:], b.gacc[:, 0:4],
                                     start=True, stop=True),
                  waits=[(eV, "p0.acc.47"), (eA, "p1.kev.15.1")], label="p1b.gmm")
            for tc in range(TCOL):
                eT.op(lambda e, tc=tc: e.matmul(
                    b.psB[:, tc*512:(tc+1)*512], b.onesR_t[:],
                    b.nrowR[0:1, tc*512:(tc+1)*512], start=True, stop=True),
                    waits=([(eV, "p1b.srow"), (eA, "p1b.gvac")] if tc == 0 else ()),
                    label=("p1b.sbmm" if tc == TCOL - 1 else None))

        def act(E):
            eA.op(lambda e: e.activation(b.nrowR, b.psA[0:1, :], AF.Sqrt),
                  waits=[(eT, "p1b.nsqmm")], label="p1b.nsqvac")
            eA.op(lambda e: e.activation(b.gcells[:], b.psB[0:1, 0:4], AF.Identity),
                  waits=[(eT, "p1b.gmm")], label="p1b.gvac")
            eA.op(lambda e: e.activation(b.T1("gA")[:], b.psB[:], AF.Identity),
                  waits=[(eT, "p1b.sbmm"), (eV, "p1.kv.15.1")], label="p1b.sB")

        def vec(E):
            eV.op(lambda e: e.tensor_scalar(b.nrowR, b.nrowR, 1e-12, None, ALU.max),
                  waits=[(eA, "p1b.nsqvac")], label="p1b.nmax")
            eV.op(lambda e: e.reciprocal(b.nrowR, b.nrowR),
                  waits=[(eV, "p1b.nmax")], label="p1b.srow")
            for c in range(16):
                for ih in range(2):
                    wts = [(eP, f"p1b.kvs.{c}.{ih}")]
                    if ih == 0 and c >= 1:
                        wts.append((eV, f"p1b.acol.{c-1}"))
                    eV.op(lambda e, ih=ih: e.tensor_reduce(
                        b.t16b[:, ih:ih+1], b.T1("hA")[:], AX.X, ALU.add),
                        waits=wts, label=f"p1b.red.{c}.{ih}")
                eV.op(lambda e, c=c: e.tensor_tensor(
                    b.acol[:, c:c+1], b.t16b[:, 0:1], b.t16b[:, 1:2], ALU.add),
                    waits=[(eV, f"p1b.red.{c}.1")], label=f"p1b.acol.{c}")
            eV.op(lambda e: e.tensor_scalar(b.acolB[:], b.acol[:],
                                            b.bsel_t[:, 1:2], None, ALU.mult),
                  waits=[(eV, "p1b.acol.15")], label="p1b.a1")
            eV.op(lambda e: e.tensor_scalar(b.acol2[:], b.acol[:],
                                            b.bsel_t[:, 0:1], None, ALU.mult),
                  waits=[(eV, "p1b.a1")], label="p1b.a0")

        def gp(E):
            for c in range(16):
                for ih in range(2):
                    wts = [(eA, "p1b.sB"), (eV, f"p1.kv.{c}.{ih}")]
                    if c == 0 and ih == 0:
                        wts += [(eV, "p1.rope.7.1"), (eA, "p1.cos.7.1")]
                    elif ih == 0:
                        wts.append((eV, f"p1b.red.{c-1}.1"))
                    else:
                        wts.append((eV, f"p1b.red.{c}.0"))
                    eP.op(lambda e, c=c, ih=ih: e.tensor_tensor(
                        b.T1("hA")[:], b.bigbuf[:, c*T + ih*HF: c*T + (ih+1)*HF],
                        b.T1("gA")[:, hsl(ih)], ALU.mult),
                        waits=wts, label=f"p1b.kvs.{c}.{ih}")

        def sync(E):
            DQ["d_cc"].op(lambda e: e.dma_start(
                out=b.ccin[0:1, :].rearrange("r (p f) -> (r p) f", p=128),
                in_=b.acol2[:]), waits=[(eV, "p1b.a0")], label="cc.a0")
            DQ["d_cc"].op(lambda e: e.dma_start(
                out=b.ccin[1:2, :].rearrange("r (p f) -> (r p) f", p=128),
                in_=b.acolB[:]), waits=[(eV, "p1b.a1")], label="cc.a1")
            DQ["d_cc"].op(lambda e: e.dma_start(out=b.ccin[2:3, 0:4],
                                                in_=b.gcells[:]),
                          waits=[(eA, "p1b.gvac")], label="cc.g")
            DQ["d_cc"].op(lambda e: e.dma_start(out=b.ccin[2:3, 4:T],
                                                in_=b.zrowD[0:1, 4:T]),
                          label="cc.gz")
            DQ["d_cc"].op(lambda e: e.dma_start(out=b.ccin[3:4, :], in_=b.zrowD),
                          label="cc.z")
            if "nsq" in b.taps:
                nw = [(eV, "p1.nsq.15.1")]
                for prev in ([f"gc{n}" for n in range(15, -1, -1)]
                             + [f"hc{n}" for n in range(15, -1, -1)]
                             + ["nsq0", "sq00", "rope0", "v0", "k0", "arena"]):
                    if prev in b.taps:
                        nw.append((DQ["d_tap"], f"tap.{prev}"))
                        break
                DQ["d_tap"].op(lambda e: e.dma_start(out=b.tap_outs["nsq"],
                                                     in_=b.T1("gC")[:]),
                               waits=nw, label="tap.nsq")
            if "srow" in b.taps:
                sw = [(eV, "p1b.srow")]
                if "nsq" in b.taps:
                    sw.append((DQ["d_tap"], "tap.nsq"))
                DQ["d_tap"].op(lambda e: e.dma_start(out=b.tap_outs["srow"],
                                                     in_=b.nrowR),
                               waits=sw, label="tap.srow")
            if "kv" in b.taps:
                wts = [(eV, "p1.kv.15.1")]
                if "srow" in b.taps:
                    wts.append((DQ["d_tap"], "tap.srow"))
                DQ["d_tap"].op(lambda e: e.dma_start(out=b.tap_outs["kv"],
                                                     in_=b.bigbuf[:]),
                               waits=wts, label="tap.kv")

        return {"tensor": pe, "scalar": act, "vector": vec, "gpsimd": gp,
                "sync": sync}

    def ph_pcc(self):
        b = self
        eT, eV, eA, eP, DQ = b.eT, b.eV, b.eA, b.eP, b.DQ

        def gp(E):
            E.wait_val(DQ["d_cc"], 16 * 5)
            eP.op(lambda e: e.collective_compute(
                "AllReduce", ALU.add, [list(range(8))], [b.ccin.opt()],
                [b.ccout.opt()]), label="cc.done")

        def sync(E):
            E.wait(eP, "cc.done")
            DQ["d_ccR"].op(lambda e: e.dma_start(
                out=b.acol2[:],
                in_=b.ccout[0:1, :].rearrange("r (p f) -> (r p) f", p=128)),
                waits=[(eV, "p1b.a0")], label="ccR.a0")
            DQ["d_ccR"].op(lambda e: e.dma_start(
                out=b.acolB[:],
                in_=b.ccout[1:2, :].rearrange("r (p f) -> (r p) f", p=128)),
                label="ccR.a1")
            DQ["d_ccR"].op(lambda e: e.dma_start(out=b.gcells[:],
                                                 in_=b.ccout[2:3, 0:4]),
                           waits=[(DQ["d_ccR"], "ccR.a1")], label="ccR.g")

        def vec(E):
            eV.op(lambda e: e.tensor_scalar(b.acol2[:], b.acol2[:],
                                            b.bsel_t[:, 0:1], None, ALU.mult),
                  waits=[(DQ["d_ccR"], "ccR.a1")],
                  label="cc.s0")
            eV.op(lambda e: e.tensor_scalar(b.acolB[:], b.acolB[:],
                                            b.bsel_t[:, 1:2], None, ALU.mult),
                  waits=[(eV, "cc.s0")], label="cc.s1")
            eV.op(lambda e: e.tensor_tensor(b.acol2[:], b.acol2[:], b.acolB[:],
                                            ALU.add),
                  waits=[(eV, "cc.s1")], label="cc.asel")
            eV.op(lambda e: e.tensor_scalar(b.gvals[0:1, 0:3], b.gcells[0:1, 0:3],
                                            1.0 / NW, 1e-5, ALU.mult, ALU.max),
                  waits=[(DQ["d_ccR"], "ccR.g")], label="cc.g3")
            eV.op(lambda e: e.reciprocal(b.gvals[0:1, 3:6], b.gvals[0:1, 0:3]),
                  waits=[(eV, "cc.g3")], label="cc.ginv")
            eV.op(lambda e: e.memset(b.gvals[0:1, 6:8], 0.0), label="cc.gpad")

        def pe(E):
            eT.op(lambda e: e.matmul(b.psA[:, 512:520], b.onesR_t[:],
                                     b.gvals[0:1, 0:8], start=True, stop=True),
                  waits=[(eV, "cc.ginv"), (eV, "cc.gpad")], label="cc.gbmm")

        def act(E):
            eA.op(lambda e: e.activation(b.acol2[:], b.acol2[:], AF.Identity),
                  waits=[(eV, "cc.asel")], label="cc.acol2")
            eA.op(lambda e: e.activation(b.gB8[:], b.psA[:, 512:520], AF.Identity),
                  waits=[(eT, "cc.gbmm")], label="cc.gB8")

        return {"gpsimd": gp, "sync": sync, "vector": vec, "tensor": pe,
                "scalar": act}

    def ph_p2(self):
        b = self
        eT, eV, eA, eP, DQ, PS = b.eT, b.eV, b.eA, b.eP, b.DQ, b.PS

        def gp(E):
            def emit_owqw(j2):
                j, ihp = j2 // 2, j2 % 2
                fc, j4 = j // 4, j % 4
                buf = ["gB", "gC"][j % 2]
                wts = [(eV, f"p2.owq.{j2}")]
                if j2 >= 1:
                    wts.append((DQ["d_owqW"], f"owqW.{j2-1}"))
                DQ["d_owqW"].op(lambda e, fc=fc, j4=j4, ihp=ihp, bu=buf:
                                e.dma_start(
                    out=b.owq_cache[fc][:, j4*2048 + ihp*HF:
                                        j4*2048 + (ihp+1)*HF],
                    in_=b.T1(bu)[:, hsl(ihp)]),
                    waits=wts, label=f"owqW.{j2}")

            for c in range(16):
                par = c % 2
                wts = [(eT, f"p2.qmm.{c-2}")] if c >= 2 else [
                    (eT, "p1.kmm.15"), (eT, "p1.vmm.15")]
                DQ[f"d_wq{par}"].op(lambda e, c=c, par=par: e.dma_start(
                    out=b.T1("wkv16")[:, par*D:(par+1)*D]
                    .rearrange("p (dc e) -> p dc e", e=128),
                    in_=b.WinT[c].rearrange("(dc p) e -> p dc e", p=128)),
                    waits=wts, label=f"wq.{c}")
                if c >= 1:
                    for j2 in range(8 * (c - 1), 8 * c):
                        emit_owqw(j2)
            for j2 in range(8 * 15, 128):
                emit_owqw(j2)

        def sync(E):
            prevR = [None]

            def emit_tabR(c, ih):
                for sc, buf, lab, rdlab in ((0, "hE", "tabRs", "krs"),
                                            (1, "hD", "tabRc", "kcs")):
                    wts = []
                    if prevR[0] is not None:
                        wts.append((DQ["d_tabR"], prevR[0]))
                    if c == 0 and ih == 0:
                        if sc == 0:
                            wts += [(DQ["d_tabW"], "tabWc.7.1"),
                                    (eV, "p1.rope.7.1")]
                    else:
                        pc, pih = (c, 0) if ih == 1 else (c - 1, 1)
                        wts.append((eV, f"p2.{rdlab}.{pc}.{pih}"))
                    DQ["d_tabR"].op(lambda e, c=c, ih=ih, sc=sc, buf=buf:
                                    e.dma_start(
                        out=b.T1(buf)[:], in_=b.tab_sp[c, ih, sc]),
                        waits=wts, label=f"{lab}.{c}.{ih}")
                    prevR[0] = f"{lab}.{c}.{ih}"

            for j in range(64):
                par = j % 2
                buf = ["gB", "gC"][par]
                if j % 4 == 0 and j // 4 < 8:
                    emit_tabR(j // 4, 0)
                    emit_tabR(j // 4, 1)
                if j == 0:
                    wts = [(eV, "p1.kv.15.1")]
                elif j == 1:
                    wts = [(eT, "p1b.nsqmm")]
                    if "nsq" in b.taps:
                        wts.append((DQ["d_tap"], "tap.nsq"))
                else:
                    wts = [(DQ["d_owqW"], f"owqW.{2*(j-2)+1}")]
                DQ[f"d_owp{par}"].op(lambda e, j=j, bu=buf: e.dma_start(
                    out=b.T1(bu)[:].rearrange("p (dc f) -> p dc f", f=128),
                    in_=b.OwT[j].rearrange("(dc p) f -> p dc f", p=128)),
                    waits=wts, label=f"owp.{j}")


        def pe(E):
            for c in range(16):
                par = c % 2
                G1 = PS[c % 2]
                wts = [(DQ[f"d_wq{par}"], f"wq.{c}"), (eA, "cc.acol2"),
                       (eA, "cc.gB8")]
                if c == 0:
                    wts += [(eP, "p1b.kvs.15.1"), (eA, "p1b.sB")]
                if c >= 2:
                    wts.append((eA, f"p2.qev.{c-2}.1"))
                if c >= 2 and (c - 2) < 8:
                    wts.append((eV, f"p2.rope.{c-2}.1"))
                for dch in range(NCH):
                    for tc in range(TCOL):
                        last = dch == NCH - 1 and tc == TCOL - 1
                        eT.op(lambda e, G1=G1, dch=dch, tc=tc, par=par: e.matmul(
                            G1[:, tc*512:(tc+1)*512],
                            b.T1("wkv16")[:, par*D + dch*128: par*D + (dch+1)*128],
                            b.arena[:, dch*T + tc*512: dch*T + (tc+1)*512],
                            start=(dch == 0), stop=(dch == NCH - 1)),
                            waits=(wts if (dch == 0 and tc == 0) else ()),
                            label=(f"p2.qmm.{c}" if last else None))
                if c < 8:
                    for tc in range(TCOL):
                        eT.op(lambda e, G1=G1, tc=tc: e.matmul(
                            G1[:, tc*512:(tc+1)*512], b.perm_t[:],
                            b.T1("gA")[:, tc*512:(tc+1)*512], start=True,
                            stop=True),
                            waits=([(eA, f"p2.qev.{c}.1")] if tc == 0 else ()),
                            label=(f"p2.perm.{c}" if tc == TCOL - 1 else None))

        def act(E):
            for c in range(16):
                for ih in range(2):
                    wts = [(eT, f"p2.qmm.{c}")]
                    if c >= 1:
                        wts.append((eV, f"p2.y.{c-1}.1"))
                    eA.op(lambda e, c=c, ih=ih: e.activation(
                        b.T1("gA")[:, hsl(ih)], PS[c % 2][:, hsl(ih)], AF.Identity,
                        bias=b.bin_t[:, c:c+1], scale=1.0),
                        waits=wts, label=f"p2.qev.{c}.{ih}")
                for jj in range(4):
                    j = 4 * c + jj
                    par = j % 2
                    buf = ["gB", "gC"][par]
                    for ihp in range(2):
                        j2 = 2 * j + ihp
                        sl2 = hsl(ihp)
                        wts = [(DQ[f"d_owp{par}"], f"owp.{j}"), (eA, "cc.gB8"),
                           (eV, "p0.rndp"), (eV, "p0.rndn")]
                        eA.op(lambda e, bu=buf, sl2=sl2: e.activation(
                            b.T1(bu)[:, sl2], b.T1(bu)[:, sl2], AF.Identity,
                            bias=b.rndP[:], scale=b.gB8[:, 5:6]),
                            waits=wts, label=f"p2.owr1.{j2}")

        def combine_ld(c, ih, G):
            s = hsl(ih)
            wts = [(eT, f"p2.perm.{c}"), (DQ["d_tabR"], f"tabRs.{c}.{ih}")]
            if ih == 1:
                wts.append((eV, f"p2.rope.{c}.0"))
            elif c >= 1:
                wts.append((eV, f"p2.rope.{c-1}.1"))
            else:
                wts += [(eV, "p1.rope.7.1"), (eP, "p1b.kvs.15.1")]
            eV.op(lambda e, s=s, G=G: e.tensor_tensor(
                b.T1("hB")[:], G[:, s], b.T1("hE")[:], ALU.mult),
                waits=wts, label=f"p2.krs.{c}.{ih}")
            eV.op(lambda e, s=s: e.tensor_tensor(
                b.T1("gA")[:, s], b.T1("gA")[:, s], b.T1("hD")[:], ALU.mult),
                waits=[(eV, f"p2.krs.{c}.{ih}"),
                       (DQ["d_tabR"], f"tabRc.{c}.{ih}")],
                label=f"p2.kcs.{c}.{ih}")
            eV.op(lambda e, s=s: e.tensor_tensor(
                b.T1("gA")[:, s], b.T1("gA")[:, s], b.T1("hB")[:], ALU.add),
                waits=[(eV, f"p2.kcs.{c}.{ih}")], label=f"p2.rope.{c}.{ih}")

        def vec(E):
            for c in range(16):
                if c < 8:
                    combine_ld(c, 0, PS[c % 2])
                    combine_ld(c, 1, PS[c % 2])
                for ih in range(2):
                    wts = ([(eV, f"p2.rope.{c}.{ih}")] if c < 8
                           else [(eA, f"p2.qev.{c}.{ih}")])
                    wts.append((eA, "cc.acol2"))
                    if c == 0 and ih == 0:
                        wts.append((eP, "p1b.kvs.15.1"))
                        if "kv" in b.taps:
                            wts.append((DQ["d_tap"], "tap.kv"))
                    eV.op(lambda e, c=c, ih=ih: e.tensor_scalar(
                        b.bigbuf[:, c*T + ih*HF: c*T + (ih+1)*HF],
                        b.T1("gA")[:, hsl(ih)], b.acol2[:, c:c+1], None, ALU.mult),
                        waits=wts, label=f"p2.y.{c}.{ih}")
                for jj in range(4):
                    j = 4 * c + jj
                    par = j % 2
                    buf = ["gB", "gC"][par]
                    for ihp in range(2):
                        j2 = 2 * j + ihp
                        sl2 = hsl(ihp)
                        eV.op(lambda e, bu=buf, sl2=sl2: e.tensor_scalar(
                            b.T1(bu)[:, sl2], b.T1(bu)[:, sl2], -RND, 1.0,
                            ALU.add, ALU.min),
                            waits=[(eA, f"p2.owr1.{j2}")],
                            label=f"p2.owr2.{j2}")
                        eV.op(lambda e, bu=buf, sl2=sl2: e.tensor_scalar(
                            b.T1(bu)[:, sl2], b.T1(bu)[:, sl2], -1.0, None,
                            ALU.max),
                            waits=[(eV, f"p2.owr2.{j2}")],
                            label=f"p2.owq.{j2}")

        return {"gpsimd": gp, "sync": sync, "tensor": pe,
                "scalar": act, "vector": vec}

    def ph_ao(self):
        b = self
        eT, eV, eA, eP, DQ, PS = b.eT, b.eV, b.eA, b.eP, b.DQ, b.PS

        def gp(E):
            for fc in range(16):
                par = fc % 2
                wts = ([(eT, f"ao.mm.{fc-2}")] if fc >= 2
                       else [(eT, f"p2.qmm.{14 + fc}")])
                DQ[f"d_wo{par}"].op(lambda e, fc=fc, par=par: e.dma_start(
                    out=b.T1("wkv16")[:, par*D:(par+1)*D]
                    .rearrange("p (dc e) -> p dc e", e=128),
                    in_=b.WoT[fc].rearrange("(dc p) e -> p dc e", p=128)),
                    waits=wts, label=f"wo.{fc}")

        def sync(E):
            for k in range(32):
                fc, ih = k // 2, k % 2
                par = k % 2
                buf = ["hC", "hD"][par]
                wts = [(eV, f"ao.x2pre.{k-2}")] if k >= 2 else [
                    (eV, "p2.y.15.1")]
                DQ[f"d_xres{par}"].op(lambda e, fc=fc, ih=ih, bu=buf: e.dma_start(
                    out=b.T1(bu)[:],
                    in_=b.XT[fc*128:(fc+1)*128, ih*HF:(ih+1)*HF]),
                    waits=wts, label=f"xres.{k}")

        def pe(E):
            for fc in range(16):
                par = fc % 2
                G1 = PS[fc % 2]
                wts = [(DQ[f"d_wo{par}"], f"wo.{fc}")]
                if fc == 0:
                    wts += [(eV, "p2.y.15.1"), (eV, "p2.rope.7.1"),
                            (eA, "p2.qev.15.1")]
                if fc == 1:
                    wts += [(eA, "p2.qev.15.1")]
                if fc >= 2:
                    wts.append((eV, f"ao.x2pre.{2*(fc-2)+1}"))
                for dch in range(NCH):
                    for tc in range(TCOL):
                        last = dch == NCH - 1 and tc == TCOL - 1
                        eT.op(lambda e, G1=G1, dch=dch, tc=tc, par=par: e.matmul(
                            G1[:, tc*512:(tc+1)*512],
                            b.T1("wkv16")[:, par*D + dch*128: par*D + (dch+1)*128],
                            b.bigbuf[:, dch*T + tc*512: dch*T + (tc+1)*512],
                            start=(dch == 0), stop=(dch == NCH - 1)),
                            waits=(wts if (dch == 0 and tc == 0) else ()),
                            label=(f"ao.mm.{fc}" if last else None))

        def act(E):
            for k in range(32):
                fc, ih = k // 2, k % 2
                par = k % 2
                wts = [(eV, f"ao.x2pre.{k}")]
                if k >= 1:
                    wts.append((eV, f"ao.ssq.{k-1}"))
                eA.op(lambda e, p=par: e.activation(b.T1("hE")[:],
                                                    b.T1(["hA", "hB"][p])[:],
                                                    AF.Square),
                      waits=wts, label=f"ao.sq.{k}")
                DQ[f"d_x2preW{par}"].op(lambda e, fc=fc, ih=ih, par=par: e.dma_start(
                    out=b.x2pre_sp[fc*128:(fc+1)*128, ih*HF:(ih+1)*HF],
                    in_=b.T1(["hA", "hB"][par])[:]),
                    waits=[(eV, f"ao.ssum.{k}"), (eV, f"ao.mx.{k}"),
                           (eV, f"ao.mn.{k}")],
                    label=f"x2preW.{k}")

        def vec(E):
            for k in range(32):
                fc, ih = k // 2, k % 2
                par = k % 2
                xp = ["hA", "hB"][par]
                xr = ["hC", "hD"][par]
                wts = [(eT, f"ao.mm.{fc}"), (DQ[f"d_xres{par}"], f"xres.{k}")]
                if k >= 2:
                    wts.append((DQ[f"d_x2preW{par}"], f"x2preW.{k-2}"))
                if k == 0:
                    wts += [(eV, "p2.y.15.1"), (eV, "p2.owq.127")]
                if k == 1:
                    wts.append((eA, "p1.cos.7.1"))
                eV.op(lambda e, xp=xp, xr=xr, fc=fc, ih=ih: e.scalar_tensor_tensor(
                    b.T1(xp)[:], PS[fc % 2][:, hsl(ih)], b.bout_t[:, fc:fc+1],
                    b.T1(xr)[:], ALU.add, ALU.add),
                    waits=wts, label=f"ao.x2pre0.{k}")
                eV.op(lambda e, xp=xp, xr=xr: e.tensor_tensor(
                    b.T1(xp)[:], b.T1(xp)[:], b.T1(xr)[:], ALU.add),
                    waits=[(eV, f"ao.x2pre0.{k}")], label=f"ao.x2pre.{k}")
                sgm = hsl(ih)
                if k <= 1:
                    xb = ["hA", "hB"][par]
                    pre = [(eV, f"ao.x2pre.{k}")]
                    prev = ([] if k == 0 else [(eV, f"ao.ssum.{k-1}")])
                    init_wts = {
                        "ssum": pre + prev + ([(eV, "p2.y.15.1")] if k == 0 else []),
                        "mx": pre + ([(eV, "p2.owq.127"),
                                      (DQ["d_owqW"], "owqW.127")] if k == 0
                                     else [(eV, "ao.mx.0")]),
                        "mn": pre + ([(eV, "p1.th0.7.1")] if k == 0
                                     else [(eV, "ao.mn.0")]),
                    }
                    eV.op(lambda e, xb=xb, sgm=sgm: e.tensor_copy(
                        b.T1("gA")[:, sgm], b.T1(xb)[:]),
                        waits=init_wts["ssum"], label=f"ao.ssum.{k}")
                    eV.op(lambda e, xb=xb, sgm=sgm: e.tensor_copy(
                        b.T1("gC")[:, sgm], b.T1(xb)[:]),
                        waits=init_wts["mx"], label=f"ao.mx.{k}")
                    eV.op(lambda e, xb=xb, sgm=sgm: e.tensor_copy(
                        b.T1("gD")[:, sgm], b.T1(xb)[:]),
                        waits=init_wts["mn"], label=f"ao.mn.{k}")
                    sq_wts = [(eA, f"ao.sq.{k}")]
                    if k == 0:
                        sq_wts += [(eV, "p2.owq.126"),
                                   (DQ["d_owqW"], "owqW.127")]
                    else:
                        sq_wts.append((eV, "ao.ssq.0"))
                    eV.op(lambda e, sgm=sgm: e.tensor_copy(
                        b.T1("gB")[:, sgm], b.T1("hE")[:]),
                        waits=sq_wts, label=f"ao.ssq.{k}")
                else:
                    eV.op(lambda e, xp=xp, sgm=sgm: e.tensor_tensor(
                        b.T1("gA")[:, sgm], b.T1("gA")[:, sgm], b.T1(xp)[:],
                        ALU.add),
                        waits=[(eV, f"ao.x2pre.{k}"), (eV, f"ao.ssum.{k-1}")],
                        label=f"ao.ssum.{k}")
                    eV.op(lambda e, xp=xp, sgm=sgm: e.tensor_tensor(
                        b.T1("gC")[:, sgm], b.T1("gC")[:, sgm], b.T1(xp)[:],
                        ALU.max),
                        waits=[(eV, f"ao.mx.{k-1}")], label=f"ao.mx.{k}")
                    eV.op(lambda e, xp=xp, sgm=sgm: e.tensor_tensor(
                        b.T1("gD")[:, sgm], b.T1("gD")[:, sgm], b.T1(xp)[:],
                        ALU.min),
                        waits=[(eV, f"ao.mn.{k-1}")], label=f"ao.mn.{k}")
                    eV.op(lambda e, sgm=sgm: e.tensor_tensor(
                        b.T1("gB")[:, sgm], b.T1("gB")[:, sgm], b.T1("hE")[:],
                        ALU.add),
                        waits=[(eA, f"ao.sq.{k}"), (eV, f"ao.ssq.{k-1}")],
                        label=f"ao.ssq.{k}")

        return {"gpsimd": gp, "sync": sync, "tensor": pe,
                "scalar": act, "vector": vec}

    def ph_ln1(self):
        b = self
        eT, eV, eA, eP, DQ, PS = b.eT, b.eV, b.eA, b.eP, b.DQ, b.PS
        srcs = ["gA", "gB", "gC", "gD"]
        red_ops = [ALU.add, ALU.add, ALU.max, ALU.min]
        lastdeps = ["ao.ssum.31", "ao.ssq.31", "ao.mx.31", "ao.mn.31"]

        def pe(E):
            for k in range(64):
                i, j = k // 16, k % 16
                wts = [(eV, lastdeps[i])]
                if k == 0:
                    wts += [(eV, "ao.x2pre.31"), (eA, "ao.sq.31")]
                if k >= 2:
                    wts.append((eV, f"ln1.red.{k-2}"))
                eT.op(lambda e, i=i, j=j, k=k: e.transpose(
                    PS[k % 2][:, 0:128], b.T1(srcs[i])[:, j*128:(j+1)*128],
                    b.eye_t[:]),
                    waits=wts, label=f"ln1.tr.{k}")
            packs = [("mT16", "ln1.m"), ("rT16", "ln1.rstd"), ("sT16", "ln1.sx"),
                     ("siT16", "ln1.six")]
            for i, (tn, dep) in enumerate(packs):
                wts = [(eV, dep)]
                if i < 2:
                    wts.append((eV, f"ln1.red.{62+i}"))
                else:
                    wts.append((eA, f"ln1.rev.{i-2}"))
                eT.op(lambda e, tn=tn, i=i: e.transpose(
                    PS[i % 2][0:16, 128:256], getattr(b, tn)[:], b.eye_t[:]),
                    waits=wts, label=f"ln1.rtr.{i}")
            labs = ["mB", "rB", "sxB", "siB"]
            for i, lab in enumerate(labs):
                row = b.ln1rows[i % 2]
                for tc in range(TCOL):
                    wts = []
                    if tc == 0:
                        wts = [(DQ["d_bnc"], f"bncR.{i}"), (eA, "ln1.rev.3"),
                               (eV, f"ln1.red.{62 + (i % 2)}")]
                        if i >= 2:
                            wts.append((eA, f"ln1.ev.{labs[i-2]}"))
                    eT.op(lambda e, row=row, i=i, tc=tc: e.matmul(
                        PS[i % 2][:, tc*512:(tc+1)*512], b.onesR_t[:],
                        row[0:1, tc*512:(tc+1)*512], start=True, stop=True),
                        waits=wts,
                        label=(f"ln1.bc.{lab}" if tc == TCOL - 1 else None))

        def vec(E):
            dsts = [b.sT16, b.t16a, b.t16b, b.hT16]
            for k in range(64):
                i, j = k // 16, k % 16
                wts = [(eT, f"ln1.tr.{k}")]
                if k == 0:
                    wts.append((eV, "ao.mn.31"))
                eV.op(lambda e, i=i, j=j, k=k: e.tensor_reduce(
                    dsts[i][:, j:j+1], PS[k % 2][0:128, 0:128], AX.X, red_ops[i]),
                    waits=wts, label=f"ln1.red.{k}")
            eV.op(lambda e: e.tensor_scalar(b.mT16[:], b.sT16[:], 1.0 / D, None,
                                            ALU.mult),
                  waits=[(eV, "ln1.red.15")], label="ln1.m")
            eV.op(lambda e: e.tensor_scalar(b.t16a[:], b.t16a[:], 1.0 / D, None,
                                            ALU.mult),
                  waits=[(eV, "ln1.red.31")], label="ln1.ex2")
            eV.op(lambda e: e.tensor_tensor(b.dT16[:], b.mT16[:], b.mT16[:],
                                            ALU.mult),
                  waits=[(eV, "ln1.m")], label="ln1.msq")
            eV.op(lambda e: e.tensor_tensor(b.t16a[:], b.t16a[:], b.dT16[:],
                                            ALU.subtract),
                  waits=[(eV, "ln1.msq"), (eV, "ln1.ex2")], label="ln1.var")
            eV.op(lambda e: e.tensor_scalar(b.t16a[:], b.t16a[:], LN_EPS, None,
                                            ALU.add),
                  waits=[(eV, "ln1.var")], label="ln1.vare")
            eV.op(lambda e: e.reciprocal(b.rT16[:], b.t16a[:]),
                  waits=[(eA, "ln1.sqrt")], label="ln1.rstd")
            eV.op(lambda e: e.tensor_tensor(b.t16b[:], b.t16b[:], b.mT16[:],
                                            ALU.subtract),
                  waits=[(eV, "ln1.red.47"), (eV, "ln1.m")], label="ln1.g1")
            eV.op(lambda e: e.tensor_tensor(b.hT16[:], b.mT16[:], b.hT16[:],
                                            ALU.subtract),
                  waits=[(eV, "ln1.red.63")], label="ln1.g2")
            eV.op(lambda e: e.tensor_tensor(b.t16b[:], b.t16b[:], b.hT16[:],
                                            ALU.max),
                  waits=[(eV, "ln1.g1"), (eV, "ln1.g2")], label="ln1.g3")
            eV.op(lambda e: e.tensor_tensor(b.t16b[:], b.t16b[:], b.rT16[:],
                                            ALU.mult),
                  waits=[(eV, "ln1.g3"), (eV, "ln1.rstd")], label="ln1.g4")
            eV.op(lambda e: e.tensor_scalar(b.t16b[:], b.t16b[:], 1e-5, None,
                                            ALU.max),
                  waits=[(eV, "ln1.g4")], label="ln1.gc")
            eV.op(lambda e: e.reciprocal(b.sT16[:], b.t16b[:]),
                  waits=[(eV, "ln1.gc")], label="ln1.sx0")
            eV.op(lambda e: e.tensor_scalar(b.sT16[:], b.sT16[:], 127.0, None,
                                            ALU.mult),
                  waits=[(eV, "ln1.sx0")], label="ln1.sx")
            eV.op(lambda e: e.tensor_scalar(b.siT16[:], b.t16b[:], 1.0 / 127.0,
                                            None, ALU.mult),
                  waits=[(eV, "ln1.gc")], label="ln1.six")

        def act(E):
            eA.op(lambda e: e.activation(b.t16a[:], b.t16a[:], AF.Sqrt),
                  waits=[(eV, "ln1.vare")], label="ln1.sqrt")
            for i in range(4):
                wts = [(eT, f"ln1.rtr.{i}")]
                if i >= 1:
                    wts.append((DQ["d_bnc"], f"bncW.{i-1}"))
                eA.op(lambda e, i=i: e.activation(b.rbounce[:],
                                                  PS[i % 2][0:16, 128:256],
                                                  AF.Identity),
                      waits=wts, label=f"ln1.rev.{i}")
            for i, lab in enumerate(["mB", "rB", "sxB", "siB"]):
                eA.op(lambda e, i=i: e.activation(b.T1(srcs[i])[:], PS[i % 2][:],
                                                  AF.Identity),
                      waits=[(eT, f"ln1.bc.{lab}")], label=f"ln1.ev.{lab}")

        def sync(E):
            for i in range(4):
                row = b.ln1rows[i % 2]
                wts = [(eA, f"ln1.rev.{i}")]
                if i >= 1:
                    wts.append((DQ["d_bnc"], f"bncR.{i-1}"))
                DQ["d_bnc"].op(lambda e, i=i: e.dma_start(
                    out=b.bounce[i:i+1, :].rearrange("r (p f) -> (r p) f", p=16),
                    in_=b.rbounce[:]), waits=wts, label=f"bncW.{i}")
                wts = [(DQ["d_bnc"], f"bncW.{i}")]
                if i == 0:
                    wts.append((eT, "p1b.sbmm"))
                    if "srow" in b.taps:
                        wts.append((DQ["d_tap"], "tap.srow"))
                if i >= 2:
                    wts.append((eT, f"ln1.bc.{['mB', 'rB'][i-2]}"))
                DQ["d_bnc"].op(lambda e, i=i, row=row: e.dma_start(
                    out=row, in_=b.bounce[i:i+1, :]),
                    waits=wts, label=f"bncR.{i}")

        return {"tensor": pe, "vector": vec, "scalar": act, "sync": sync}

    def ph_ln1b(self):
        b = self
        eT, eV, eA, eP, DQ, PS = b.eT, b.eV, b.eA, b.eP, b.DQ, b.PS

        def sync(E):
            for k in range(32):
                fc, ih = k // 2, k % 2
                par = k % 2
                buf = ["hA", "hB"][par]
                wts = [(DQ["d_x2preW0"], "x2preW.30"),
                       (DQ["d_x2preW1"], "x2preW.31")]
                if k >= 2:
                    wts.append((eV, f"ln1b.s.{k-2}"))
                else:
                    wts.append((eV, "ao.x2pre.31"))
                DQ[f"d_x2preR{par}"].op(lambda e, fc=fc, ih=ih, bu=buf: e.dma_start(
                    out=b.T1(bu)[:],
                    in_=b.x2pre_sp[fc*128:(fc+1)*128, ih*HF:(ih+1)*HF]),
                    waits=wts, label=f"x2preR.{k}")

        def scalarD(E):
            for k in range(32):
                fc, ih = k // 2, k % 2
                par = k % 2
                DQ[f"d_x2W{par}"].op(lambda e, fc=fc, ih=ih, par=par: e.dma_start(
                    out=b.x2_sp[fc*128:(fc+1)*128, ih*HF:(ih+1)*HF],
                    in_=b.T1(["hC", "hD"][par])[:]),
                    waits=[(eV, f"ln1b.q1.{k}")], label=f"x2W.{k}")

        def vec(E):
            for k in range(32):
                fc, ih = k // 2, k % 2
                par = k % 2
                xp = ["hA", "hB"][par]
                xq = ["hC", "hD"][par]
                sgm = hsl(ih)
                wts = [(DQ[f"d_x2preR{par}"], f"x2preR.{k}"), (eA, "ln1.ev.mB"),
                       (eA, "ln1.ev.rB")]
                if k >= 2:
                    wts.append((DQ[f"d_x2W{par}"], f"x2W.{k-2}"))
                else:
                    wts.append((eV, "ao.x2pre.31"))
                eV.op(lambda e, xp=xp, xq=xq, sgm=sgm: e.tensor_tensor(
                    b.T1(xq)[:], b.T1(xp)[:], b.T1("gA")[:, sgm], ALU.subtract),
                    waits=wts, label=f"ln1b.s.{k}")
                eV.op(lambda e, xq=xq, sgm=sgm: e.tensor_tensor(
                    b.T1(xq)[:], b.T1(xq)[:], b.T1("gB")[:, sgm], ALU.mult),
                    waits=[(eV, f"ln1b.s.{k}")], label=f"ln1b.x2.{k}")
                wts = [(eV, f"ln1b.x2.{k}"), (eA, "ln1.ev.sxB")]
                if k >= 1:
                    wts.append((eV, f"ln1b.dq.{k-1}"))
                if k == 0:
                    wts += [(eT, "ao.mm.15"), (eA, "ao.sq.31")]
                eV.op(lambda e, xq=xq, sgm=sgm: e.tensor_tensor(
                    b.T1("hE")[:], b.T1(xq)[:], b.T1("gC")[:, sgm], ALU.mult),
                    waits=wts, label=f"ln1b.q1.{k}")
                eV.op(lambda e: e.tensor_scalar(b.T1("hE")[:], b.T1("hE")[:],
                                                RND, -RND, ALU.add, ALU.add),
                      waits=[(eV, f"ln1b.q1.{k}")], label=f"ln1b.rnd.{k}")
                eV.op(lambda e: e.tensor_scalar(b.T1("hE")[:], b.T1("hE")[:],
                                                127.0, -128.0, ALU.min, ALU.max),
                      waits=[(eV, f"ln1b.rnd.{k}")], label=f"ln1b.clip.{k}")
                eV.op(lambda e, fc=fc, ih=ih, sgm=sgm: e.tensor_tensor(
                    b.bigbuf[:, fc*T + ih*HF: fc*T + (ih+1)*HF], b.T1("hE")[:],
                    b.T1("gD")[:, sgm], ALU.mult),
                    waits=[(eV, f"ln1b.clip.{k}"), (eA, "ln1.ev.siB")],
                    label=f"ln1b.dq.{k}")

        return {"sync": sync, "scalarD": scalarD, "vector": vec}

    def ph_f1(self):
        b = self
        eT, eV, eA, eP, DQ, PS = b.eT, b.eV, b.eA, b.eP, b.DQ, b.PS

        def sync(E):
            for hc in range(HCH):
                for ih in range(2):
                    k = 2 * hc + ih
                    par = k % 2
                    gbuf = ["hA", "hB"][par]
                    vbuf = ["hC", "hD"][par]
                    wts = [(eV, f"f1.wqg.{k-2}")] if k >= 2 else [
                        (DQ[f"d_x2preR{par}"], f"x2preR.{30+par}"),
                        (eV, "ln1b.s.31")]
                    DQ[f"d_gw{par}"].op(lambda e, hc=hc, ih=ih, bu=gbuf: e.dma_start(
                        out=b.T1(bu)[:].rearrange("p (dc h) -> p dc h", h=128),
                        in_=b.GwT[hc][ih*HF:(ih+1)*HF, :]
                        .rearrange("(dc p) h -> p dc h", p=128)),
                        waits=wts, label=f"gw.{k}")
                    wts = [(eV, f"f1.wqv.{k-2}")] if k >= 2 else [
                        (DQ[f"d_x2W{par}"], f"x2W.{30+par}"),
                        (eV, "ln1b.x2.31")]
                    DQ[f"d_vw{par}"].op(lambda e, hc=hc, ih=ih, bu=vbuf: e.dma_start(
                        out=b.T1(bu)[:].rearrange("p (dc h) -> p dc h", h=128),
                        in_=b.VwT[hc][ih*HF:(ih+1)*HF, :]
                        .rearrange("(dc p) h -> p dc h", p=128)),
                        waits=wts, label=f"vw.{k}")

        def act(E):
            for hc in range(HCH):
                for ih in range(2):
                    k = 2 * hc + ih
                    par = k % 2
                    gbuf = ["hA", "hB"][par]
                    vbuf = ["hC", "hD"][par]
                    wts = [(DQ[f"d_gw{par}"], f"gw.{k}"), (eA, "cc.gB8"),
                           (eV, "p0.rndp"), (eV, "p0.rndn")]
                    if k >= 2:
                        wts.append((eV, f"f1.wqg.{k-2}"))
                    eA.op(lambda e, bu=gbuf: e.activation(
                        b.T1(bu)[:], b.T1(bu)[:], AF.Identity, bias=b.rndP[:],
                        scale=b.gB8[:, 3:4]),
                        waits=wts, label=f"f1.gr1.{k}")
                    eA.op(lambda e, bu=gbuf: e.activation(
                        b.T1(bu)[:], b.T1(bu)[:], AF.Identity, bias=b.rndN[:],
                        scale=1.0),
                        waits=[(eA, f"f1.gr1.{k}")], label=f"f1.gr2.{k}")
                    wts = [(DQ[f"d_vw{par}"], f"vw.{k}")]
                    if k >= 2:
                        wts.append((eV, f"f1.wqv.{k-2}"))
                    eA.op(lambda e, bu=vbuf: e.activation(
                        b.T1(bu)[:], b.T1(bu)[:], AF.Identity, bias=b.rndP[:],
                        scale=b.gB8[:, 4:5]),
                        waits=wts, label=f"f1.vr1.{k}")
                    eA.op(lambda e, bu=vbuf: e.activation(
                        b.T1(bu)[:], b.T1(bu)[:], AF.Identity, bias=b.rndN[:],
                        scale=1.0),
                        waits=[(eA, f"f1.vr1.{k}")], label=f"f1.vr2.{k}")
                for ih in range(2):
                    k = 2 * hc + ih
                    if k >= 2:
                        kp = k - 2
                        DQ[f"d_hW{kp % 2}"].op(lambda e, kp=kp: e.dma_start(
                            out=b.h_sp[kp // 2][:, (kp % 2)*HF:(kp % 2 + 1)*HF],
                            in_=b.T1("gC")[:, (kp % 2)*HF:(kp % 2 + 1)*HF]),
                            waits=[(eV, f"f1.amax.{kp}")], label=f"hW.{kp}")
                    wts = [(eT, f"f1.gmm.{hc}")]
                    if k >= 2:
                        wts += [(eV, f"f1.amax.{k-2}"),
                                (DQ[f"d_hW{ih}"], f"hW.{k-2}")]
                    else:
                        wts.append((eV, "ln1b.dq.31"))
                    eA.op(lambda e, hc=hc, ih=ih: e.activation(
                        b.T1("gC")[:, hsl(ih)], PS[hc % 2][:, hsl(ih)], AF.Silu,
                        bias=b.gb_t[:, hc:hc+1], scale=b.gB8[:, 0:1]),
                        waits=wts, label=f"f1.gev.{k}")
                    wts = [(eT, f"f1.vmm.{hc}")]
                    if k >= 1:
                        wts.append((eV, f"f1.h.{k-1}"))
                    else:
                        wts.append((eV, "ln1b.dq.31"))
                    eA.op(lambda e, hc=hc, ih=ih: e.activation(
                        b.T1("hE")[:], PS[(hc+1) % 2][:, hsl(ih)], AF.Identity,
                        bias=b.vb_t[:, hc:hc+1], scale=b.gB8[:, 1:2]),
                        waits=wts, label=f"f1.vev.{k}")
                    if k == 127:
                        for kp in (126, 127):
                            DQ[f"d_hW{kp % 2}"].op(lambda e, kp=kp: e.dma_start(
                                out=b.h_sp[kp // 2][:, (kp % 2)*HF:
                                                    (kp % 2 + 1)*HF],
                                in_=b.T1("gC")[:, (kp % 2)*HF:(kp % 2 + 1)*HF]),
                                waits=[(eV, f"f1.amax.{kp}")], label=f"hW.{kp}")

        def vec(E):
            for hc in range(HCH):
                for ih in range(2):
                    k = 2 * hc + ih
                    par = k % 2
                    wts = [(eA, f"f1.gr2.{k}")]
                    if k <= 1:
                        wts.append((eT, "p2.qmm.15"))
                    if k >= 4:
                        wts.append((eT, f"f1.gmm.{hc-2}"))
                    eV.op(lambda e, hc=hc, ih=ih, p=par: e.tensor_scalar(
                        b.arena[:, (hc % 2)*D + ih*HF: (hc % 2)*D + (ih+1)*HF],
                        b.T1(["hA", "hB"][p])[:], 1.0, -1.0, ALU.min, ALU.max),
                        waits=wts, label=f"f1.wqg.{k}")
                    wts = [(eA, f"f1.vr2.{k}")]
                    if k <= 1:
                        wts.append((eT, "p2.qmm.15"))
                    if k >= 4:
                        wts.append((eT, f"f1.vmm.{hc-2}"))
                    eV.op(lambda e, hc=hc, ih=ih, p=par: e.tensor_scalar(
                        b.arena[:, (2 + hc % 2)*D + ih*HF:
                                (2 + hc % 2)*D + (ih+1)*HF],
                        b.T1(["hC", "hD"][p])[:], 1.0, -1.0, ALU.min, ALU.max),
                        waits=wts, label=f"f1.wqv.{k}")
                for ih in range(2):
                    k = 2 * hc + ih
                    wts = [(eA, f"f1.gev.{k}"), (eA, f"f1.vev.{k}")]
                    eV.op(lambda e, ih=ih: e.tensor_tensor(
                        b.T1("gC")[:, hsl(ih)], b.T1("gC")[:, hsl(ih)],
                        b.T1("hE")[:], ALU.mult),
                        waits=wts, label=f"f1.h.{k}")
                    if k == 0:
                        eV.op(lambda e: e.scalar_tensor_tensor(
                            b.T1("gD")[:, 0:HF], b.T1("gC")[:, 0:HF], -1.0,
                            b.T1("gC")[:, 0:HF], ALU.mult, ALU.max),
                            waits=[(eV, "f1.h.0"), (eV, "ln1b.dq.31")],
                            label="f1.amax.0")
                    elif k == 1:
                        eV.op(lambda e: e.scalar_tensor_tensor(
                            b.T1("gD")[:, HF:T], b.T1("gC")[:, HF:T], -1.0,
                            b.T1("gC")[:, HF:T], ALU.mult, ALU.max),
                            waits=[(eV, "f1.h.1"), (eV, "f1.amax.0")],
                            label="f1.amax.1")
                    else:
                        eV.op(lambda e, ih=ih: e.scalar_tensor_tensor(
                            b.T1("gD")[:, hsl(ih)], b.T1("gC")[:, hsl(ih)], -1.0,
                            b.T1("gD")[:, hsl(ih)], ALU.mult, ALU.max),
                            waits=[(eV, f"f1.h.{k}"), (eV, f"f1.amax.{k-1}")],
                            label=f"f1.amaxa.{k}")
                        eV.op(lambda e, ih=ih: e.tensor_tensor(
                            b.T1("gD")[:, hsl(ih)], b.T1("gD")[:, hsl(ih)],
                            b.T1("gC")[:, hsl(ih)], ALU.max),
                            waits=[(eV, f"f1.amaxa.{k}")],
                            label=f"f1.amax.{k}")

        def pe(E):
            for hc in range(HCH):
                par = hc % 2
                G1, G2 = PS[hc % 2], PS[(hc + 1) % 2]
                wts = [(eV, f"f1.wqg.{2*hc+1}")]
                if hc >= 1:
                    wts.append((eA, f"f1.vev.{2*hc-1}"))
                else:
                    wts += [(eA, "ln1.ev.sxB"), (eA, "ln1.ev.siB")]
                for dch in range(NCH):
                    for tc in range(TCOL):
                        last = dch == NCH - 1 and tc == TCOL - 1
                        w2 = list(wts) if (dch == 0 and tc == 0) else []
                        if tc == 0:
                            w2.append((eV, f"ln1b.dq.{2*dch}"))
                        elif tc == 2:
                            w2.append((eV, f"ln1b.dq.{2*dch+1}"))
                        eT.op(lambda e, G1=G1, dch=dch, tc=tc, p=par: e.matmul(
                            G1[:, tc*512:(tc+1)*512],
                            b.arena[:, p*D + dch*128: p*D + (dch+1)*128],
                            b.bigbuf[:, dch*T + tc*512: dch*T + (tc+1)*512],
                            start=(dch == 0), stop=(dch == NCH - 1)),
                            waits=w2,
                            label=(f"f1.gmm.{hc}" if last else None))
                wts = [(eV, f"f1.wqv.{2*hc+1}")]
                if hc >= 1:
                    wts.append((eA, f"f1.gev.{2*hc}"))
                else:
                    wts += [(eA, "ln1.ev.mB"), (eA, "ln1.ev.rB")]
                for dch in range(NCH):
                    for tc in range(TCOL):
                        last = dch == NCH - 1 and tc == TCOL - 1
                        eT.op(lambda e, G2=G2, dch=dch, tc=tc, p=par: e.matmul(
                            G2[:, tc*512:(tc+1)*512],
                            b.arena[:, (2+p)*D + dch*128: (2+p)*D + (dch+1)*128],
                            b.bigbuf[:, dch*T + tc*512: dch*T + (tc+1)*512],
                            start=(dch == 0), stop=(dch == NCH - 1)),
                            waits=(wts if (dch == 0 and tc == 0) else ()),
                            label=(f"f1.vmm.{hc}" if last else None))

        return {"sync": sync, "scalar": act, "vector": vec,
                "tensor": pe}

    def ph_f1b(self):
        b = self
        eT, eV, eA, eP, DQ, PS = b.eT, b.eV, b.eA, b.eP, b.DQ, b.PS

        def pe(E):
            for j in range(16):
                wts = [(eV, "f1.amax.127")]
                if j == 0:
                    wts += [(eA, "f1.gev.127"), (eA, "f1.vev.127")]
                if j >= 2:
                    wts.append((eV, f"f1b.red.{j-2}"))
                eT.op(lambda e, j=j: e.transpose(
                    PS[j % 2][:, 0:128], b.T1("gD")[:, j*128:(j+1)*128],
                    b.eye_t[:]),
                    waits=wts, label=f"f1b.tr.{j}")
            eT.op(lambda e: e.transpose(PS[0][0:16, 128:256], b.hT16[:],
                                        b.eye_t[:]),
                  waits=[(eV, "f1b.sh"), (eV, "f1b.red.14")], label="f1b.rtr.0")
            eT.op(lambda e: e.transpose(PS[1][0:16, 128:256], b.dT16[:],
                                        b.eye_t[:]),
                  waits=[(eV, "f1b.dq3"), (eV, "f1b.red.15")], label="f1b.rtr.1")

        def vec(E):
            for j in range(16):
                wts = [(eT, f"f1b.tr.{j}")]
                if j == 0:
                    wts.append((eV, "f1.amax.127"))
                eV.op(lambda e, j=j: e.tensor_reduce(
                    b.t16a[:, j:j+1], PS[j % 2][0:128, 0:128], AX.X, ALU.max),
                    waits=wts, label=f"f1b.red.{j}")
            eV.op(lambda e: e.tensor_scalar(b.t16a[:], b.t16a[:], 1e-5, None,
                                            ALU.max),
                  waits=[(eV, "f1b.red.15")], label="f1b.gc")
            eV.op(lambda e: e.reciprocal(b.hT16[:], b.t16a[:]),
                  waits=[(eV, "f1b.gc")], label="f1b.ginv")
            eV.op(lambda e: e.tensor_scalar(b.hT16[:], b.hT16[:], 127.0, None,
                                            ALU.mult),
                  waits=[(eV, "f1b.ginv")], label="f1b.sh")
            eV.op(lambda e: e.tensor_scalar(b.dT16[:], b.t16a[:], b.gB8[:, 2:3],
                                            1.0 / 127.0, ALU.mult, ALU.mult),
                  waits=[(eV, "f1b.gc"), (eA, "cc.gB8")], label="f1b.dq3")

        def act(E):
            eA.op(lambda e: e.activation(b.rbounce[:], PS[0][0:16, 128:256],
                                         AF.Identity),
                  waits=[(eT, "f1b.rtr.0"), (DQ["d_bnc"], "bncR.3")],
                  label="f1b.rev.0")
            eA.op(lambda e: e.activation(b.rbounce[:], PS[1][0:16, 128:256],
                                         AF.Identity),
                  waits=[(eT, "f1b.rtr.1"), (DQ["d_bnc"], "bncW.4")],
                  label="f1b.rev.1")

        def sync(E):
            DQ["d_bnc"].op(lambda e: e.dma_start(
                out=b.bounce[4:5, :].rearrange("r (p f) -> (r p) f", p=16),
                in_=b.rbounce[:]),
                waits=[(eA, "f1b.rev.0"), (DQ["d_bnc"], "bncR.3")],
                label="bncW.4")
            DQ["d_bnc"].op(lambda e: e.dma_start(out=b.sHrow,
                                                 in_=b.bounce[4:5, :]),
                           waits=[(DQ["d_bnc"], "bncW.4"),
                                  (eT, "ln1.bc.sxB")], label="bncR.4")
            DQ["d_bnc"].op(lambda e: e.dma_start(
                out=b.bounce[5:6, :].rearrange("r (p f) -> (r p) f", p=16),
                in_=b.rbounce[:]),
                waits=[(eA, "f1b.rev.1"), (DQ["d_bnc"], "bncR.4")],
                label="bncW.5")
            DQ["d_bnc"].op(lambda e: e.dma_start(out=b.dq3row,
                                                 in_=b.bounce[5:6, :]),
                           waits=[(DQ["d_bnc"], "bncW.5"),
                                  (eT, "ln1.bc.siB")], label="bncR.5")

        return {"tensor": pe, "vector": vec, "scalar": act, "sync": sync}

    def ph_f2(self, tb_sel):
        b = self
        eT, eV, eA, eP, DQ, PS = b.eT, b.eV, b.eA, b.eP, b.DQ, b.PS

        def prev_kf(tb, fc):
            return (tb, fc - 2) if fc >= 2 else (tb - 1, 14 + fc)

        def pe(E):
            for tb in [tb_sel]:
                for t2 in range(2):
                    wts = []
                    if t2 == 0:
                        wts = [(DQ["d_bnc"], "bncR.4"), (eV, "f1b.red.14"),
                               (eA, "f1b.rev.0")]
                        if tb == 0:
                            wts += [(eT, "f1.gmm.63"), (eA, "f1.gev.127")]
                        else:
                            wts += [(eA, "f2.sHB.0"), (eV, "f2.hq1.63"),
                                    (eA, "f2.embc.0"), (eA, "f2.erbc.0")]
                    eT.op(lambda e, tb=tb, t2=t2: e.matmul(
                        b.psA[:, t2*512:(t2+1)*512], b.onesR_t[:],
                        b.sHrow[0:1, tb*TB + t2*512: tb*TB + (t2+1)*512],
                        start=True, stop=True),
                        waits=wts, label=(f"f2.shmm.{tb}" if t2 == 1 else None))
                for t2 in range(2):
                    wts = []
                    if t2 == 0:
                        wts = [(DQ["d_bnc"], "bncR.5"), (eV, "f1b.red.15"),
                               (eA, "f1b.rev.1")]
                        if tb == 0:
                            wts += [(eT, "f1.vmm.63"), (eA, "f1.vev.127")]
                        else:
                            wts += [(eA, "f2.dq3B.0"), (eV, "f2.o1.0.15")]
                    eT.op(lambda e, tb=tb, t2=t2: e.matmul(
                        b.psB[:, t2*512:(t2+1)*512], b.onesR_t[:],
                        b.dq3row[0:1, tb*TB + t2*512: tb*TB + (t2+1)*512],
                        start=True, stop=True),
                        waits=wts, label=(f"f2.dqmm.{tb}" if t2 == 1 else None))
                for fc in range(16):
                    par = fc % 2
                    G = PS[fc % 2]
                    k = tb * 16 + fc
                    wts = []
                    if fc == 0:
                        wts.append((eA, f"f2.sHB.{tb}"))
                    if fc == 1:
                        wts.append((eA, f"f2.dq3B.{tb}"))
                    if k >= 2:
                        pt, pf = prev_kf(tb, fc)
                        wts.append((eV, f"f2.o1.{pt}.{pf}"))
                    if tb == 1 and fc <= 1:
                        wts.append((eA, "f2.erbc.0"))
                    for hc in range(HCH):
                        src = b.bigbuf if hc < 32 else b.arena
                        off = (hc % 32) * TB
                        q = hc // 16
                        qb = b.T2(f"owq{(4 * fc + q) % 2}")
                        for t2 in range(2):
                            last = hc == HCH - 1 and t2 == 1
                            qlast = (hc % 16 == 15) and t2 == 1
                            w2 = list(wts) if (hc == 0 and t2 == 0) else []
                            if t2 == 0:
                                w2.append((eV, f"f2.hint.{tb}.{hc}"))
                            if hc % 16 == 0 and t2 == 0:
                                w2.append((DQ[f"d_owqR{(4*fc+q) % 2}"],
                                           f"owqR.{tb}.{fc}.{q}"))
                            lab = None
                            if last:
                                lab = f"f2.mm.{tb}.{fc}"
                            elif qlast:
                                lab = f"f2.mmq.{tb}.{fc}.{q}"
                            eT.op(lambda e, G=G, hc=hc, t2=t2, src=src, off=off,
                                  qb=qb: e.matmul(
                                      G[:, t2*512:(t2+1)*512],
                                      qb[:, (hc % 16)*128:(hc % 16 + 1)*128],
                                      src[:, off + t2*512: off + (t2+1)*512],
                                      start=(hc == 0), stop=(hc == HCH - 1)),
                                  waits=w2, label=lab)
                for i in range(2):
                    src = [b.T2("s2sum"), b.T2("s2sq")][i]
                    dep = [f"f2.ssum.{tb}.15", f"f2.ssq.{tb}.15"][i]
                    for t2 in range(2):
                        wts = []
                        if t2 == 0:
                            wts = [(eV, dep), (eV, f"f2.o1.{tb}.15")]
                            if tb == 0 and i == 0:
                                wts.append((eA, "f2.sHB.0"))
                            if tb == 1 and i == 0:
                                wts.append((eA, "f2.rsum.0"))
                            if tb == 1 and i == 1:
                                wts.append((eA, "f2.rsq.0"))
                        dst = (b.psA[0:1, 1024 + t2*512: 1024 + (t2+1)*512]
                               if i == 0 else
                               b.psB[0:1, 1024 + t2*512: 1024 + (t2+1)*512])
                        eT.op(lambda e, dst=dst, src=src, t2=t2: e.matmul(
                            dst, b.onesC_t[:], src[:, t2*512:(t2+1)*512],
                            start=True, stop=True),
                            waits=wts,
                            label=(f"f2.rmm{i}.{tb}" if t2 == 1 else None))
                for i, (row, dep) in enumerate([(b.T2("m2B")[0:1, 0:TB], f"f2.var.{tb}"),
                                                (b.T2("r2B")[0:1, 0:TB], f"f2.rstd.{tb}")]):
                    for t2 in range(2):
                        wts = []
                        if t2 == 0:
                            wts = [(eV, dep), (eV, f"f2.o1.{tb}.15"),
                                   (eV, f"f2.o1.{tb}.14")]
                            if i == 1:
                                wts.append((eA, f"f2.embc.{tb}"))
                        eT.op(lambda e, row=row, t2=t2: e.matmul(
                            b.psA[:, t2*512:(t2+1)*512], b.onesR_t[:],
                            row[0:1, t2*512:(t2+1)*512], start=True, stop=True),
                            waits=wts,
                            label=(f"f2.bc{i}.{tb}" if t2 == 1 else None))

        def act(E):
            for tb in [tb_sel]:
                eA.op(lambda e: e.activation(b.T2("sHB")[:], b.psA[:, 0:TB],
                                             AF.Identity),
                      waits=[(eT, f"f2.shmm.{tb}")], label=f"f2.sHB.{tb}")
                eA.op(lambda e: e.activation(b.T2("dq3B")[:], b.psB[:, 0:TB],
                                             AF.Identity),
                      waits=[(eT, f"f2.dqmm.{tb}")], label=f"f2.dq3B.{tb}")
                for fc in range(16):
                    par = fc % 2
                    wts = [(eV, f"f2.res.{tb}.{fc}")]
                    if fc >= 1:
                        wts.append((eV, f"f2.ssq.{tb}.{fc-1}"))
                    elif tb == 1:
                        wts.append((eV, "f2.ssq.0.15"))
                    eA.op(lambda e, p=par: e.activation(
                        b.T2("scr4")[:], b.T2(f"o3t{p}")[:], AF.Square),
                        waits=wts, label=f"f2.sq.{tb}.{fc}")
                    DQ[f"d_o3W{par}"].op(lambda e, fc=fc, tb=tb, p=par:
                                         e.dma_start(
                        out=b.o3_sp[fc*128:(fc+1)*128, tb*TB:(tb+1)*TB],
                        in_=b.T2(f"o3t{p}")[:]),
                        waits=[(eV, f"f2.res.{tb}.{fc}")],
                        label=f"o3W.{tb}.{fc}")
                eA.op(lambda e: e.activation(b.T2("m2B")[0:1, 0:TB], b.psA[0:1, 1024:2048],
                                             AF.Identity),
                      waits=[(eT, f"f2.rmm0.{tb}")], label=f"f2.rsum.{tb}")
                eA.op(lambda e: e.activation(b.T2("r2B")[0:1, 0:TB], b.psB[0:1, 1024:2048],
                                             AF.Identity),
                      waits=[(eT, f"f2.rmm1.{tb}")], label=f"f2.rsq.{tb}")
                eA.op(lambda e: e.activation(b.T2("r2B")[0:1, 0:TB], b.T2("r2B")[0:1, 0:TB], AF.Sqrt),
                      waits=[(eV, f"f2.vare.{tb}")], label=f"f2.sqrt.{tb}")
                wts = [(eT, f"f2.bc0.{tb}")]
                if tb == 1:
                    wts.append((eV, "f3.s.0.15"))
                eA.op(lambda e: e.activation(b.T2("m2B")[:], b.psA[:, 0:TB],
                                             AF.Identity),
                      waits=wts, label=f"f2.embc.{tb}")
                wts = [(eT, f"f2.bc1.{tb}")]
                if tb == 1:
                    wts.append((eV, "f3.o.0.15"))
                eA.op(lambda e: e.activation(b.T2("r2B")[:], b.psA[:, 0:TB],
                                             AF.Identity),
                      waits=wts, label=f"f2.erbc.{tb}")

        def sync(E):
            for tb in [tb_sel]:
                for hc in range(HCH):
                    par = hc % 2
                    k = tb * HCH + hc
                    wts = []
                    if tb == 0:
                        wts += [(DQ["d_hW0"], "hW.126"),
                                (DQ["d_hW1"], "hW.127")]
                    if k >= 2:
                        wts.append((eV, f"f2.hq1.{k-2}"))
                    if tb == 1 and hc <= 1:
                        wts.append((eV, f"f3.o.0.{14+hc}"))
                    DQ[f"d_hR{par}"].op(lambda e, hc=hc, tb=tb, p=par: e.dma_start(
                        out=b.T2(f"hrd{p}")[:],
                        in_=b.h_sp[hc][:, tb*TB:(tb+1)*TB]),
                        waits=wts, label=f"hR.{k}")
                for fc in range(16):
                    k = tb * 16 + fc
                    for q in range(4):
                        g = tb * 64 + 4 * fc + q
                        wts = []
                        if tb == 0:
                            wts.append((DQ["d_owqW"], "owqW.127"))
                        if g >= 2:
                            ptb, r = divmod(g - 2, 64)
                            pfc, pq = divmod(r, 4)
                            wts.append((eT, f"f2.mm.{ptb}.{pfc}" if pq == 3
                                        else f"f2.mmq.{ptb}.{pfc}.{pq}"))
                        DQ[f"d_owqR{(4*fc+q) % 2}"].op(
                            lambda e, fc=fc, q=q: e.dma_start(
                                out=b.T2(f"owq{(4*fc+q) % 2}")[:],
                                in_=b.owq_cache[fc][:, q*2048:(q+1)*2048]),
                            waits=wts, label=f"owqR.{tb}.{fc}.{q}")
                    par = fc % 2
                    wts = []
                    if tb == 0:
                        wts += [(DQ["d_x2W0"], "x2W.30"),
                                (DQ["d_x2W1"], "x2W.31")]
                    if k >= 2:
                        pt, pf = prev_kf(tb, fc)
                        wts.append((eV, f"f2.res.{pt}.{pf}"))
                    DQ[f"d_x2R{par}"].op(lambda e, fc=fc, tb=tb, p=par: e.dma_start(
                        out=b.T2(f"x2r{p}")[:],
                        in_=b.x2_sp[fc*128:(fc+1)*128, tb*TB:(tb+1)*TB]),
                        waits=wts, label=f"x2R.{tb}.{fc}")

        def vec(E):
            for tb in [tb_sel]:
                for hc in range(HCH):
                    par = hc % 2
                    k = tb * HCH + hc
                    wts = [(DQ[f"d_hR{par}"], f"hR.{k}"), (eA, f"f2.sHB.{tb}")]
                    if k >= 1:
                        pt, ph_ = (tb, hc - 1) if hc >= 1 else (tb - 1, 63)
                        wts.append((eV, f"f2.hint.{pt}.{ph_}"))
                    if k == 0:
                        wts += [(eT, "f1.gmm.63"), (eT, "f1.vmm.63"),
                                (eV, "f1.wqg.127"), (eV, "f1.wqv.127")]
                    if tb == 1 and hc == 0:
                        wts += [(eT, "f2.mm.0.15"), (eV, "f3.o.0.15")]
                    eV.op(lambda e, p=par: e.tensor_tensor(
                        b.T2("scr4")[:], b.T2(f"hrd{p}")[:], b.T2("sHB")[:],
                        ALU.mult),
                        waits=wts, label=f"f2.hq1.{k}")
                    eV.op(lambda e: e.tensor_scalar(
                        b.T2("scr4")[:], b.T2("scr4")[:], RND, -RND,
                        ALU.add, ALU.add),
                        waits=[(eV, f"f2.hq1.{k}")], label=f"f2.hrnd.{k}")
                    dst = b.bigbuf if hc < 32 else b.arena
                    off = (hc % 32) * TB
                    eV.op(lambda e, d=dst, o=off: e.tensor_scalar(
                        d[:, o:o+TB], b.T2("scr4")[:], 127.0, -128.0,
                        ALU.min, ALU.max),
                        waits=[(eV, f"f2.hrnd.{k}")], label=f"f2.hint.{tb}.{hc}")
                for fc in range(16):
                    par = fc % 2
                    k = tb * 16 + fc
                    wts = [(eT, f"f2.mm.{tb}.{fc}"), (eA, f"f2.dq3B.{tb}"),
                           (DQ[f"d_x2R{par}"], f"x2R.{tb}.{fc}")]
                    if k >= 2:
                        pt, pf = prev_kf(tb, fc)
                        wts.append((DQ[f"d_o3W{par}"], f"o3W.{pt}.{pf}"))
                    if tb == 1 and fc <= 1:
                        wts.append((DQ["d_out"], f"out.0.{14+fc}"))
                    eV.op(lambda e, p=par, tb=tb, fc=fc: e.tensor_tensor(
                        b.T2(f"o3t{p}")[:], PS[fc % 2][:, 0:TB], b.T2("dq3B")[:],
                        ALU.mult),
                        waits=wts, label=f"f2.o1.{tb}.{fc}")
                    eV.op(lambda e, p=par, fc=fc: e.tensor_scalar(
                        b.T2(f"o3t{p}")[:], b.T2(f"o3t{p}")[:],
                        b.ob_t[:, fc:fc+1], None, ALU.add),
                        waits=[(eV, f"f2.o1.{tb}.{fc}")],
                        label=f"f2.o2.{tb}.{fc}")
                    eV.op(lambda e, p=par: e.tensor_tensor(
                        b.T2(f"o3t{p}")[:], b.T2(f"o3t{p}")[:],
                        b.T2(f"x2r{p}")[:], ALU.add),
                        waits=[(eV, f"f2.o2.{tb}.{fc}")],
                        label=f"f2.res.{tb}.{fc}")
                    if fc == 0:
                        eV.op(lambda e, p=par: e.tensor_copy(
                            b.T2("s2sum")[:], b.T2(f"o3t{p}")[:]),
                            waits=[(eV, f"f2.res.{tb}.0")] + (
                                [(eT, "f2.rmm0.0")] if tb else []),
                            label=f"f2.ssum.{tb}.0")
                        eV.op(lambda e: e.tensor_copy(b.T2("s2sq")[:],
                                                      b.T2("scr4")[:]),
                              waits=[(eA, f"f2.sq.{tb}.0")] + (
                                  [(eT, "f2.rmm1.0")] if tb else []),
                              label=f"f2.ssq.{tb}.0")
                    else:
                        eV.op(lambda e, p=par: e.tensor_tensor(
                            b.T2("s2sum")[:], b.T2("s2sum")[:],
                            b.T2(f"o3t{p}")[:], ALU.add),
                            waits=[(eV, f"f2.res.{tb}.{fc}"),
                                   (eV, f"f2.ssum.{tb}.{fc-1}")],
                            label=f"f2.ssum.{tb}.{fc}")
                        eV.op(lambda e: e.tensor_tensor(
                            b.T2("s2sq")[:], b.T2("s2sq")[:], b.T2("scr4")[:],
                            ALU.add),
                            waits=[(eA, f"f2.sq.{tb}.{fc}"),
                                   (eV, f"f2.ssq.{tb}.{fc-1}")],
                            label=f"f2.ssq.{tb}.{fc}")
                eV.op(lambda e: e.tensor_scalar(b.T2("m2B")[0:1, 0:TB], b.T2("m2B")[0:1, 0:TB], 1.0 / D, None,
                                                ALU.mult),
                      waits=[(eA, f"f2.rsum.{tb}")] + (
                          [(eT, "f2.bc0.0"), (eT, "f2.bc1.0")] if tb else []),
                      label=f"f2.m2.{tb}")
                eV.op(lambda e: e.tensor_tensor(
                    b.T2("scr4")[0:1, 0:TB], b.T2("m2B")[0:1, 0:TB],
                    b.T2("m2B")[0:1, 0:TB], ALU.mult),
                      waits=[(eV, f"f2.m2.{tb}"), (eV, f"f2.ssq.{tb}.15"),
                             (eA, f"f2.sq.{tb}.15")],
                      label=f"f2.msq.{tb}")
                eV.op(lambda e: e.tensor_scalar(b.T2("r2B")[0:1, 0:TB], b.T2("r2B")[0:1, 0:TB], 1.0 / D, None,
                                                ALU.mult),
                      waits=[(eA, f"f2.rsq.{tb}")], label=f"f2.e2.{tb}")
                eV.op(lambda e: e.tensor_tensor(
                    b.T2("r2B")[0:1, 0:TB], b.T2("r2B")[0:1, 0:TB],
                    b.T2("scr4")[0:1, 0:TB], ALU.subtract),
                      waits=[(eV, f"f2.msq.{tb}"), (eV, f"f2.e2.{tb}")],
                      label=f"f2.var.{tb}")
                eV.op(lambda e: e.tensor_scalar(b.T2("r2B")[0:1, 0:TB], b.T2("r2B")[0:1, 0:TB], LN_EPS, None,
                                                ALU.add),
                      waits=[(eV, f"f2.var.{tb}")], label=f"f2.vare.{tb}")
                eV.op(lambda e: e.reciprocal(b.T2("r2B")[0:1, 0:TB], b.T2("r2B")[0:1, 0:TB]),
                      waits=[(eA, f"f2.sqrt.{tb}")], label=f"f2.rstd.{tb}")
                for fc in range(16):
                    par = fc % 2
                    k = tb * 16 + fc
                    wts = [(DQ[f"d_o3R{par}"], f"o3R.{tb}.{fc}"),
                           (eA, f"f2.embc.{tb}"), (eA, f"f2.erbc.{tb}")]
                    if k >= 2:
                        pt, pf = prev_kf(tb, fc)
                        wts.append((DQ["d_out"], f"out.{pt}.{pf}"))
                    eV.op(lambda e, p=par: e.tensor_tensor(
                        b.T2(f"o3t{p}")[:], b.T2(f"hrd{p}")[:], b.T2("m2B")[:],
                        ALU.subtract), waits=wts, label=f"f3.s.{tb}.{fc}")
                    eV.op(lambda e, p=par: e.tensor_tensor(
                        b.T2(f"o3t{p}")[:], b.T2(f"o3t{p}")[:], b.T2("r2B")[:],
                        ALU.mult),
                        waits=[(eV, f"f3.s.{tb}.{fc}")], label=f"f3.o.{tb}.{fc}")

        def sync2(E):
            for tb in [tb_sel]:
                for fc in range(16):
                    par = fc % 2
                    k = tb * 16 + fc
                    wts = [(DQ["d_o3W0"], f"o3W.{tb}.14"),
                           (DQ["d_o3W1"], f"o3W.{tb}.15"),
                           (eV, f"f2.hq1.{tb*HCH + HCH - 1}")]
                    if k >= 2:
                        pt, pf = prev_kf(tb, fc)
                        wts.append((eV, f"f3.s.{pt}.{pf}"))
                    DQ[f"d_o3R{par}"].op(lambda e, fc=fc, tb=tb, p=par: e.dma_start(
                        out=b.T2(f"hrd{p}")[:],
                        in_=b.o3_sp[fc*128:(fc+1)*128, tb*TB:(tb+1)*TB]),
                        waits=wts, label=f"o3R.{tb}.{fc}")

        def scalarD2(E):
            for tb in [tb_sel]:
                for fc in range(16):
                    par = fc % 2
                    wts = [(eV, f"f3.o.{tb}.{fc}")]
                    if tb * 16 + fc >= 1:
                        pt, pf = (tb, fc - 1) if fc >= 1 else (tb - 1, 15)
                        wts.append((DQ["d_out"], f"out.{pt}.{pf}"))
                    DQ["d_out"].op(lambda e, fc=fc, tb=tb, p=par: e.dma_start(
                        out=b.OUT[fc*128:(fc+1)*128, tb*TB:(tb+1)*TB],
                        in_=b.T2(f"o3t{p}")[:]),
                        waits=wts, label=f"out.{tb}.{fc}")

        return {"tensor": pe, "scalar": act, "sync": sync,
                "vector": vec, "sync2": sync2, "scalarD2": scalarD2}

    def ph_fin(self):
        b = self
        DQ = b.DQ

        def sync(E):
            for tp in b.taps:
                if tp == "A":
                    DQ["d_tap"].op(lambda e: e.dma_start(out=b.tap_outs["A"],
                                                         in_=b.acol2[:]),
                                   waits=[(b.eA, "cc.acol2")], label="tap.A")
                elif tp == "gvals":
                    DQ["d_tap"].op(lambda e: e.dma_start(out=b.tap_outs["gvals"],
                                                         in_=b.gvals[:]),
                                   waits=[(b.eV, "cc.ginv")], label="tap.gvals")
                elif tp == "x2":
                    DQ["d_tap"].op(lambda e: e.dma_start(out=b.tap_outs["x2"],
                                                         in_=b.x2_sp),
                                   waits=[(b.DQ["d_x2W1"], "x2W.31")],
                                   label="tap.x2")
                elif tp == "x2pre":
                    DQ["d_tap"].op(lambda e: e.dma_start(out=b.tap_outs["x2pre"],
                                                         in_=b.x2pre_sp),
                                   waits=[(b.DQ["d_x2preW1"], "x2preW.31")],
                                   label="tap.x2pre")
                elif tp == "h":
                    DQ["d_tap"].op(lambda e: e.dma_start(
                        out=b.tap_outs["h"],
                        in_=b.h_sp.rearrange("a p t -> (a p) t")),
                        waits=[(b.DQ["d_hW1"], "hW.127")], label="tap.h")
            E.wait_val(DQ["d_out"], 16 * 32)
            if b.taps:
                E.wait_val(DQ["d_tap"], 16 * len(b.taps))

        return {"sync": sync}

    def run_phases(self, block=None):
        order1 = [("p0", self.ph_p0), ("p1", self.ph_p1), ("p1b", self.ph_p1b),
                  ("pcc", self.ph_pcc), ("p2", self.ph_p2), ("ao", self.ph_ao),
                  ("ln1", self.ph_ln1), ("ln1b", self.ph_ln1b),
                  ("f1", self.ph_f1), ("f1b", self.ph_f1b)]
        order2 = [("f2t0", lambda: self.ph_f2(0)),
                  ("f2t1", lambda: self.ph_f2(1)), ("fin", self.ph_fin)]
        sect_engine = {"sync": "sync", "gpsimd": "gpsimd", "tensor": "tensor",
                       "scalar": "scalar", "vector": "vector",
                       "scalarD": "scalar", "sync2": "sync",
                       "scalarD2": "scalar", "gpsimd2": "gpsimd"}

        def emit(order):
            for pname, pfn in order:
                secs = pfn()
                for sec, fn in secs.items():
                    engname = sect_engine[sec]
                    if self.mode == "count":
                        fn(_FakeE())
                    else:
                        def runner(engobj, fn=fn):
                            for e in ([self.eT, self.eV, self.eA, self.eP]
                                      + list(self.DQ.values())):
                                e.real = engobj
                            fn(_SecE(self, engobj))
                        getattr(block, engname)(runner)

        emit(order1)
        if self.mode == "emit":
            self.era1.close()
            self.open_era2()
        else:
            for n in ["owq0", "owq1", "sHB", "dq3B", "scr4", "hrd0", "hrd1",
                      "x2r0", "x2r1", "o3t0", "o3t1", "s2sum", "s2sq",
                      "m2B", "r2B"]:
                self.t2[n] = self.arena
        emit(order2)


def build_nc(taps=()):
    b = Builder(taps=taps)
    b.open_era1()
    b.mode = "count"
    b.run_phases()
    counts = {e.name: e.cnt for e in [b.eT, b.eV, b.eA, b.eP] + list(b.DQ.values())}
    for e in [b.eT, b.eV, b.eA, b.eP] + list(b.DQ.values()):
        e.cnt = 0
    b.mode = "emit"
    for e in [b.eT, b.eV, b.eA, b.eP] + list(b.DQ.values()):
        cm = b.nc.semaphore(e.name)
        e.sem = cm.__enter__()
        b.stack.callback(cm.__exit__, None, None, None)
    with b.nc.Block() as block:
        b.run_phases(block)
    for e in [b.eT, b.eV, b.eA, b.eP] + list(b.DQ.values()):
        assert e.cnt == counts[e.name], (e.name, e.cnt, counts[e.name])
    b.era2.close()
    return b.nc



def _prep_shared(inp):
    f32 = np.float32
    W_in = np.asarray(inp["L0_attn_W_in"], f32)
    W_out = np.asarray(inp["L0_attn_W_out"], f32)
    gate_w = np.asarray(inp["L0_ffn_gate_w"], f32)
    val_w = np.asarray(inp["L0_ffn_val_w"], f32)
    out_w = np.asarray(inp["L0_ffn_out_w"], f32)

    def blocks_of_T(w):
        wt = np.ascontiguousarray(w.T)
        E = wt.shape[1]
        return np.ascontiguousarray(
            wt.reshape(D, E // 128, 128).transpose(1, 0, 2))

    d = {}
    d["WinT"] = blocks_of_T(W_in)
    d["WoT"] = blocks_of_T(W_out)
    d["GwT"] = blocks_of_T(gate_w)
    d["VwT"] = blocks_of_T(val_w)
    owt = np.ascontiguousarray(out_w.T)
    ow_blocks = np.empty((64, D, 128), f32)
    for fc in range(16):
        for jj in range(4):
            ow_blocks[4 * fc + jj] = owt[2048*jj:2048*(jj+1), 128*fc:128*(fc+1)]
    d["OwT"] = np.ascontiguousarray(ow_blocks)

    def bias_T(v, k):
        return np.ascontiguousarray(np.asarray(v, f32).reshape(k, 128).T)

    d["b_inT"] = bias_T(inp["L0_attn_b_in"], 48)
    d["b_outT"] = bias_T(inp["L0_attn_b_out"], 16)
    d["gate_bT"] = bias_T(inp["L0_ffn_gate_b"], 64)
    d["val_bT"] = bias_T(inp["L0_ffn_val_b"], 64)
    d["out_bT"] = bias_T(inp["L0_ffn_out_b"], 16)

    inv_freq = (1.0 / 10000.0 ** (np.arange(0, 1024, 2, dtype=f32)
                                  / 1024.0)).astype(f32)
    invfF = np.repeat(inv_freq, 2)
    d["invfF"] = np.ascontiguousarray(invfF.reshape(8, 128).T)
    d["onesR"] = np.ones((1, 128), f32)
    d["onesC"] = np.ones((128, 1), f32)
    d["eye"] = np.eye(128, dtype=f32)
    P = np.zeros((128, 128), f32)
    for j in range(64):
        P[2*j+1, 2*j] = -1.0
        P[2*j, 2*j+1] = 1.0
    d["perm"] = P
    d["zrowD"] = np.zeros((1, T), f32)
    return d


def _prep_core(inp, shared, core):
    f32 = np.float32
    X = np.asarray(inp["L0_attn_X"], f32)
    bsel = np.zeros((128, 2), f32)
    bdx = core // 4
    q = core % 4
    bsel[:, bdx] = 1.0
    dcore = dict(shared)
    dcore["XT"] = np.ascontiguousarray(X[bdx, q*T:(q+1)*T, :].T)
    dcore["ramp"] = (np.arange(T, dtype=f32) + q * T)[None, :]
    dcore["bsel"] = bsel
    gate_w = np.asarray(inp["L0_ffn_gate_w"], f32)
    val_w = np.asarray(inp["L0_ffn_val_w"], f32)
    out_w = np.asarray(inp["L0_ffn_out_w"], f32)

    def slice8(wT, c):
        ncols = wT.shape[1] // 8
        sl = wT[:, c*ncols:(c+1)*ncols]
        rows = sl.shape[0]
        npieces = rows * ncols // (D * 128)
        return np.ascontiguousarray(
            sl.reshape(rows // D, D, ncols // 128, 128)
            .transpose(0, 2, 1, 3).reshape(npieces, D, 128))

    dcore["gW8"] = slice8(np.ascontiguousarray(gate_w.T), core)
    dcore["vW8"] = slice8(np.ascontiguousarray(val_w.T), core)
    ow8 = np.ascontiguousarray(out_w.T)
    sl = ow8[:, core*256:(core+1)*256]
    dcore["oW8"] = np.ascontiguousarray(
        sl.reshape(4, D, 2, 128).transpose(0, 2, 1, 3).reshape(8, D, 128))
    return dcore


def kernel(**inputs):
    shared = _prep_shared(inputs)
    in_maps = [_prep_core(inputs, shared, c) for c in range(8)]
    nc = build_nc()
    res = run_bass_kernel_spmd(nc, in_maps, core_ids=list(range(8)))
    out = np.empty((2, 8192, D), np.float32)
    for c in range(8):
        bdx, q = c // 4, c % 4
        out[bdx, q*T:(q+1)*T, :] = res.results[c]["OUT"].T
    return out


# version 07
# speedup vs baseline: 1.0434x; 1.0138x over previous
import contextlib
import numpy as np

import concourse.bass as bass
import concourse.mybir as mybir
from concourse.bass_utils import run_bass_kernel_spmd

F32 = mybir.dt.float32
FP16 = mybir.dt.float16
AF = mybir.ActivationFunctionType
ALU = mybir.AluOpType
AX = mybir.AxisListType

RND = 12582912.0
PI = float(np.float32(np.pi))
HALF_PI = float(np.float32(np.pi / 2))
TWO_PI = float(np.float32(2 * np.pi))
INV_2PI = float(np.float32(1.0 / (2 * np.pi)))
CW_C1 = 6.28125
CW_C2 = float(np.float32(2 * np.pi - 6.28125))

T = 2048
HF = 1024
D = 2048
H = 8192
NCH = 16
HCH = 64
TCOL = 4
TB = 1024
LN_EPS = 1e-5
NW = float(H * D)


def hsl(i):
    return slice(i * HF, (i + 1) * HF)


class Eng:
    def __init__(self, owner, name, step=1):
        self.o = owner
        self.name = name
        self.step = step
        self.cnt = 0
        self.real = None
        self.sem = None

    def op(self, emit_fn, waits=(), label=None):
        self.cnt += self.step
        if self.o.mode == "count":
            if label is not None:
                key = (self.name, label)
                assert key not in self.o.tok, f"dup label {key}"
                self.o.tok[key] = self.cnt
        else:
            if label is not None:
                exp = self.o.tok[(self.name, label)]
                assert exp == self.cnt, (
                    f"token drift at {self.name}:{label}: table {exp}, "
                    f"emit {self.cnt}")
            for s_eng, lab in waits:
                self.real.wait_ge(s_eng.sem, self.o.tok[(s_eng.name, lab)])
            ins = emit_fn(self.real)
            ins.then_inc(self.sem, self.step)
        return self.cnt


class _SecE:
    def __init__(self, b, engobj):
        self.b = b
        self.engobj = engobj

    def wait(self, s_eng, lab):
        self.engobj.wait_ge(s_eng.sem, self.b.tok[(s_eng.name, lab)])

    def wait_val(self, s_eng, val):
        self.engobj.wait_ge(s_eng.sem, val)


class _FakeE:
    def wait(self, *a, **k):
        pass

    def wait_val(self, *a, **k):
        pass


class Builder:
    def __init__(self, taps=()):
        self.taps = taps
        self.mode = "count"
        self.tok = {}
        nc = self.nc = bass.Bass("TRN2", target_bir_lowering=False,
                                 dynamic_dma_scratch_size=16384)
        self.stack = contextlib.ExitStack()

        din = lambda n, s, dt=F32: nc.dram_tensor(n, s, dt, kind="ExternalInput").ap()
        self.XT = din("XT", [D, T])
        self.WinT = din("WinT", [48, D, 128])
        self.WoT = din("WoT", [16, D, 128])
        self.GwT = din("GwT", [64, D, 128])
        self.VwT = din("VwT", [64, D, 128])
        self.OwT = din("OwT", [64, D, 128])
        self.b_inT = din("b_inT", [128, 48])
        self.b_outT = din("b_outT", [128, 16])
        self.gate_bT = din("gate_bT", [128, 64])
        self.val_bT = din("val_bT", [128, 64])
        self.out_bT = din("out_bT", [128, 16])
        self.ramp = din("ramp", [1, T])
        self.invfF = din("invfF", [128, 8])
        self.onesR = din("onesR", [1, 128])
        self.onesC = din("onesC", [128, 1])
        self.eye = din("eye", [128, 128])
        self.perm = din("perm", [128, 128])
        self.bsel = din("bsel", [128, 2])
        self.zrowD = din("zrowD", [1, T])
        self.gW8 = din("gW8", [8, D, 128])
        self.vW8 = din("vW8", [8, D, 128])
        self.oW8 = din("oW8", [8, D, 128])

        self.OUT = nc.dram_tensor("OUT", [D, T], F32, kind="ExternalOutput").ap()
        dint = lambda n, s, dt=F32: nc.dram_tensor(n, s, dt, kind="Internal").ap()
        self.x2pre_sp = dint("x2pre_sp", [D, T])
        self.x2_sp = dint("x2_sp", [D, T])
        self.h_sp = dint("h_sp", [HCH, 128, T])
        self.owq_cache = dint("owq_cache", [16, 128, H], FP16)
        self.o3_sp = dint("o3_sp", [D, T])
        self.tab_sp = dint("tab_sp", [8, 2, 2, 128, HF])
        self.ccin = dint("ccin", [4, T])
        self.ccout = dint("ccout", [4, T])
        self.bounce = dint("bounce", [8, T])
        self.tap_outs = {}
        for tp in taps:
            shp = {"A": [128, 16], "gvals": [1, 8], "x2": [D, T],
                   "x2pre": [D, T], "h": [HCH * 128, T],
                   "kv": [128, NCH * T], "srow": [1, T],
                   "arena": [128, NCH * T], "k0": [128, T],
                   "v0": [128, T], "rope0": [128, T],
                   "nsq": [128, T], "sq00": [128, HF],
                   "nsq0": [128, T],
                   **{f"gc{n}": [128, T] for n in range(16)},
                   **{f"hc{n}": [128, HF] for n in range(16)},
                   "rampend": [128, T], "ka3": [128, T],
                   "kr3": [128, T]}[tp]
            dtt = FP16 if tp in ("kv", "arena") else F32
            self.tap_outs[tp] = nc.dram_tensor("tap_" + tp, shp, dtt,
                                               kind="ExternalOutput").ap()

        sb = lambda n, s, dt: self.stack.enter_context(nc.sbuf_tensor(n, s, dt))
        self.arena = sb("arena", [128, NCH * T], FP16)
        self.bigbuf = sb("bigbuf", [128, NCH * T], FP16)
        self.bin_t = sb("bin_t", [128, 48], F32)
        self.bout_t = sb("bout_t", [128, 16], F32)
        self.gb_t = sb("gb_t", [128, 64], F32)
        self.vb_t = sb("vb_t", [128, 64], F32)
        self.ob_t = sb("ob_t", [128, 16], F32)
        self.invf_t = sb("invf_t", [128, 8], F32)
        self.onesR_t = sb("onesR_t", [1, 128], F32)
        self.onesC_t = sb("onesC_t", [128, 1], F32)
        self.eye_t = sb("eye_t", [128, 128], F32)
        self.perm_t = sb("perm_t", [128, 128], F32)
        self.bsel_t = sb("bsel_t", [128, 2], F32)
        self.acol = sb("acol", [128, 16], F32)
        self.acol2 = sb("acol2", [128, 16], F32)
        self.rbounce = sb("rbounce", [16, 128], F32)
        self.acolB = sb("acolB", [128, 16], F32)
        self.gacc = sb("gacc", [128, 4], F32)
        self.gcells = sb("gcells", [1, 4], F32)
        self.gvals = sb("gvals", [1, 8], F32)
        self.gB8 = sb("gB8", [128, 8], F32)
        self.rndP = sb("rndP", [128, 1], F32)
        self.rndN = sb("rndN", [128, 1], F32)
        for n in ["sT16", "mT16", "rT16", "siT16", "hT16", "dT16", "t16a", "t16b"]:
            setattr(self, n, sb(n, [128, 16], F32))
        self.rowbuf = sb("rowbuf", [1, 2 * T], F32)
        self.rampR = self.rowbuf[0:1, 0:T]
        self.nrowR = self.rowbuf[0:1, 0:T]
        self.ln1rows = [self.rowbuf[0:1, 0:T], self.rowbuf[0:1, T:2*T]]
        self.sHrow = self.rowbuf[0:1, 0:T]
        self.dq3row = self.rowbuf[0:1, T:2*T]

        self.psA = self.stack.enter_context(nc.psum_tensor("psA", [128, T], F32))
        self.psB = self.stack.enter_context(nc.psum_tensor("psB", [128, T], F32))
        self.PS = [self.psA, self.psB]

        self.eT = Eng(self, "eT")
        self.eV = Eng(self, "eV")
        self.eA = Eng(self, "eA")
        self.eP = Eng(self, "eP")
        names = ["d_tabW", "d_tabR",
                 "d_cst", "d_xt", "d_pre0", "d_pre1", "d_wk", "d_wv",
                 "d_wq0", "d_wq1", "d_wo0", "d_wo1", "d_xres0", "d_xres1",
                 "d_x2preW0", "d_x2preW1", "d_x2preR0", "d_x2preR1",
                 "d_x2W0", "d_x2W1", "d_x2R0", "d_x2R1",
                 "d_gw0", "d_gw1", "d_vw0", "d_vw1", "d_owp0", "d_owp1", "d_owp2", "d_owp3", "d_owqW",
                 "d_hW0", "d_hW1", "d_hR0", "d_hR1",
                 "d_owqR0", "d_owqR1", "d_owqR2", "d_owqR3",
                 "d_o3W0", "d_o3W1", "d_o3R0", "d_o3R1",
                 "d_out", "d_cc", "d_ccR", "d_bnc", "d_tap"]
        self.DQ = {n: Eng(self, n, 16) for n in names}
        self.era1 = None
        self.t1 = {}
        self.t2 = {}

    def open_era1(self):
        self.era1 = contextlib.ExitStack()
        e = self.era1.enter_context
        nc = self.nc
        for n in ["gA", "gB", "gC", "gD"]:
            self.t1[n] = e(nc.sbuf_tensor(n, [128, T], F32))
        for n in ["hA", "hB", "hC", "hD", "hE"]:
            self.t1[n] = e(nc.sbuf_tensor(n, [128, HF], F32))
        self.t1["wkv16"] = e(nc.sbuf_tensor("wkv16", [128, 2 * D], FP16))

    def open_era2(self):
        self.era2 = contextlib.ExitStack()
        e = self.era2.enter_context
        nc = self.nc
        t2 = self.t2
        for i in range(2):
            t2[f"owq{i}"] = e(nc.sbuf_tensor(f"owq{i}", [128, 2048], FP16))
        t2["sHB"] = e(nc.sbuf_tensor("sHB", [128, TB], F32))
        t2["dq3B"] = e(nc.sbuf_tensor("dq3B", [128, TB], F32))
        for n in ["scr4", "hrd0", "hrd1", "x2r0", "x2r1", "o3t0", "o3t1",
                  "s2sum", "s2sq", "m2B", "r2B"]:
            t2[n] = e(nc.sbuf_tensor(n, [128, TB], F32))

    def T1(self, n):
        return self.t1[n]

    def T2(self, n):
        return self.t2[n]

    def ph_p0(self):
        b = self
        eT, eV, eA, eP, DQ = b.eT, b.eV, b.eA, b.eP, b.DQ

        def sync(E):
            consts = [(b.bin_t[:], b.b_inT), (b.bout_t[:], b.b_outT),
                      (b.gb_t[:], b.gate_bT), (b.vb_t[:], b.val_bT),
                      (b.ob_t[:], b.out_bT), (b.rampR, b.ramp),
                      (b.invf_t[:], b.invfF), (b.onesR_t[:], b.onesR),
                      (b.onesC_t[:], b.onesC), (b.eye_t[:], b.eye),
                      (b.perm_t[:], b.perm), (b.bsel_t[:], b.bsel)]
            for dst, src in consts:
                DQ["d_cst"].op(lambda e, d=dst, s=src: e.dma_start(out=d, in_=s))
            for k in range(48):
                pc, ih = k // 2, k % 2
                src = [b.gW8, b.vW8, b.oW8][pc // 8][pc % 8][ih*HF:(ih+1)*HF, :]
                par = k % 2
                buf = ["hA", "hB"][par]
                wts = [(eA, f"p0.abs.{k-2}")] if k >= 2 else []
                DQ[f"d_pre{par}"].op(
                    lambda e, s=src, bu=buf: e.dma_start(
                        out=b.T1(bu)[:].rearrange("p (dc h) -> p dc h", h=128),
                        in_=s.rearrange("(dc p) h -> p dc h", p=128)),
                    waits=wts, label=f"pre.{k}")

        def gp(E):
            DQ["d_xt"].op(lambda e: e.dma_start(
                out=b.arena[:].rearrange("p (dc t) -> p dc t", t=T),
                in_=b.XT.rearrange("(dc p) t -> p dc t", p=128)), label="xt")

        def pe(E):
            E.wait_val(DQ["d_cst"], 16 * 12)
            for tc in range(TCOL):
                eT.op(lambda e, tc=tc: e.matmul(
                    b.psA[:, tc*512:(tc+1)*512], b.onesR_t[:],
                    b.rampR[0:1, tc*512:(tc+1)*512], start=True, stop=True),
                    label=("p0.rampmm" if tc == 3 else None))

        def act(E):
            E.wait(eT, "p0.rampmm")
            eA.op(lambda e: e.activation(b.T1("gD")[:], b.psA[:], AF.Identity),
                  label="p0.rampB")
            for k in range(48):
                par = k % 2
                buf = ["hA", "hB"][par]
                wts = [(DQ[f"d_pre{par}"], f"pre.{k}")]
                if k >= 1:
                    wts.append((eV, f"p0.red.{k-1}"))
                eA.op(lambda e, bu=buf: e.activation(b.T1("hC")[:], b.T1(bu)[:],
                                                     AF.Abs),
                      waits=wts, label=f"p0.abs.{k}")

        def vec(E):
            eV.op(lambda e: e.memset(b.gacc[:], 0.0), label="p0.gacc0")
            eV.op(lambda e: e.memset(b.rndP[:], RND), label="p0.rndp")
            eV.op(lambda e: e.memset(b.rndN[:], -RND), label="p0.rndn")
            for k in range(48):
                m = k // 16
                wts = [(eA, f"p0.abs.{k}")]
                if k >= 1:
                    wts.append((eV, f"p0.acc.{k-1}"))
                eV.op(lambda e: e.tensor_reduce(b.t16a[:, 0:1], b.T1("hC")[:],
                                                AX.X, ALU.add),
                      waits=wts, label=f"p0.red.{k}")
                eV.op(lambda e, m=m: e.tensor_tensor(b.gacc[:, m:m+1],
                                                     b.gacc[:, m:m+1],
                                                     b.t16a[:, 0:1], ALU.add),
                      waits=[(eV, f"p0.red.{k}"), (eV, "p0.gacc0")],
                      label=f"p0.acc.{k}")

        return {"sync": sync, "gpsimd": gp, "tensor": pe, "scalar": act,
                "vector": vec}

    def _theta_chain(self, pref, c, extra0, sq_guard, ih):
        b = self
        eV, eA = b.eV, b.eA
        if True:
            s = hsl(ih)
            wts = [(eA, "p0.rampB")]
            if ih == 0:
                wts += list(extra0)
            else:
                wts += [(eA, f"{pref}.cos.{c}.0"),
                        (eV, f"{pref}.rope.{c}.0")]
                if pref == "p1":
                    wts.append((eV, f"p1.nsq.{c}.0"))
            eV.op(lambda e, s=s: e.tensor_scalar(
                b.T1("hA")[:], b.T1("gD")[:, s], b.invf_t[:, c:c+1], None,
                ALU.mult), waits=wts, label=f"{pref}.th0.{c}.{ih}")
            eV.op(lambda e: e.tensor_scalar(b.T1("hB")[:], b.T1("hA")[:],
                                            INV_2PI, RND, ALU.mult, ALU.add),
                  waits=[(eV, f"{pref}.th0.{c}.{ih}")], label=f"{pref}.r.{c}.{ih}")
            wts = [(eV, f"{pref}.r.{c}.{ih}")]
            if sq_guard is not None and ih == 0:
                wts.append(sq_guard)
            eV.op(lambda e: e.tensor_scalar(b.T1("hC")[:], b.T1("hB")[:],
                                            -RND, CW_C2, ALU.add, ALU.mult),
                  waits=wts, label=f"{pref}.nc2.{c}.{ih}")
            eV.op(lambda e: e.tensor_scalar(b.T1("hB")[:], b.T1("hB")[:],
                                            -RND, CW_C1, ALU.add, ALU.mult),
                  waits=[(eV, f"{pref}.nc2.{c}.{ih}")], label=f"{pref}.nc1.{c}.{ih}")
            eV.op(lambda e: e.tensor_tensor(b.T1("hA")[:], b.T1("hA")[:],
                                            b.T1("hB")[:], ALU.subtract),
                  waits=[(eV, f"{pref}.nc1.{c}.{ih}")], label=f"{pref}.tha.{c}.{ih}")
            eV.op(lambda e: e.tensor_tensor(b.T1("hA")[:], b.T1("hA")[:],
                                            b.T1("hC")[:], ALU.subtract),
                  waits=[(eV, f"{pref}.tha.{c}.{ih}")], label=f"{pref}.thr0.{c}.{ih}")
            eV.op(lambda e: e.tensor_scalar(b.T1("hA")[:], b.T1("hA")[:],
                                            PI, -PI, ALU.min, ALU.max),
                  waits=[(eV, f"{pref}.thr0.{c}.{ih}")], label=f"{pref}.thr.{c}.{ih}")
            eV.op(lambda e: e.tensor_scalar(b.T1("hA")[:], b.T1("hA")[:],
                                            HALF_PI, None, ALU.add),
                  waits=[(eA, f"{pref}.sin.{c}.{ih}")], label=f"{pref}.ps0.{c}.{ih}")
            eV.op(lambda e: e.tensor_scalar(b.T1("hB")[:], b.T1("hA")[:],
                                            PI, None, ALU.is_gt),
                  waits=[(eV, f"{pref}.ps0.{c}.{ih}")], label=f"{pref}.msk.{c}.{ih}")
            eV.op(lambda e: e.scalar_tensor_tensor(
                b.T1("hA")[:], b.T1("hB")[:], -TWO_PI, b.T1("hA")[:],
                ALU.mult, ALU.add),
                waits=[(eV, f"{pref}.msk.{c}.{ih}")], label=f"{pref}.psr0.{c}.{ih}")
            eV.op(lambda e: e.tensor_scalar(b.T1("hA")[:], b.T1("hA")[:],
                                            PI, -PI, ALU.min, ALU.max),
                waits=[(eV, f"{pref}.psr0.{c}.{ih}")], label=f"{pref}.psr.{c}.{ih}")

    def _tables_act(self, pref, c, ih):
        b = self
        eV, eA = b.eV, b.eA
        pc, pih = (c, 0) if ih == 1 else (c - 1, 1)
        wts = [(eV, f"{pref}.thr.{c}.{ih}")]
        if pref == "p1" and (c, ih) != (0, 0):
            wts.append((b.DQ["d_tabW"], f"tabWs.{pc}.{pih}"))
        eA.op(lambda e: e.activation(b.T1("hE")[:], b.T1("hA")[:], AF.Sin),
              waits=wts, label=f"{pref}.sin.{c}.{ih}")
        wts = [(eV, f"{pref}.psr.{c}.{ih}")]
        if pref == "p1" and (c, ih) != (0, 0):
            wts.append((b.DQ["d_tabW"], f"tabWc.{pc}.{pih}"))
        eA.op(lambda e: e.activation(b.T1("hD")[:], b.T1("hA")[:], AF.Sin),
              waits=wts, label=f"{pref}.cos.{c}.{ih}")

    def _rope_combine(self, pref, c, G, ih):
        b = self
        eV, eA, eT = b.eV, b.eA, b.eT
        if True:
            s = hsl(ih)
            wts = [(eT, f"{pref}.perm.{c}"), (eA, f"{pref}.cos.{c}.{ih}"),
                   (eV, f"{pref}.psr.{c}.{ih}")]
            if ih == 1:
                wts.append((eV, f"{pref}.rope.{c}.0"))
            eV.op(lambda e, s=s, G=G: e.tensor_tensor(
                b.T1("hB")[:], G[:, s], b.T1("hE")[:], ALU.mult),
                waits=wts, label=f"{pref}.krs.{c}.{ih}")
            kcw = [(eV, f"{pref}.krs.{c}.{ih}")]
            if pref == "p1" and c == 0 and ih == 0 and "k0" in b.taps:
                kcw.append((b.DQ["d_tap"], "tap.k0"))
            if pref == "p1" and c == 3 and ih == 0 and "ka3" in b.taps:
                kcw.append((b.DQ["d_tap"], "tap.ka3"))
            eV.op(lambda e, s=s: e.tensor_tensor(
                b.T1("gA")[:, s], b.T1("gA")[:, s], b.T1("hD")[:], ALU.mult),
                waits=kcw, label=f"{pref}.kcs.{c}.{ih}")
            eV.op(lambda e, s=s: e.tensor_tensor(
                b.T1("gA")[:, s], b.T1("gA")[:, s], b.T1("hB")[:], ALU.add),
                waits=[(eV, f"{pref}.kcs.{c}.{ih}")], label=f"{pref}.rope.{c}.{ih}")

    def ph_p1(self):
        b = self
        eT, eV, eA, eP, DQ, PS = b.eT, b.eV, b.eA, b.eP, b.DQ, b.PS

        def sync(E):
            chain = []
            def tp(name, ap, wts):
                DQ["d_tap"].op(lambda e, ap=ap: e.dma_start(
                    out=b.tap_outs[name], in_=ap),
                    waits=wts + ([(DQ["d_tap"], chain[-1])] if chain else []),
                    label=f"tap.{name}")
                chain.append(f"tap.{name}")
            if "arena" in b.taps:
                tp("arena", b.arena[:], [(DQ["d_xt"], "xt")])
            if "k0" in b.taps:
                tp("k0", b.T1("gA")[:], [(eA, "p1.kev.0.1")])
            if "v0" in b.taps:
                tp("v0", b.T1("gB")[:], [(eA, "p1.vev.0.1")])
            prev = [None]
            for c in range(8):
                for ih in range(2):
                    for sc, buf, lab in ((0, "hE", "tabWs"), (1, "hD", "tabWc")):
                        wts = [(eA, f"p1.{'sin' if sc == 0 else 'cos'}.{c}.{ih}")]
                        if prev[0] is not None:
                            wts.append((DQ["d_tabW"], prev[0]))
                        DQ["d_tabW"].op(lambda e, c=c, ih=ih, sc=sc, buf=buf:
                                        e.dma_start(
                            out=b.tab_sp[c, ih, sc],
                            in_=b.T1(buf)[:]),
                            waits=wts, label=f"{lab}.{c}.{ih}")
                        prev[0] = f"{lab}.{c}.{ih}"
            if "rope0" in b.taps:
                tp("rope0", b.T1("gA")[:], [(eV, "p1.rope.0.1")])
            if "sq00" in b.taps:
                tp("sq00", b.T1("hC")[:], [(eA, "p1.sq.0.0")])
            if "nsq0" in b.taps:
                tp("nsq0", b.T1("gC")[:], [(eV, "p1.nsq.0.1")])
            for n in range(16):
                if n == 3:
                    if "ka3" in b.taps:
                        tp("ka3", b.T1("gA")[:], [(eA, "p1.kev.3.1")])
                    if "kr3" in b.taps:
                        tp("kr3", b.T1("gA")[:], [(eV, "p1.rope.3.1")])
                if f"hc{n}" in b.taps:
                    tp(f"hc{n}", b.T1("hC")[:], [(eA, f"p1.sq.{n}.0")])
                if f"gc{n}" in b.taps:
                    tp(f"gc{n}", b.T1("gC")[:], [(eV, f"p1.nsq.{n}.1")])
            if "rampend" in b.taps:
                tp("rampend", b.T1("gD")[:], [(eV, "p1.nsq.15.1")])

        def gp(E):
            for c in range(16):
                wts = []
                if c > 0:
                    wts += [(eT, f"p1.kmm.{c-1}"), (eT, f"p1.vmm.{c-1}")]
                DQ["d_wk"].op(lambda e, c=c: e.dma_start(
                    out=b.T1("wkv16")[:, 0:D].rearrange("p (dc e) -> p dc e", e=128),
                    in_=b.WinT[16 + c].rearrange("(dc p) e -> p dc e", p=128)),
                    waits=wts, label=f"wk.{c}")
                DQ["d_wv"].op(lambda e, c=c: e.dma_start(
                    out=b.T1("wkv16")[:, D:2*D].rearrange("p (dc e) -> p dc e", e=128),
                    in_=b.WinT[32 + c].rearrange("(dc p) e -> p dc e", p=128)),
                    waits=([(eT, f"p1.kmm.{c-1}"), (eT, f"p1.vmm.{c-1}")]
                           if c > 0 else ()), label=f"wv.{c}")

        def pe(E):
            E.wait(DQ["d_xt"], "xt")
            for c in range(16):
                G1, G2 = PS[c % 2], PS[(c + 1) % 2]
                wts = [(DQ["d_wk"], f"wk.{c}")]
                if c == 0:
                    wts.append((eA, "p0.rampB"))
                if c >= 1:
                    wts.append((eA, f"p1.vev.{c-1}.1"))
                if c >= 2 and (c - 2) < 8:
                    wts.append((eV, f"p1.rope.{c-2}.1"))
                for dch in range(NCH):
                    for tc in range(TCOL):
                        last = dch == NCH - 1 and tc == TCOL - 1
                        eT.op(lambda e, G1=G1, dch=dch, tc=tc: e.matmul(
                            G1[:, tc*512:(tc+1)*512],
                            b.T1("wkv16")[:, dch*128:(dch+1)*128],
                            b.arena[:, dch*T + tc*512: dch*T + (tc+1)*512],
                            start=(dch == 0), stop=(dch == NCH - 1)),
                            waits=(wts if (dch == 0 and tc == 0) else ()),
                            label=(f"p1.kmm.{c}" if last else None))
                wts = [(DQ["d_wv"], f"wv.{c}")]
                if c >= 1:
                    wts.append((eA, f"p1.kev.{c-1}.1"))
                if c >= 1 and (c - 1) < 8:
                    wts.append((eV, f"p1.rope.{c-1}.1"))
                for dch in range(NCH):
                    for tc in range(TCOL):
                        last = dch == NCH - 1 and tc == TCOL - 1
                        eT.op(lambda e, G2=G2, dch=dch, tc=tc: e.matmul(
                            G2[:, tc*512:(tc+1)*512],
                            b.T1("wkv16")[:, D + dch*128:D + (dch+1)*128],
                            b.arena[:, dch*T + tc*512: dch*T + (tc+1)*512],
                            start=(dch == 0), stop=(dch == NCH - 1)),
                            waits=(wts if (dch == 0 and tc == 0) else ()),
                            label=(f"p1.vmm.{c}" if last else None))
                if c < 8:
                    for tc in range(TCOL):
                        eT.op(lambda e, G1=G1, tc=tc: e.matmul(
                            G1[:, tc*512:(tc+1)*512], b.perm_t[:],
                            b.T1("gA")[:, tc*512:(tc+1)*512], start=True,
                            stop=True),
                            waits=([(eA, f"p1.kev.{c}.1")] if tc == 0 else ()),
                            label=(f"p1.perm.{c}" if tc == TCOL - 1 else None))

        def act(E):
            def emit_sq(c, ih):
                wts = ([(eV, f"p1.rope.{c}.{ih}")] if c < 8
                       else [(eA, f"p1.kev.{c}.{ih}")])
                if ih == 1 and f"hc{c}" in b.taps:
                    wts.append((b.DQ["d_tap"], f"tap.hc{c}"))
                if c == 0 and ih == 0:
                    wts.append((eV, "p0.red.47"))
                elif ih == 0:
                    wts.append((eV, f"p1.nsq.{c-1}.1"))
                else:
                    wts.append((eV, f"p1.nsq.{c}.0"))
                eA.op(lambda e: e.activation(
                    b.T1("hC")[:], b.T1("gA")[:, hsl(ih)], AF.Square),
                    waits=wts, label=f"p1.sq.{c}.{ih}")

            for c in range(16):
                if c < 8:
                    self._tables_act("p1", c, 0)
                for ih in range(2):
                    wts = [(eT, f"p1.kmm.{c}")]
                    if c >= 1:
                        wts.append((eV, f"p1.kv.{c-1}.1"))
                    if c == 1 and "rope0" in b.taps:
                        wts.append((b.DQ["d_tap"], "tap.rope0"))
                    if c == 4 and "kr3" in b.taps:
                        wts.append((b.DQ["d_tap"], "tap.kr3"))
                    eA.op(lambda e, c=c, ih=ih: e.activation(
                        b.T1("gA")[:, hsl(ih)], PS[c % 2][:, hsl(ih)], AF.Identity,
                        bias=b.bin_t[:, 16+c:17+c], scale=1.0),
                        waits=wts, label=f"p1.kev.{c}.{ih}")
                for ih in range(2):
                    wts = [(eT, f"p1.vmm.{c}")]
                    if c >= 1:
                        wts.append((eV, f"p1.kv.{c-1}.1"))
                    if c == 1 and ih == 0 and "v0" in b.taps:
                        wts.append((b.DQ["d_tap"], "tap.v0"))
                    eA.op(lambda e, c=c, ih=ih: e.activation(
                        b.T1("gB")[:, hsl(ih)], PS[(c+1) % 2][:, hsl(ih)],
                        AF.Identity, bias=b.bin_t[:, 32+c:33+c], scale=1.0),
                        waits=wts, label=f"p1.vev.{c}.{ih}")
                emit_sq(c, 0)
                if c < 8:
                    self._tables_act("p1", c, 1)
                emit_sq(c, 1)

        def vec(E):
            def emit_nsq(c, ih):
                wts = [(eA, f"p1.sq.{c}.{ih}")]
                if c == 0 and ih == 0 and "sq00" in b.taps:
                    wts.append((b.DQ["d_tap"], "tap.sq00"))
                if c == 1 and ih == 0 and "nsq0" in b.taps:
                    wts.append((b.DQ["d_tap"], "tap.nsq0"))
                if ih == 0 and f"hc{c}" in b.taps:
                    wts.append((b.DQ["d_tap"], f"tap.hc{c}"))
                if c >= 1 and ih == 0 and f"gc{c-1}" in b.taps:
                    wts.append((b.DQ["d_tap"], f"tap.gc{c-1}"))
                if c == 0:
                    if ih == 1:
                        wts.append((eV, "p1.nsq.0.0"))
                    eV.op(lambda e, ih=ih: e.tensor_copy(
                        b.T1("gC")[:, hsl(ih)], b.T1("hC")[:]),
                        waits=wts, label=f"p1.nsq.0.{ih}")
                else:
                    wts.append((eV, f"p1.nsq.{c-1}.1" if ih == 0
                                else f"p1.nsq.{c}.0"))
                    eV.op(lambda e, ih=ih: e.tensor_tensor(
                        b.T1("gC")[:, hsl(ih)], b.T1("gC")[:, hsl(ih)],
                        b.T1("hC")[:], ALU.add),
                        waits=wts, label=f"p1.nsq.{c}.{ih}")

            def emit_kv(c, ih):
                wts = [(eA, f"p1.vev.{c}.{ih}"), (eA, f"p1.sq.{c}.{ih}")]
                eV.op(lambda e, c=c, ih=ih: e.tensor_tensor(
                    b.bigbuf[:, c*T + ih*HF: c*T + (ih+1)*HF],
                    b.T1("gA")[:, hsl(ih)], b.T1("gB")[:, hsl(ih)], ALU.mult),
                    waits=wts, label=f"p1.kv.{c}.{ih}")

            for c in range(16):
                if c < 8:
                    if c == 0:
                        extra0 = [(eV, "p0.red.47"), (eA, "p0.abs.47")]
                        sq_guard = None
                    else:
                        extra0 = [(eV, f"p1.rope.{c-1}.1"),
                                  (eA, f"p1.cos.{c-1}.1")]
                        sq_guard = (eA, f"p1.sq.{c-1}.1")
                    self._theta_chain("p1", c, extra0, sq_guard, 0)
                    self._rope_combine("p1", c, PS[c % 2], 0)
                    emit_nsq(c, 0)
                    emit_kv(c, 0)
                    self._theta_chain("p1", c, extra0, sq_guard, 1)
                    self._rope_combine("p1", c, PS[c % 2], 1)
                    emit_nsq(c, 1)
                    emit_kv(c, 1)
                else:
                    for ih in range(2):
                        emit_nsq(c, ih)
                        emit_kv(c, ih)

        return {"gpsimd": gp, "tensor": pe, "scalar": act, "vector": vec,
                "sync": sync}

    def ph_p1b(self):
        b = self
        eT, eV, eA, eP, DQ, PS = b.eT, b.eV, b.eA, b.eP, b.DQ, b.PS

        def pe(E):
            for tc in range(TCOL):
                eT.op(lambda e, tc=tc: e.matmul(
                    b.psA[0:1, tc*512:(tc+1)*512], b.onesC_t[:],
                    b.T1("gC")[:, tc*512:(tc+1)*512], start=True, stop=True),
                    waits=([(eV, "p1.nsq.15.1"), (eA, "p1.vev.15.1"),
                            (eV, "p1.kv.15.1")] if tc == 0 else ()),
                    label=("p1b.nsqmm" if tc == TCOL - 1 else None))
            eT.op(lambda e: e.matmul(b.psB[0:1, 0:4], b.onesC_t[:], b.gacc[:, 0:4],
                                     start=True, stop=True),
                  waits=[(eV, "p0.acc.47"), (eA, "p1.kev.15.1")], label="p1b.gmm")
            for tc in range(TCOL):
                eT.op(lambda e, tc=tc: e.matmul(
                    b.psB[:, tc*512:(tc+1)*512], b.onesR_t[:],
                    b.nrowR[0:1, tc*512:(tc+1)*512], start=True, stop=True),
                    waits=([(eV, "p1b.srow"), (eA, "p1b.gvac")] if tc == 0 else ()),
                    label=("p1b.sbmm" if tc == TCOL - 1 else None))

        def act(E):
            eA.op(lambda e: e.activation(b.nrowR, b.psA[0:1, :], AF.Sqrt),
                  waits=[(eT, "p1b.nsqmm")], label="p1b.nsqvac")
            eA.op(lambda e: e.activation(b.gcells[:], b.psB[0:1, 0:4], AF.Identity),
                  waits=[(eT, "p1b.gmm")], label="p1b.gvac")
            eA.op(lambda e: e.activation(b.T1("gA")[:], b.psB[:], AF.Identity),
                  waits=[(eT, "p1b.sbmm"), (eV, "p1.kv.15.1")], label="p1b.sB")

        def vec(E):
            eV.op(lambda e: e.tensor_scalar(b.nrowR, b.nrowR, 1e-12, None, ALU.max),
                  waits=[(eA, "p1b.nsqvac")], label="p1b.nmax")
            eV.op(lambda e: e.reciprocal(b.nrowR, b.nrowR),
                  waits=[(eV, "p1b.nmax")], label="p1b.srow")
            for c in range(16):
                for ih in range(2):
                    wts = [(eP, f"p1b.kvs.{c}.{ih}")]
                    if ih == 0 and c >= 1:
                        wts.append((eV, f"p1b.acol.{c-1}"))
                    eV.op(lambda e, ih=ih: e.tensor_reduce(
                        b.t16b[:, ih:ih+1], b.T1("hA")[:], AX.X, ALU.add),
                        waits=wts, label=f"p1b.red.{c}.{ih}")
                eV.op(lambda e, c=c: e.tensor_tensor(
                    b.acol[:, c:c+1], b.t16b[:, 0:1], b.t16b[:, 1:2], ALU.add),
                    waits=[(eV, f"p1b.red.{c}.1")], label=f"p1b.acol.{c}")
            eV.op(lambda e: e.tensor_scalar(b.acolB[:], b.acol[:],
                                            b.bsel_t[:, 1:2], None, ALU.mult),
                  waits=[(eV, "p1b.acol.15")], label="p1b.a1")
            eV.op(lambda e: e.tensor_scalar(b.acol2[:], b.acol[:],
                                            b.bsel_t[:, 0:1], None, ALU.mult),
                  waits=[(eV, "p1b.a1")], label="p1b.a0")

        def gp(E):
            for c in range(16):
                for ih in range(2):
                    wts = [(eA, "p1b.sB"), (eV, f"p1.kv.{c}.{ih}")]
                    if c == 0 and ih == 0:
                        wts += [(eV, "p1.rope.7.1"), (eA, "p1.cos.7.1")]
                    elif ih == 0:
                        wts.append((eV, f"p1b.red.{c-1}.1"))
                    else:
                        wts.append((eV, f"p1b.red.{c}.0"))
                    eP.op(lambda e, c=c, ih=ih: e.tensor_tensor(
                        b.T1("hA")[:], b.bigbuf[:, c*T + ih*HF: c*T + (ih+1)*HF],
                        b.T1("gA")[:, hsl(ih)], ALU.mult),
                        waits=wts, label=f"p1b.kvs.{c}.{ih}")

        def sync(E):
            DQ["d_cc"].op(lambda e: e.dma_start(
                out=b.ccin[0:1, :].rearrange("r (p f) -> (r p) f", p=128),
                in_=b.acol2[:]), waits=[(eV, "p1b.a0")], label="cc.a0")
            DQ["d_cc"].op(lambda e: e.dma_start(
                out=b.ccin[1:2, :].rearrange("r (p f) -> (r p) f", p=128),
                in_=b.acolB[:]), waits=[(eV, "p1b.a1")], label="cc.a1")
            DQ["d_cc"].op(lambda e: e.dma_start(out=b.ccin[2:3, 0:4],
                                                in_=b.gcells[:]),
                          waits=[(eA, "p1b.gvac")], label="cc.g")
            DQ["d_cc"].op(lambda e: e.dma_start(out=b.ccin[2:3, 4:T],
                                                in_=b.zrowD[0:1, 4:T]),
                          label="cc.gz")
            DQ["d_cc"].op(lambda e: e.dma_start(out=b.ccin[3:4, :], in_=b.zrowD),
                          label="cc.z")
            if "nsq" in b.taps:
                nw = [(eV, "p1.nsq.15.1")]
                for prev in ([f"gc{n}" for n in range(15, -1, -1)]
                             + [f"hc{n}" for n in range(15, -1, -1)]
                             + ["nsq0", "sq00", "rope0", "v0", "k0", "arena"]):
                    if prev in b.taps:
                        nw.append((DQ["d_tap"], f"tap.{prev}"))
                        break
                DQ["d_tap"].op(lambda e: e.dma_start(out=b.tap_outs["nsq"],
                                                     in_=b.T1("gC")[:]),
                               waits=nw, label="tap.nsq")
            if "srow" in b.taps:
                sw = [(eV, "p1b.srow")]
                if "nsq" in b.taps:
                    sw.append((DQ["d_tap"], "tap.nsq"))
                DQ["d_tap"].op(lambda e: e.dma_start(out=b.tap_outs["srow"],
                                                     in_=b.nrowR),
                               waits=sw, label="tap.srow")
            if "kv" in b.taps:
                wts = [(eV, "p1.kv.15.1")]
                if "srow" in b.taps:
                    wts.append((DQ["d_tap"], "tap.srow"))
                DQ["d_tap"].op(lambda e: e.dma_start(out=b.tap_outs["kv"],
                                                     in_=b.bigbuf[:]),
                               waits=wts, label="tap.kv")

        return {"tensor": pe, "scalar": act, "vector": vec, "gpsimd": gp,
                "sync": sync}

    def ph_pcc(self):
        b = self
        eT, eV, eA, eP, DQ = b.eT, b.eV, b.eA, b.eP, b.DQ

        def gp(E):
            E.wait_val(DQ["d_cc"], 16 * 5)
            eP.op(lambda e: e.collective_compute(
                "AllReduce", ALU.add, [list(range(8))], [b.ccin.opt()],
                [b.ccout.opt()]), label="cc.done")

        def sync(E):
            E.wait(eP, "cc.done")
            DQ["d_ccR"].op(lambda e: e.dma_start(
                out=b.acol2[:],
                in_=b.ccout[0:1, :].rearrange("r (p f) -> (r p) f", p=128)),
                waits=[(eV, "p1b.a0")], label="ccR.a0")
            DQ["d_ccR"].op(lambda e: e.dma_start(
                out=b.acolB[:],
                in_=b.ccout[1:2, :].rearrange("r (p f) -> (r p) f", p=128)),
                label="ccR.a1")
            DQ["d_ccR"].op(lambda e: e.dma_start(out=b.gcells[:],
                                                 in_=b.ccout[2:3, 0:4]),
                           waits=[(DQ["d_ccR"], "ccR.a1")], label="ccR.g")

        def vec(E):
            eV.op(lambda e: e.tensor_scalar(b.acol2[:], b.acol2[:],
                                            b.bsel_t[:, 0:1], None, ALU.mult),
                  waits=[(DQ["d_ccR"], "ccR.a1")],
                  label="cc.s0")
            eV.op(lambda e: e.tensor_scalar(b.acolB[:], b.acolB[:],
                                            b.bsel_t[:, 1:2], None, ALU.mult),
                  waits=[(eV, "cc.s0")], label="cc.s1")
            eV.op(lambda e: e.tensor_tensor(b.acol2[:], b.acol2[:], b.acolB[:],
                                            ALU.add),
                  waits=[(eV, "cc.s1")], label="cc.asel")
            eV.op(lambda e: e.tensor_scalar(b.gvals[0:1, 0:3], b.gcells[0:1, 0:3],
                                            1.0 / NW, 1e-5, ALU.mult, ALU.max),
                  waits=[(DQ["d_ccR"], "ccR.g")], label="cc.g3")
            eV.op(lambda e: e.reciprocal(b.gvals[0:1, 3:6], b.gvals[0:1, 0:3]),
                  waits=[(eV, "cc.g3")], label="cc.ginv")
            eV.op(lambda e: e.memset(b.gvals[0:1, 6:8], 0.0), label="cc.gpad")

        def pe(E):
            eT.op(lambda e: e.matmul(b.psB[:, 512:520], b.onesR_t[:],
                                     b.gvals[0:1, 0:8], start=True, stop=True),
                  waits=[(eV, "cc.ginv"), (eV, "cc.gpad"), (eA, "p1b.sB")],
                  label="cc.gbmm")

        def act(E):
            eA.op(lambda e: e.activation(b.acol2[:], b.acol2[:], AF.Identity),
                  waits=[(eV, "cc.asel")], label="cc.acol2")
            eA.op(lambda e: e.activation(b.gB8[:], b.psB[:, 512:520], AF.Identity),
                  waits=[(eT, "cc.gbmm")], label="cc.gB8")

        return {"gpsimd": gp, "sync": sync, "vector": vec, "tensor": pe,
                "scalar": act}

    def ph_p2(self):
        b = self
        eT, eV, eA, eP, DQ, PS = b.eT, b.eV, b.eA, b.eP, b.DQ, b.PS

        def gp(E):
            def emit_owqw(j2):
                j, ihp = j2 // 2, j2 % 2
                fc, j4 = j // 4, j % 4
                buf = ["gB", "gC"][j % 2]
                wts = [(eV, f"p2.owq.{j2}")]
                if j2 >= 1:
                    wts.append((DQ["d_owqW"], f"owqW.{j2-1}"))
                DQ["d_owqW"].op(lambda e, fc=fc, j4=j4, ihp=ihp, bu=buf:
                                e.dma_start(
                    out=b.owq_cache[fc][:, j4*2048 + ihp*HF:
                                        j4*2048 + (ihp+1)*HF],
                    in_=b.T1(bu)[:, hsl(ihp)]),
                    waits=wts, label=f"owqW.{j2}")

            for c in range(16):
                par = c % 2
                wts = [(eT, f"p2.qmm.{c-2}")] if c >= 2 else [
                    (eT, "p1.kmm.15"), (eT, "p1.vmm.15")]
                DQ[f"d_wq{par}"].op(lambda e, c=c, par=par: e.dma_start(
                    out=b.T1("wkv16")[:, par*D:(par+1)*D]
                    .rearrange("p (dc e) -> p dc e", e=128),
                    in_=b.WinT[c].rearrange("(dc p) e -> p dc e", p=128)),
                    waits=wts, label=f"wq.{c}")
                if c >= 1:
                    for j2 in range(8 * (c - 1), 8 * c):
                        emit_owqw(j2)
            for j2 in range(8 * 15, 128):
                emit_owqw(j2)

        def sync(E):
            prevR = [None]

            def emit_tabR(c, ih):
                for sc, buf, lab, rdlab in ((0, "hE", "tabRs", "krs"),
                                            (1, "hD", "tabRc", "kcs")):
                    wts = []
                    if prevR[0] is not None:
                        wts.append((DQ["d_tabR"], prevR[0]))
                    if c == 0 and ih == 0:
                        if sc == 0:
                            wts += [(DQ["d_tabW"], "tabWc.7.1"),
                                    (eV, "p1.rope.7.1")]
                    else:
                        pc, pih = (c, 0) if ih == 1 else (c - 1, 1)
                        wts.append((eV, f"p2.{rdlab}.{pc}.{pih}"))
                    DQ["d_tabR"].op(lambda e, c=c, ih=ih, sc=sc, buf=buf:
                                    e.dma_start(
                        out=b.T1(buf)[:], in_=b.tab_sp[c, ih, sc]),
                        waits=wts, label=f"{lab}.{c}.{ih}")
                    prevR[0] = f"{lab}.{c}.{ih}"

            for j2 in range(128):
                j, ihp = j2 // 2, j2 % 2
                par = j % 2
                buf = ["gB", "gC"][par]
                if j2 % 8 == 0 and j2 // 8 < 8:
                    emit_tabR(j2 // 8, 0)
                    emit_tabR(j2 // 8, 1)
                if j2 < 4:
                    if par == 0:
                        wts = [(eV, "p1.kv.15.1")]
                    else:
                        wts = [(eT, "p1b.nsqmm")]
                        if "nsq" in b.taps:
                            wts.append((DQ["d_tap"], "tap.nsq"))
                else:
                    wts = [(DQ["d_owqW"], f"owqW.{j2-4}")]
                DQ[f"d_owp{j2 % 4}"].op(lambda e, j=j, ihp=ihp, bu=buf:
                                        e.dma_start(
                    out=b.T1(bu)[:, hsl(ihp)]
                    .rearrange("p (dc f) -> p dc f", f=128),
                    in_=b.OwT[j][ihp*HF:(ihp+1)*HF, :]
                    .rearrange("(dc p) f -> p dc f", p=128)),
                    waits=wts, label=f"owp.{j2}")


        def pe(E):
            for c in range(16):
                par = c % 2
                G1 = PS[c % 2]
                wts = [(DQ[f"d_wq{par}"], f"wq.{c}")]
                if c == 0:
                    wts += [(eP, "p1b.kvs.15.1"), (eA, "p1b.nsqvac")]
                if c == 1:
                    wts += [(eA, "p1b.sB"), (eA, "cc.gB8")]
                if c >= 2:
                    wts.append((eA, f"p2.qev.{c-2}.1"))
                if c >= 2 and (c - 2) < 8:
                    wts.append((eV, f"p2.rope.{c-2}.1"))
                for dch in range(NCH):
                    for tc in range(TCOL):
                        last = dch == NCH - 1 and tc == TCOL - 1
                        eT.op(lambda e, G1=G1, dch=dch, tc=tc, par=par: e.matmul(
                            G1[:, tc*512:(tc+1)*512],
                            b.T1("wkv16")[:, par*D + dch*128: par*D + (dch+1)*128],
                            b.arena[:, dch*T + tc*512: dch*T + (tc+1)*512],
                            start=(dch == 0), stop=(dch == NCH - 1)),
                            waits=(wts if (dch == 0 and tc == 0) else ()),
                            label=(f"p2.qmm.{c}" if last else None))
                if c < 8:
                    for tc in range(TCOL):
                        eT.op(lambda e, G1=G1, tc=tc: e.matmul(
                            G1[:, tc*512:(tc+1)*512], b.perm_t[:],
                            b.T1("gA")[:, tc*512:(tc+1)*512], start=True,
                            stop=True),
                            waits=([(eA, f"p2.qev.{c}.1")] if tc == 0 else ()),
                            label=(f"p2.perm.{c}" if tc == TCOL - 1 else None))

        def act(E):
            for c in range(16):
                for ih in range(2):
                    wts = [(eT, f"p2.qmm.{c}")]
                    if c >= 1:
                        wts.append((eV, f"p2.y.{c-1}.1"))
                    eA.op(lambda e, c=c, ih=ih: e.activation(
                        b.T1("gA")[:, hsl(ih)], PS[c % 2][:, hsl(ih)], AF.Identity,
                        bias=b.bin_t[:, c:c+1], scale=1.0),
                        waits=wts, label=f"p2.qev.{c}.{ih}")
                for jj in range(4):
                    j = 4 * c + jj
                    par = j % 2
                    buf = ["gB", "gC"][par]
                    for ihp in range(2):
                        j2 = 2 * j + ihp
                        sl2 = hsl(ihp)
                        wts = [(DQ[f"d_owp{j2 % 4}"], f"owp.{j2}"),
                               (eA, "cc.gB8"),
                               (eV, "p0.rndp"), (eV, "p0.rndn")]
                        eA.op(lambda e, bu=buf, sl2=sl2: e.activation(
                            b.T1(bu)[:, sl2], b.T1(bu)[:, sl2], AF.Identity,
                            bias=b.rndP[:], scale=b.gB8[:, 5:6]),
                            waits=wts, label=f"p2.owr1.{j2}")

        def combine_ld(c, ih, G):
            s = hsl(ih)
            wts = [(eT, f"p2.perm.{c}"), (DQ["d_tabR"], f"tabRs.{c}.{ih}")]
            if ih == 1:
                wts.append((eV, f"p2.rope.{c}.0"))
            elif c >= 1:
                wts.append((eV, f"p2.rope.{c-1}.1"))
            else:
                wts += [(eV, "p1.rope.7.1"), (eP, "p1b.kvs.15.1")]
            eV.op(lambda e, s=s, G=G: e.tensor_tensor(
                b.T1("hB")[:], G[:, s], b.T1("hE")[:], ALU.mult),
                waits=wts, label=f"p2.krs.{c}.{ih}")
            eV.op(lambda e, s=s: e.tensor_tensor(
                b.T1("gA")[:, s], b.T1("gA")[:, s], b.T1("hD")[:], ALU.mult),
                waits=[(eV, f"p2.krs.{c}.{ih}"),
                       (DQ["d_tabR"], f"tabRc.{c}.{ih}")],
                label=f"p2.kcs.{c}.{ih}")
            eV.op(lambda e, s=s: e.tensor_tensor(
                b.T1("gA")[:, s], b.T1("gA")[:, s], b.T1("hB")[:], ALU.add),
                waits=[(eV, f"p2.kcs.{c}.{ih}")], label=f"p2.rope.{c}.{ih}")

        def vec(E):
            for c in range(16):
                if c < 8:
                    combine_ld(c, 0, PS[c % 2])
                    combine_ld(c, 1, PS[c % 2])
                for ih in range(2):
                    wts = ([(eV, f"p2.rope.{c}.{ih}")] if c < 8
                           else [(eA, f"p2.qev.{c}.{ih}")])
                    wts.append((eA, "cc.acol2"))
                    if c == 0 and ih == 0:
                        wts.append((eP, "p1b.kvs.15.1"))
                        if "kv" in b.taps:
                            wts.append((DQ["d_tap"], "tap.kv"))
                    eV.op(lambda e, c=c, ih=ih: e.tensor_scalar(
                        b.bigbuf[:, c*T + ih*HF: c*T + (ih+1)*HF],
                        b.T1("gA")[:, hsl(ih)], b.acol2[:, c:c+1], None, ALU.mult),
                        waits=wts, label=f"p2.y.{c}.{ih}")
                for jj in range(4):
                    j = 4 * c + jj
                    par = j % 2
                    buf = ["gB", "gC"][par]
                    for ihp in range(2):
                        j2 = 2 * j + ihp
                        sl2 = hsl(ihp)
                        eV.op(lambda e, bu=buf, sl2=sl2: e.tensor_scalar(
                            b.T1(bu)[:, sl2], b.T1(bu)[:, sl2], -RND, 1.0,
                            ALU.add, ALU.min),
                            waits=[(eA, f"p2.owr1.{j2}")],
                            label=f"p2.owr2.{j2}")
                        eV.op(lambda e, bu=buf, sl2=sl2: e.tensor_scalar(
                            b.T1(bu)[:, sl2], b.T1(bu)[:, sl2], -1.0, None,
                            ALU.max),
                            waits=[(eV, f"p2.owr2.{j2}")],
                            label=f"p2.owq.{j2}")

        return {"gpsimd": gp, "sync": sync, "tensor": pe,
                "scalar": act, "vector": vec}

    def ph_ao(self):
        b = self
        eT, eV, eA, eP, DQ, PS = b.eT, b.eV, b.eA, b.eP, b.DQ, b.PS

        def gp(E):
            for fc in range(16):
                par = fc % 2
                wts = ([(eT, f"ao.mm.{fc-2}")] if fc >= 2
                       else [(eT, f"p2.qmm.{14 + fc}")])
                DQ[f"d_wo{par}"].op(lambda e, fc=fc, par=par: e.dma_start(
                    out=b.T1("wkv16")[:, par*D:(par+1)*D]
                    .rearrange("p (dc e) -> p dc e", e=128),
                    in_=b.WoT[fc].rearrange("(dc p) e -> p dc e", p=128)),
                    waits=wts, label=f"wo.{fc}")

        def sync(E):
            for k in range(32):
                fc, ih = k // 2, k % 2
                par = k % 2
                buf = ["hC", "hD"][par]
                wts = [(eV, f"ao.x2pre.{k-2}")] if k >= 2 else [
                    (eV, "p2.y.15.1")]
                DQ[f"d_xres{par}"].op(lambda e, fc=fc, ih=ih, bu=buf: e.dma_start(
                    out=b.T1(bu)[:],
                    in_=b.XT[fc*128:(fc+1)*128, ih*HF:(ih+1)*HF]),
                    waits=wts, label=f"xres.{k}")

        def pe(E):
            for fc in range(16):
                par = fc % 2
                G1 = PS[fc % 2]
                wts = [(DQ[f"d_wo{par}"], f"wo.{fc}")]
                if fc == 0:
                    wts += [(eV, "p2.y.15.1"), (eV, "p2.rope.7.1"),
                            (eA, "p2.qev.15.1")]
                if fc == 1:
                    wts += [(eA, "p2.qev.15.1")]
                if fc >= 2:
                    wts.append((eV, f"ao.x2pre.{2*(fc-2)+1}"))
                for dch in range(NCH):
                    for tc in range(TCOL):
                        last = dch == NCH - 1 and tc == TCOL - 1
                        eT.op(lambda e, G1=G1, dch=dch, tc=tc, par=par: e.matmul(
                            G1[:, tc*512:(tc+1)*512],
                            b.T1("wkv16")[:, par*D + dch*128: par*D + (dch+1)*128],
                            b.bigbuf[:, dch*T + tc*512: dch*T + (tc+1)*512],
                            start=(dch == 0), stop=(dch == NCH - 1)),
                            waits=(wts if (dch == 0 and tc == 0) else ()),
                            label=(f"ao.mm.{fc}" if last else None))

        def act(E):
            for k in range(32):
                fc, ih = k // 2, k % 2
                par = k % 2
                wts = [(eV, f"ao.x2pre.{k}")]
                if k >= 1:
                    wts.append((eV, f"ao.ssq.{k-1}"))
                eA.op(lambda e, p=par: e.activation(b.T1("hE")[:],
                                                    b.T1(["hA", "hB"][p])[:],
                                                    AF.Square),
                      waits=wts, label=f"ao.sq.{k}")
                DQ[f"d_x2preW{par}"].op(lambda e, fc=fc, ih=ih, par=par: e.dma_start(
                    out=b.x2pre_sp[fc*128:(fc+1)*128, ih*HF:(ih+1)*HF],
                    in_=b.T1(["hA", "hB"][par])[:]),
                    waits=[(eV, f"ao.ssum.{k}"), (eV, f"ao.mx.{k}"),
                           (eV, f"ao.mn.{k}")],
                    label=f"x2preW.{k}")

        def vec(E):
            for k in range(32):
                fc, ih = k // 2, k % 2
                par = k % 2
                xp = ["hA", "hB"][par]
                xr = ["hC", "hD"][par]
                wts = [(eT, f"ao.mm.{fc}"), (DQ[f"d_xres{par}"], f"xres.{k}")]
                if k >= 2:
                    wts.append((DQ[f"d_x2preW{par}"], f"x2preW.{k-2}"))
                if k == 0:
                    wts += [(eV, "p2.y.15.1"), (eV, "p2.owq.127")]
                if k == 1:
                    wts.append((eA, "p1.cos.7.1"))
                eV.op(lambda e, xp=xp, xr=xr, fc=fc, ih=ih: e.scalar_tensor_tensor(
                    b.T1(xp)[:], PS[fc % 2][:, hsl(ih)], b.bout_t[:, fc:fc+1],
                    b.T1(xr)[:], ALU.add, ALU.add),
                    waits=wts, label=f"ao.x2pre0.{k}")
                eV.op(lambda e, xp=xp, xr=xr: e.tensor_tensor(
                    b.T1(xp)[:], b.T1(xp)[:], b.T1(xr)[:], ALU.add),
                    waits=[(eV, f"ao.x2pre0.{k}")], label=f"ao.x2pre.{k}")
                sgm = hsl(ih)
                if k <= 1:
                    xb = ["hA", "hB"][par]
                    pre = [(eV, f"ao.x2pre.{k}")]
                    prev = ([] if k == 0 else [(eV, f"ao.ssum.{k-1}")])
                    init_wts = {
                        "ssum": pre + prev + ([(eV, "p2.y.15.1")] if k == 0 else []),
                        "mx": pre + ([(eV, "p2.owq.127"),
                                      (DQ["d_owqW"], "owqW.127")] if k == 0
                                     else [(eV, "ao.mx.0")]),
                        "mn": pre + ([(eV, "p1.th0.7.1")] if k == 0
                                     else [(eV, "ao.mn.0")]),
                    }
                    eV.op(lambda e, xb=xb, sgm=sgm: e.tensor_copy(
                        b.T1("gA")[:, sgm], b.T1(xb)[:]),
                        waits=init_wts["ssum"], label=f"ao.ssum.{k}")
                    eV.op(lambda e, xb=xb, sgm=sgm: e.tensor_copy(
                        b.T1("gC")[:, sgm], b.T1(xb)[:]),
                        waits=init_wts["mx"], label=f"ao.mx.{k}")
                    eV.op(lambda e, xb=xb, sgm=sgm: e.tensor_copy(
                        b.T1("gD")[:, sgm], b.T1(xb)[:]),
                        waits=init_wts["mn"], label=f"ao.mn.{k}")
                    sq_wts = [(eA, f"ao.sq.{k}")]
                    if k == 0:
                        sq_wts += [(eV, "p2.owq.126"),
                                   (DQ["d_owqW"], "owqW.127")]
                    else:
                        sq_wts.append((eV, "ao.ssq.0"))
                    eV.op(lambda e, sgm=sgm: e.tensor_copy(
                        b.T1("gB")[:, sgm], b.T1("hE")[:]),
                        waits=sq_wts, label=f"ao.ssq.{k}")
                else:
                    eV.op(lambda e, xp=xp, sgm=sgm: e.tensor_tensor(
                        b.T1("gA")[:, sgm], b.T1("gA")[:, sgm], b.T1(xp)[:],
                        ALU.add),
                        waits=[(eV, f"ao.x2pre.{k}"), (eV, f"ao.ssum.{k-1}")],
                        label=f"ao.ssum.{k}")
                    eV.op(lambda e, xp=xp, sgm=sgm: e.tensor_tensor(
                        b.T1("gC")[:, sgm], b.T1("gC")[:, sgm], b.T1(xp)[:],
                        ALU.max),
                        waits=[(eV, f"ao.mx.{k-1}")], label=f"ao.mx.{k}")
                    eV.op(lambda e, xp=xp, sgm=sgm: e.tensor_tensor(
                        b.T1("gD")[:, sgm], b.T1("gD")[:, sgm], b.T1(xp)[:],
                        ALU.min),
                        waits=[(eV, f"ao.mn.{k-1}")], label=f"ao.mn.{k}")
                    eV.op(lambda e, sgm=sgm: e.tensor_tensor(
                        b.T1("gB")[:, sgm], b.T1("gB")[:, sgm], b.T1("hE")[:],
                        ALU.add),
                        waits=[(eA, f"ao.sq.{k}"), (eV, f"ao.ssq.{k-1}")],
                        label=f"ao.ssq.{k}")

        return {"gpsimd": gp, "sync": sync, "tensor": pe,
                "scalar": act, "vector": vec}

    def ph_ln1(self):
        b = self
        eT, eV, eA, eP, DQ, PS = b.eT, b.eV, b.eA, b.eP, b.DQ, b.PS
        srcs = ["gA", "gB", "gC", "gD"]
        red_ops = [ALU.add, ALU.add, ALU.max, ALU.min]
        lastdeps = ["ao.ssum.31", "ao.ssq.31", "ao.mx.31", "ao.mn.31"]

        def pe(E):
            for k in range(64):
                i, j = k // 16, k % 16
                wts = [(eV, lastdeps[i])]
                if k == 0:
                    wts += [(eV, "ao.x2pre.31"), (eA, "ao.sq.31")]
                if k >= 2:
                    wts.append((eV, f"ln1.red.{k-2}"))
                eT.op(lambda e, i=i, j=j, k=k: e.transpose(
                    PS[k % 2][:, 0:128], b.T1(srcs[i])[:, j*128:(j+1)*128],
                    b.eye_t[:]),
                    waits=wts, label=f"ln1.tr.{k}")
            packs = [("mT16", "ln1.m"), ("rT16", "ln1.rstd"), ("sT16", "ln1.sx"),
                     ("siT16", "ln1.six")]
            for i, (tn, dep) in enumerate(packs):
                wts = [(eV, dep)]
                if i < 2:
                    wts.append((eV, f"ln1.red.{62+i}"))
                else:
                    wts.append((eA, f"ln1.rev.{i-2}"))
                eT.op(lambda e, tn=tn, i=i: e.transpose(
                    PS[i % 2][0:16, 128:256], getattr(b, tn)[:], b.eye_t[:]),
                    waits=wts, label=f"ln1.rtr.{i}")
            labs = ["mB", "rB", "sxB", "siB"]
            for i, lab in enumerate(labs):
                row = b.ln1rows[i % 2]
                for tc in range(TCOL):
                    wts = []
                    if tc == 0:
                        wts = [(DQ["d_bnc"], f"bncR.{i}"), (eA, "ln1.rev.3"),
                               (eV, f"ln1.red.{62 + (i % 2)}")]
                        if i >= 2:
                            wts.append((eA, f"ln1.ev.{labs[i-2]}"))
                    eT.op(lambda e, row=row, i=i, tc=tc: e.matmul(
                        PS[i % 2][:, tc*512:(tc+1)*512], b.onesR_t[:],
                        row[0:1, tc*512:(tc+1)*512], start=True, stop=True),
                        waits=wts,
                        label=(f"ln1.bc.{lab}" if tc == TCOL - 1 else None))

        def vec(E):
            dsts = [b.sT16, b.t16a, b.t16b, b.hT16]
            for k in range(64):
                i, j = k // 16, k % 16
                wts = [(eT, f"ln1.tr.{k}")]
                if k == 0:
                    wts.append((eV, "ao.mn.31"))
                eV.op(lambda e, i=i, j=j, k=k: e.tensor_reduce(
                    dsts[i][:, j:j+1], PS[k % 2][0:128, 0:128], AX.X, red_ops[i]),
                    waits=wts, label=f"ln1.red.{k}")
            eV.op(lambda e: e.tensor_scalar(b.mT16[:], b.sT16[:], 1.0 / D, None,
                                            ALU.mult),
                  waits=[(eV, "ln1.red.15")], label="ln1.m")
            eV.op(lambda e: e.tensor_scalar(b.t16a[:], b.t16a[:], 1.0 / D, None,
                                            ALU.mult),
                  waits=[(eV, "ln1.red.31")], label="ln1.ex2")
            eV.op(lambda e: e.tensor_tensor(b.dT16[:], b.mT16[:], b.mT16[:],
                                            ALU.mult),
                  waits=[(eV, "ln1.m")], label="ln1.msq")
            eV.op(lambda e: e.tensor_tensor(b.t16a[:], b.t16a[:], b.dT16[:],
                                            ALU.subtract),
                  waits=[(eV, "ln1.msq"), (eV, "ln1.ex2")], label="ln1.var")
            eV.op(lambda e: e.tensor_scalar(b.t16a[:], b.t16a[:], LN_EPS, None,
                                            ALU.add),
                  waits=[(eV, "ln1.var")], label="ln1.vare")
            eV.op(lambda e: e.reciprocal(b.rT16[:], b.t16a[:]),
                  waits=[(eA, "ln1.sqrt")], label="ln1.rstd")
            eV.op(lambda e: e.tensor_tensor(b.t16b[:], b.t16b[:], b.mT16[:],
                                            ALU.subtract),
                  waits=[(eV, "ln1.red.47"), (eV, "ln1.m")], label="ln1.g1")
            eV.op(lambda e: e.tensor_tensor(b.hT16[:], b.mT16[:], b.hT16[:],
                                            ALU.subtract),
                  waits=[(eV, "ln1.red.63")], label="ln1.g2")
            eV.op(lambda e: e.tensor_tensor(b.t16b[:], b.t16b[:], b.hT16[:],
                                            ALU.max),
                  waits=[(eV, "ln1.g1"), (eV, "ln1.g2")], label="ln1.g3")
            eV.op(lambda e: e.tensor_tensor(b.t16b[:], b.t16b[:], b.rT16[:],
                                            ALU.mult),
                  waits=[(eV, "ln1.g3"), (eV, "ln1.rstd")], label="ln1.g4")
            eV.op(lambda e: e.tensor_scalar(b.t16b[:], b.t16b[:], 1e-5, None,
                                            ALU.max),
                  waits=[(eV, "ln1.g4")], label="ln1.gc")
            eV.op(lambda e: e.reciprocal(b.sT16[:], b.t16b[:]),
                  waits=[(eV, "ln1.gc")], label="ln1.sx0")
            eV.op(lambda e: e.tensor_scalar(b.sT16[:], b.sT16[:], 127.0, None,
                                            ALU.mult),
                  waits=[(eV, "ln1.sx0")], label="ln1.sx")
            eV.op(lambda e: e.tensor_scalar(b.siT16[:], b.t16b[:], 1.0 / 127.0,
                                            None, ALU.mult),
                  waits=[(eV, "ln1.gc")], label="ln1.six")

        def act(E):
            eA.op(lambda e: e.activation(b.t16a[:], b.t16a[:], AF.Sqrt),
                  waits=[(eV, "ln1.vare")], label="ln1.sqrt")
            for i in range(4):
                wts = [(eT, f"ln1.rtr.{i}")]
                if i >= 1:
                    wts.append((DQ["d_bnc"], f"bncW.{i-1}"))
                eA.op(lambda e, i=i: e.activation(b.rbounce[:],
                                                  PS[i % 2][0:16, 128:256],
                                                  AF.Identity),
                      waits=wts, label=f"ln1.rev.{i}")
            for i, lab in enumerate(["mB", "rB", "sxB", "siB"]):
                eA.op(lambda e, i=i: e.activation(b.T1(srcs[i])[:], PS[i % 2][:],
                                                  AF.Identity),
                      waits=[(eT, f"ln1.bc.{lab}")], label=f"ln1.ev.{lab}")

        def sync(E):
            for i in range(4):
                row = b.ln1rows[i % 2]
                wts = [(eA, f"ln1.rev.{i}")]
                if i >= 1:
                    wts.append((DQ["d_bnc"], f"bncR.{i-1}"))
                DQ["d_bnc"].op(lambda e, i=i: e.dma_start(
                    out=b.bounce[i:i+1, :].rearrange("r (p f) -> (r p) f", p=16),
                    in_=b.rbounce[:]), waits=wts, label=f"bncW.{i}")
                wts = [(DQ["d_bnc"], f"bncW.{i}")]
                if i == 0:
                    wts.append((eT, "p1b.sbmm"))
                    if "srow" in b.taps:
                        wts.append((DQ["d_tap"], "tap.srow"))
                if i >= 2:
                    wts.append((eT, f"ln1.bc.{['mB', 'rB'][i-2]}"))
                DQ["d_bnc"].op(lambda e, i=i, row=row: e.dma_start(
                    out=row, in_=b.bounce[i:i+1, :]),
                    waits=wts, label=f"bncR.{i}")

        return {"tensor": pe, "vector": vec, "scalar": act, "sync": sync}

    def ph_ln1b(self):
        b = self
        eT, eV, eA, eP, DQ, PS = b.eT, b.eV, b.eA, b.eP, b.DQ, b.PS

        def sync(E):
            for k in range(32):
                fc, ih = k // 2, k % 2
                par = k % 2
                buf = ["hA", "hB"][par]
                wts = [(DQ["d_x2preW0"], "x2preW.30"),
                       (DQ["d_x2preW1"], "x2preW.31")]
                if k >= 2:
                    wts.append((eV, f"ln1b.s.{k-2}"))
                else:
                    wts.append((eV, "ao.x2pre.31"))
                DQ[f"d_x2preR{par}"].op(lambda e, fc=fc, ih=ih, bu=buf: e.dma_start(
                    out=b.T1(bu)[:],
                    in_=b.x2pre_sp[fc*128:(fc+1)*128, ih*HF:(ih+1)*HF]),
                    waits=wts, label=f"x2preR.{k}")

        def scalarD(E):
            for k in range(32):
                fc, ih = k // 2, k % 2
                par = k % 2
                DQ[f"d_x2W{par}"].op(lambda e, fc=fc, ih=ih, par=par: e.dma_start(
                    out=b.x2_sp[fc*128:(fc+1)*128, ih*HF:(ih+1)*HF],
                    in_=b.T1(["hC", "hD"][par])[:]),
                    waits=[(eV, f"ln1b.q1.{k}")], label=f"x2W.{k}")

        def vec(E):
            for k in range(32):
                fc, ih = k // 2, k % 2
                par = k % 2
                xp = ["hA", "hB"][par]
                xq = ["hC", "hD"][par]
                sgm = hsl(ih)
                wts = [(DQ[f"d_x2preR{par}"], f"x2preR.{k}"), (eA, "ln1.ev.mB"),
                       (eA, "ln1.ev.rB")]
                if k >= 2:
                    wts.append((DQ[f"d_x2W{par}"], f"x2W.{k-2}"))
                else:
                    wts.append((eV, "ao.x2pre.31"))
                eV.op(lambda e, xp=xp, xq=xq, sgm=sgm: e.tensor_tensor(
                    b.T1(xq)[:], b.T1(xp)[:], b.T1("gA")[:, sgm], ALU.subtract),
                    waits=wts, label=f"ln1b.s.{k}")
                eV.op(lambda e, xq=xq, sgm=sgm: e.tensor_tensor(
                    b.T1(xq)[:], b.T1(xq)[:], b.T1("gB")[:, sgm], ALU.mult),
                    waits=[(eV, f"ln1b.s.{k}")], label=f"ln1b.x2.{k}")
                wts = [(eV, f"ln1b.x2.{k}"), (eA, "ln1.ev.sxB")]
                if k >= 1:
                    wts.append((eV, f"ln1b.dq.{k-1}"))
                if k == 0:
                    wts += [(eT, "ao.mm.15"), (eA, "ao.sq.31")]
                eV.op(lambda e, xq=xq, sgm=sgm: e.tensor_tensor(
                    b.T1("hE")[:], b.T1(xq)[:], b.T1("gC")[:, sgm], ALU.mult),
                    waits=wts, label=f"ln1b.q1.{k}")
                eV.op(lambda e: e.tensor_scalar(b.T1("hE")[:], b.T1("hE")[:],
                                                RND, -RND, ALU.add, ALU.add),
                      waits=[(eV, f"ln1b.q1.{k}")], label=f"ln1b.rnd.{k}")
                eV.op(lambda e: e.tensor_scalar(b.T1("hE")[:], b.T1("hE")[:],
                                                127.0, -128.0, ALU.min, ALU.max),
                      waits=[(eV, f"ln1b.rnd.{k}")], label=f"ln1b.clip.{k}")
                eV.op(lambda e, fc=fc, ih=ih, sgm=sgm: e.tensor_tensor(
                    b.bigbuf[:, fc*T + ih*HF: fc*T + (ih+1)*HF], b.T1("hE")[:],
                    b.T1("gD")[:, sgm], ALU.mult),
                    waits=[(eV, f"ln1b.clip.{k}"), (eA, "ln1.ev.siB")],
                    label=f"ln1b.dq.{k}")

        return {"sync": sync, "scalarD": scalarD, "vector": vec}

    def ph_f1(self):
        b = self
        eT, eV, eA, eP, DQ, PS = b.eT, b.eV, b.eA, b.eP, b.DQ, b.PS

        def sync(E):
            for hc in range(HCH):
                for ih in range(2):
                    k = 2 * hc + ih
                    par = k % 2
                    gbuf = ["hA", "hB"][par]
                    vbuf = ["hC", "hD"][par]
                    wts = [(eV, f"f1.wqg.{k-2}")] if k >= 2 else [
                        (DQ[f"d_x2preR{par}"], f"x2preR.{30+par}"),
                        (eV, "ln1b.s.31")]
                    DQ[f"d_gw{par}"].op(lambda e, hc=hc, ih=ih, bu=gbuf: e.dma_start(
                        out=b.T1(bu)[:].rearrange("p (dc h) -> p dc h", h=128),
                        in_=b.GwT[hc][ih*HF:(ih+1)*HF, :]
                        .rearrange("(dc p) h -> p dc h", p=128)),
                        waits=wts, label=f"gw.{k}")
                    wts = [(eV, f"f1.wqv.{k-2}")] if k >= 2 else [
                        (DQ[f"d_x2W{par}"], f"x2W.{30+par}"),
                        (eV, "ln1b.x2.31")]
                    DQ[f"d_vw{par}"].op(lambda e, hc=hc, ih=ih, bu=vbuf: e.dma_start(
                        out=b.T1(bu)[:].rearrange("p (dc h) -> p dc h", h=128),
                        in_=b.VwT[hc][ih*HF:(ih+1)*HF, :]
                        .rearrange("(dc p) h -> p dc h", p=128)),
                        waits=wts, label=f"vw.{k}")

        def act(E):
            for hc in range(HCH):
                for ih in range(2):
                    k = 2 * hc + ih
                    par = k % 2
                    gbuf = ["hA", "hB"][par]
                    vbuf = ["hC", "hD"][par]
                    wts = [(DQ[f"d_gw{par}"], f"gw.{k}"), (eA, "cc.gB8"),
                           (eV, "p0.rndp"), (eV, "p0.rndn")]
                    if k >= 2:
                        wts.append((eV, f"f1.wqg.{k-2}"))
                    eA.op(lambda e, bu=gbuf: e.activation(
                        b.T1(bu)[:], b.T1(bu)[:], AF.Identity, bias=b.rndP[:],
                        scale=b.gB8[:, 3:4]),
                        waits=wts, label=f"f1.gr1.{k}")
                    eA.op(lambda e, bu=gbuf: e.activation(
                        b.T1(bu)[:], b.T1(bu)[:], AF.Identity, bias=b.rndN[:],
                        scale=1.0),
                        waits=[(eA, f"f1.gr1.{k}")], label=f"f1.gr2.{k}")
                    wts = [(DQ[f"d_vw{par}"], f"vw.{k}")]
                    if k >= 2:
                        wts.append((eV, f"f1.wqv.{k-2}"))
                    eA.op(lambda e, bu=vbuf: e.activation(
                        b.T1(bu)[:], b.T1(bu)[:], AF.Identity, bias=b.rndP[:],
                        scale=b.gB8[:, 4:5]),
                        waits=wts, label=f"f1.vr1.{k}")
                    eA.op(lambda e, bu=vbuf: e.activation(
                        b.T1(bu)[:], b.T1(bu)[:], AF.Identity, bias=b.rndN[:],
                        scale=1.0),
                        waits=[(eA, f"f1.vr1.{k}")], label=f"f1.vr2.{k}")
                for ih in range(2):
                    k = 2 * hc + ih
                    if k >= 2:
                        kp = k - 2
                        DQ[f"d_hW{kp % 2}"].op(lambda e, kp=kp: e.dma_start(
                            out=b.h_sp[kp // 2][:, (kp % 2)*HF:(kp % 2 + 1)*HF],
                            in_=b.T1("gC")[:, (kp % 2)*HF:(kp % 2 + 1)*HF]),
                            waits=[(eV, f"f1.amax.{kp}")], label=f"hW.{kp}")
                    wts = [(eT, f"f1.gmm.{hc}")]
                    if k >= 2:
                        wts += [(eV, f"f1.amax.{k-2}"),
                                (DQ[f"d_hW{ih}"], f"hW.{k-2}")]
                    else:
                        wts.append((eV, "ln1b.dq.31"))
                    eA.op(lambda e, hc=hc, ih=ih: e.activation(
                        b.T1("gC")[:, hsl(ih)], PS[hc % 2][:, hsl(ih)], AF.Silu,
                        bias=b.gb_t[:, hc:hc+1], scale=b.gB8[:, 0:1]),
                        waits=wts, label=f"f1.gev.{k}")
                    wts = [(eT, f"f1.vmm.{hc}")]
                    if k >= 1:
                        wts.append((eV, f"f1.h.{k-1}"))
                    else:
                        wts.append((eV, "ln1b.dq.31"))
                    eA.op(lambda e, hc=hc, ih=ih: e.activation(
                        b.T1("hE")[:], PS[(hc+1) % 2][:, hsl(ih)], AF.Identity,
                        bias=b.vb_t[:, hc:hc+1], scale=b.gB8[:, 1:2]),
                        waits=wts, label=f"f1.vev.{k}")
                    if k == 127:
                        for kp in (126, 127):
                            DQ[f"d_hW{kp % 2}"].op(lambda e, kp=kp: e.dma_start(
                                out=b.h_sp[kp // 2][:, (kp % 2)*HF:
                                                    (kp % 2 + 1)*HF],
                                in_=b.T1("gC")[:, (kp % 2)*HF:(kp % 2 + 1)*HF]),
                                waits=[(eV, f"f1.amax.{kp}")], label=f"hW.{kp}")

        def vec(E):
            for hc in range(HCH):
                for ih in range(2):
                    k = 2 * hc + ih
                    par = k % 2
                    wts = [(eA, f"f1.gr2.{k}")]
                    if k <= 1:
                        wts.append((eT, "p2.qmm.15"))
                    if k >= 4:
                        wts.append((eT, f"f1.gmm.{hc-2}"))
                    eV.op(lambda e, hc=hc, ih=ih, p=par: e.tensor_scalar(
                        b.arena[:, (hc % 2)*D + ih*HF: (hc % 2)*D + (ih+1)*HF],
                        b.T1(["hA", "hB"][p])[:], 1.0, -1.0, ALU.min, ALU.max),
                        waits=wts, label=f"f1.wqg.{k}")
                    wts = [(eA, f"f1.vr2.{k}")]
                    if k <= 1:
                        wts.append((eT, "p2.qmm.15"))
                    if k >= 4:
                        wts.append((eT, f"f1.vmm.{hc-2}"))
                    eV.op(lambda e, hc=hc, ih=ih, p=par: e.tensor_scalar(
                        b.arena[:, (2 + hc % 2)*D + ih*HF:
                                (2 + hc % 2)*D + (ih+1)*HF],
                        b.T1(["hC", "hD"][p])[:], 1.0, -1.0, ALU.min, ALU.max),
                        waits=wts, label=f"f1.wqv.{k}")
                for ih in range(2):
                    k = 2 * hc + ih
                    wts = [(eA, f"f1.gev.{k}"), (eA, f"f1.vev.{k}")]
                    eV.op(lambda e, ih=ih: e.tensor_tensor(
                        b.T1("gC")[:, hsl(ih)], b.T1("gC")[:, hsl(ih)],
                        b.T1("hE")[:], ALU.mult),
                        waits=wts, label=f"f1.h.{k}")
                    if k == 0:
                        eV.op(lambda e: e.scalar_tensor_tensor(
                            b.T1("gD")[:, 0:HF], b.T1("gC")[:, 0:HF], -1.0,
                            b.T1("gC")[:, 0:HF], ALU.mult, ALU.max),
                            waits=[(eV, "f1.h.0"), (eV, "ln1b.dq.31")],
                            label="f1.amax.0")
                    elif k == 1:
                        eV.op(lambda e: e.scalar_tensor_tensor(
                            b.T1("gD")[:, HF:T], b.T1("gC")[:, HF:T], -1.0,
                            b.T1("gC")[:, HF:T], ALU.mult, ALU.max),
                            waits=[(eV, "f1.h.1"), (eV, "f1.amax.0")],
                            label="f1.amax.1")
                    else:
                        eV.op(lambda e, ih=ih: e.scalar_tensor_tensor(
                            b.T1("gD")[:, hsl(ih)], b.T1("gC")[:, hsl(ih)], -1.0,
                            b.T1("gD")[:, hsl(ih)], ALU.mult, ALU.max),
                            waits=[(eV, f"f1.h.{k}"), (eV, f"f1.amax.{k-1}")],
                            label=f"f1.amaxa.{k}")
                        eV.op(lambda e, ih=ih: e.tensor_tensor(
                            b.T1("gD")[:, hsl(ih)], b.T1("gD")[:, hsl(ih)],
                            b.T1("gC")[:, hsl(ih)], ALU.max),
                            waits=[(eV, f"f1.amaxa.{k}")],
                            label=f"f1.amax.{k}")

        def pe(E):
            for hc in range(HCH):
                par = hc % 2
                G1, G2 = PS[hc % 2], PS[(hc + 1) % 2]
                wts = [(eV, f"f1.wqg.{2*hc+1}")]
                if hc >= 1:
                    wts.append((eA, f"f1.vev.{2*hc-1}"))
                else:
                    wts += [(eA, "ln1.ev.sxB"), (eA, "ln1.ev.siB")]
                for dch in range(NCH):
                    for tc in range(TCOL):
                        last = dch == NCH - 1 and tc == TCOL - 1
                        w2 = list(wts) if (dch == 0 and tc == 0) else []
                        if tc == 0:
                            w2.append((eV, f"ln1b.dq.{2*dch}"))
                        elif tc == 2:
                            w2.append((eV, f"ln1b.dq.{2*dch+1}"))
                        eT.op(lambda e, G1=G1, dch=dch, tc=tc, p=par: e.matmul(
                            G1[:, tc*512:(tc+1)*512],
                            b.arena[:, p*D + dch*128: p*D + (dch+1)*128],
                            b.bigbuf[:, dch*T + tc*512: dch*T + (tc+1)*512],
                            start=(dch == 0), stop=(dch == NCH - 1)),
                            waits=w2,
                            label=(f"f1.gmm.{hc}" if last else None))
                wts = [(eV, f"f1.wqv.{2*hc+1}")]
                if hc >= 1:
                    wts.append((eA, f"f1.gev.{2*hc}"))
                else:
                    wts += [(eA, "ln1.ev.mB"), (eA, "ln1.ev.rB")]
                for dch in range(NCH):
                    for tc in range(TCOL):
                        last = dch == NCH - 1 and tc == TCOL - 1
                        eT.op(lambda e, G2=G2, dch=dch, tc=tc, p=par: e.matmul(
                            G2[:, tc*512:(tc+1)*512],
                            b.arena[:, (2+p)*D + dch*128: (2+p)*D + (dch+1)*128],
                            b.bigbuf[:, dch*T + tc*512: dch*T + (tc+1)*512],
                            start=(dch == 0), stop=(dch == NCH - 1)),
                            waits=(wts if (dch == 0 and tc == 0) else ()),
                            label=(f"f1.vmm.{hc}" if last else None))

        return {"sync": sync, "scalar": act, "vector": vec,
                "tensor": pe}

    def ph_f1b(self):
        b = self
        eT, eV, eA, eP, DQ, PS = b.eT, b.eV, b.eA, b.eP, b.DQ, b.PS

        def pe(E):
            for j in range(16):
                wts = [(eV, "f1.amax.127")]
                if j == 0:
                    wts += [(eA, "f1.gev.127"), (eA, "f1.vev.127")]
                if j >= 2:
                    wts.append((eV, f"f1b.red.{j-2}"))
                eT.op(lambda e, j=j: e.transpose(
                    PS[j % 2][:, 0:128], b.T1("gD")[:, j*128:(j+1)*128],
                    b.eye_t[:]),
                    waits=wts, label=f"f1b.tr.{j}")
            eT.op(lambda e: e.transpose(PS[0][0:16, 128:256], b.hT16[:],
                                        b.eye_t[:]),
                  waits=[(eV, "f1b.sh"), (eV, "f1b.red.14")], label="f1b.rtr.0")
            eT.op(lambda e: e.transpose(PS[1][0:16, 128:256], b.dT16[:],
                                        b.eye_t[:]),
                  waits=[(eV, "f1b.dq3"), (eV, "f1b.red.15")], label="f1b.rtr.1")

        def vec(E):
            for j in range(16):
                wts = [(eT, f"f1b.tr.{j}")]
                if j == 0:
                    wts.append((eV, "f1.amax.127"))
                eV.op(lambda e, j=j: e.tensor_reduce(
                    b.t16a[:, j:j+1], PS[j % 2][0:128, 0:128], AX.X, ALU.max),
                    waits=wts, label=f"f1b.red.{j}")
            eV.op(lambda e: e.tensor_scalar(b.t16a[:], b.t16a[:], 1e-5, None,
                                            ALU.max),
                  waits=[(eV, "f1b.red.15")], label="f1b.gc")
            eV.op(lambda e: e.reciprocal(b.hT16[:], b.t16a[:]),
                  waits=[(eV, "f1b.gc")], label="f1b.ginv")
            eV.op(lambda e: e.tensor_scalar(b.hT16[:], b.hT16[:], 127.0, None,
                                            ALU.mult),
                  waits=[(eV, "f1b.ginv")], label="f1b.sh")
            eV.op(lambda e: e.tensor_scalar(b.dT16[:], b.t16a[:], b.gB8[:, 2:3],
                                            1.0 / 127.0, ALU.mult, ALU.mult),
                  waits=[(eV, "f1b.gc"), (eA, "cc.gB8")], label="f1b.dq3")

        def act(E):
            eA.op(lambda e: e.activation(b.rbounce[:], PS[0][0:16, 128:256],
                                         AF.Identity),
                  waits=[(eT, "f1b.rtr.0"), (DQ["d_bnc"], "bncR.3")],
                  label="f1b.rev.0")
            eA.op(lambda e: e.activation(b.rbounce[:], PS[1][0:16, 128:256],
                                         AF.Identity),
                  waits=[(eT, "f1b.rtr.1"), (DQ["d_bnc"], "bncW.4")],
                  label="f1b.rev.1")

        def sync(E):
            DQ["d_bnc"].op(lambda e: e.dma_start(
                out=b.bounce[4:5, :].rearrange("r (p f) -> (r p) f", p=16),
                in_=b.rbounce[:]),
                waits=[(eA, "f1b.rev.0"), (DQ["d_bnc"], "bncR.3")],
                label="bncW.4")
            DQ["d_bnc"].op(lambda e: e.dma_start(out=b.sHrow,
                                                 in_=b.bounce[4:5, :]),
                           waits=[(DQ["d_bnc"], "bncW.4"),
                                  (eT, "ln1.bc.sxB")], label="bncR.4")
            DQ["d_bnc"].op(lambda e: e.dma_start(
                out=b.bounce[5:6, :].rearrange("r (p f) -> (r p) f", p=16),
                in_=b.rbounce[:]),
                waits=[(eA, "f1b.rev.1"), (DQ["d_bnc"], "bncR.4")],
                label="bncW.5")
            DQ["d_bnc"].op(lambda e: e.dma_start(out=b.dq3row,
                                                 in_=b.bounce[5:6, :]),
                           waits=[(DQ["d_bnc"], "bncW.5"),
                                  (eT, "ln1.bc.siB")], label="bncR.5")

        return {"tensor": pe, "vector": vec, "scalar": act, "sync": sync}

    def ph_f2(self, tb_sel):
        b = self
        eT, eV, eA, eP, DQ, PS = b.eT, b.eV, b.eA, b.eP, b.DQ, b.PS

        def prev_kf(tb, fc):
            return (tb, fc - 2) if fc >= 2 else (tb - 1, 14 + fc)

        def pe(E):
            for tb in [tb_sel]:
                for t2 in range(2):
                    wts = []
                    if t2 == 0:
                        wts = [(DQ["d_bnc"], "bncR.4"), (eV, "f1b.red.14"),
                               (eA, "f1b.rev.0")]
                        if tb == 0:
                            wts += [(eT, "f1.gmm.63"), (eA, "f1.gev.127")]
                        else:
                            wts += [(eA, "f2.sHB.0"), (eV, "f2.hq1.63"),
                                    (eA, "f2.embc.0"), (eA, "f2.erbc.0")]
                    eT.op(lambda e, tb=tb, t2=t2: e.matmul(
                        b.psA[:, t2*512:(t2+1)*512], b.onesR_t[:],
                        b.sHrow[0:1, tb*TB + t2*512: tb*TB + (t2+1)*512],
                        start=True, stop=True),
                        waits=wts, label=(f"f2.shmm.{tb}" if t2 == 1 else None))
                for t2 in range(2):
                    wts = []
                    if t2 == 0:
                        wts = [(DQ["d_bnc"], "bncR.5"), (eV, "f1b.red.15"),
                               (eA, "f1b.rev.1")]
                        if tb == 0:
                            wts += [(eT, "f1.vmm.63"), (eA, "f1.vev.127")]
                        else:
                            wts += [(eA, "f2.dq3B.0"), (eV, "f2.o1.0.15")]
                    eT.op(lambda e, tb=tb, t2=t2: e.matmul(
                        b.psB[:, t2*512:(t2+1)*512], b.onesR_t[:],
                        b.dq3row[0:1, tb*TB + t2*512: tb*TB + (t2+1)*512],
                        start=True, stop=True),
                        waits=wts, label=(f"f2.dqmm.{tb}" if t2 == 1 else None))
                for fc in range(16):
                    par = fc % 2
                    G = PS[fc % 2]
                    k = tb * 16 + fc
                    wts = []
                    if fc == 0:
                        wts.append((eA, f"f2.sHB.{tb}"))
                    if fc == 1:
                        wts.append((eA, f"f2.dq3B.{tb}"))
                    if k >= 2:
                        pt, pf = prev_kf(tb, fc)
                        wts.append((eV, f"f2.o1.{pt}.{pf}"))
                    if tb == 1 and fc <= 1:
                        wts.append((eA, "f2.erbc.0"))
                    for hc in range(HCH):
                        src = b.bigbuf if hc < 32 else b.arena
                        off = (hc % 32) * TB
                        q = hc // 16
                        qb = b.T2(f"owq{(4 * fc + q) % 2}")
                        for t2 in range(2):
                            last = hc == HCH - 1 and t2 == 1
                            qlast = (hc % 16 == 15) and t2 == 1
                            w2 = list(wts) if (hc == 0 and t2 == 0) else []
                            if t2 == 0:
                                w2.append((eV, f"f2.hint.{tb}.{hc}"))
                            if hc % 16 == 0 and t2 == 0:
                                w2.append((DQ[f"d_owqR{(4*fc+q) % 2}"],
                                           f"owqR.{tb}.{fc}.{q}"))
                            lab = None
                            if last:
                                lab = f"f2.mm.{tb}.{fc}"
                            elif qlast:
                                lab = f"f2.mmq.{tb}.{fc}.{q}"
                            eT.op(lambda e, G=G, hc=hc, t2=t2, src=src, off=off,
                                  qb=qb: e.matmul(
                                      G[:, t2*512:(t2+1)*512],
                                      qb[:, (hc % 16)*128:(hc % 16 + 1)*128],
                                      src[:, off + t2*512: off + (t2+1)*512],
                                      start=(hc == 0), stop=(hc == HCH - 1)),
                                  waits=w2, label=lab)
                for i in range(2):
                    src = [b.T2("s2sum"), b.T2("s2sq")][i]
                    dep = [f"f2.ssum.{tb}.15", f"f2.ssq.{tb}.15"][i]
                    for t2 in range(2):
                        wts = []
                        if t2 == 0:
                            wts = [(eV, dep), (eV, f"f2.o1.{tb}.15")]
                            if tb == 0 and i == 0:
                                wts.append((eA, "f2.sHB.0"))
                            if tb == 1 and i == 0:
                                wts.append((eA, "f2.rsum.0"))
                            if tb == 1 and i == 1:
                                wts.append((eA, "f2.rsq.0"))
                        dst = (b.psA[0:1, 1024 + t2*512: 1024 + (t2+1)*512]
                               if i == 0 else
                               b.psB[0:1, 1024 + t2*512: 1024 + (t2+1)*512])
                        eT.op(lambda e, dst=dst, src=src, t2=t2: e.matmul(
                            dst, b.onesC_t[:], src[:, t2*512:(t2+1)*512],
                            start=True, stop=True),
                            waits=wts,
                            label=(f"f2.rmm{i}.{tb}" if t2 == 1 else None))
                for i, (row, dep) in enumerate([(b.T2("m2B")[0:1, 0:TB], f"f2.var.{tb}"),
                                                (b.T2("r2B")[0:1, 0:TB], f"f2.rstd.{tb}")]):
                    for t2 in range(2):
                        wts = []
                        if t2 == 0:
                            wts = [(eV, dep), (eV, f"f2.o1.{tb}.15"),
                                   (eV, f"f2.o1.{tb}.14")]
                            if i == 1:
                                wts.append((eA, f"f2.embc.{tb}"))
                        eT.op(lambda e, row=row, t2=t2: e.matmul(
                            b.psA[:, t2*512:(t2+1)*512], b.onesR_t[:],
                            row[0:1, t2*512:(t2+1)*512], start=True, stop=True),
                            waits=wts,
                            label=(f"f2.bc{i}.{tb}" if t2 == 1 else None))

        def act(E):
            for tb in [tb_sel]:
                eA.op(lambda e: e.activation(b.T2("sHB")[:], b.psA[:, 0:TB],
                                             AF.Identity),
                      waits=[(eT, f"f2.shmm.{tb}")], label=f"f2.sHB.{tb}")
                eA.op(lambda e: e.activation(b.T2("dq3B")[:], b.psB[:, 0:TB],
                                             AF.Identity),
                      waits=[(eT, f"f2.dqmm.{tb}")], label=f"f2.dq3B.{tb}")
                for fc in range(16):
                    par = fc % 2
                    wts = [(eV, f"f2.res.{tb}.{fc}")]
                    if fc >= 1:
                        wts.append((eV, f"f2.ssq.{tb}.{fc-1}"))
                    elif tb == 1:
                        wts.append((eV, "f2.ssq.0.15"))
                    eA.op(lambda e, p=par: e.activation(
                        b.T2("scr4")[:], b.T2(f"o3t{p}")[:], AF.Square),
                        waits=wts, label=f"f2.sq.{tb}.{fc}")
                    DQ[f"d_o3W{par}"].op(lambda e, fc=fc, tb=tb, p=par:
                                         e.dma_start(
                        out=b.o3_sp[fc*128:(fc+1)*128, tb*TB:(tb+1)*TB],
                        in_=b.T2(f"o3t{p}")[:]),
                        waits=[(eV, f"f2.res.{tb}.{fc}")],
                        label=f"o3W.{tb}.{fc}")
                eA.op(lambda e: e.activation(b.T2("m2B")[0:1, 0:TB], b.psA[0:1, 1024:2048],
                                             AF.Identity),
                      waits=[(eT, f"f2.rmm0.{tb}")], label=f"f2.rsum.{tb}")
                eA.op(lambda e: e.activation(b.T2("r2B")[0:1, 0:TB], b.psB[0:1, 1024:2048],
                                             AF.Identity),
                      waits=[(eT, f"f2.rmm1.{tb}")], label=f"f2.rsq.{tb}")
                eA.op(lambda e: e.activation(b.T2("r2B")[0:1, 0:TB], b.T2("r2B")[0:1, 0:TB], AF.Sqrt),
                      waits=[(eV, f"f2.vare.{tb}")], label=f"f2.sqrt.{tb}")
                wts = [(eT, f"f2.bc0.{tb}")]
                if tb == 1:
                    wts.append((eV, "f3.s.0.15"))
                eA.op(lambda e: e.activation(b.T2("m2B")[:], b.psA[:, 0:TB],
                                             AF.Identity),
                      waits=wts, label=f"f2.embc.{tb}")
                wts = [(eT, f"f2.bc1.{tb}")]
                if tb == 1:
                    wts.append((eV, "f3.o.0.15"))
                eA.op(lambda e: e.activation(b.T2("r2B")[:], b.psA[:, 0:TB],
                                             AF.Identity),
                      waits=wts, label=f"f2.erbc.{tb}")

        def sync(E):
            for tb in [tb_sel]:
                for hc in range(HCH):
                    par = hc % 2
                    k = tb * HCH + hc
                    wts = []
                    if tb == 0:
                        wts += [(DQ["d_hW0"], "hW.126"),
                                (DQ["d_hW1"], "hW.127")]
                    if k >= 2:
                        wts.append((eV, f"f2.hq1.{k-2}"))
                    if tb == 1 and hc <= 1:
                        wts.append((eV, f"f3.o.0.{14+hc}"))
                    DQ[f"d_hR{par}"].op(lambda e, hc=hc, tb=tb, p=par: e.dma_start(
                        out=b.T2(f"hrd{p}")[:],
                        in_=b.h_sp[hc][:, tb*TB:(tb+1)*TB]),
                        waits=wts, label=f"hR.{k}")
                for fc in range(16):
                    k = tb * 16 + fc
                    for q in range(4):
                        g = tb * 64 + 4 * fc + q
                        wts = []
                        if tb == 0:
                            wts.append((DQ["d_owqW"], "owqW.127"))
                        if g >= 2:
                            ptb, r = divmod(g - 2, 64)
                            pfc, pq = divmod(r, 4)
                            wts.append((eT, f"f2.mm.{ptb}.{pfc}" if pq == 3
                                        else f"f2.mmq.{ptb}.{pfc}.{pq}"))
                        DQ[f"d_owqR{(4*fc+q) % 2}"].op(
                            lambda e, fc=fc, q=q: e.dma_start(
                                out=b.T2(f"owq{(4*fc+q) % 2}")[:],
                                in_=b.owq_cache[fc][:, q*2048:(q+1)*2048]),
                            waits=wts, label=f"owqR.{tb}.{fc}.{q}")
                    par = fc % 2
                    wts = []
                    if tb == 0:
                        wts += [(DQ["d_x2W0"], "x2W.30"),
                                (DQ["d_x2W1"], "x2W.31")]
                    if k >= 2:
                        pt, pf = prev_kf(tb, fc)
                        wts.append((eV, f"f2.res.{pt}.{pf}"))
                    DQ[f"d_x2R{par}"].op(lambda e, fc=fc, tb=tb, p=par: e.dma_start(
                        out=b.T2(f"x2r{p}")[:],
                        in_=b.x2_sp[fc*128:(fc+1)*128, tb*TB:(tb+1)*TB]),
                        waits=wts, label=f"x2R.{tb}.{fc}")

        def vec(E):
            for tb in [tb_sel]:
                for hc in range(HCH):
                    par = hc % 2
                    k = tb * HCH + hc
                    wts = [(DQ[f"d_hR{par}"], f"hR.{k}"), (eA, f"f2.sHB.{tb}")]
                    if k >= 1:
                        pt, ph_ = (tb, hc - 1) if hc >= 1 else (tb - 1, 63)
                        wts.append((eV, f"f2.hint.{pt}.{ph_}"))
                    if k == 0:
                        wts += [(eT, "f1.gmm.63"), (eT, "f1.vmm.63"),
                                (eV, "f1.wqg.127"), (eV, "f1.wqv.127")]
                    if tb == 1 and hc == 0:
                        wts += [(eT, "f2.mm.0.15"), (eV, "f3.o.0.15")]
                    eV.op(lambda e, p=par: e.tensor_tensor(
                        b.T2("scr4")[:], b.T2(f"hrd{p}")[:], b.T2("sHB")[:],
                        ALU.mult),
                        waits=wts, label=f"f2.hq1.{k}")
                    eV.op(lambda e: e.tensor_scalar(
                        b.T2("scr4")[:], b.T2("scr4")[:], RND, -RND,
                        ALU.add, ALU.add),
                        waits=[(eV, f"f2.hq1.{k}")], label=f"f2.hrnd.{k}")
                    dst = b.bigbuf if hc < 32 else b.arena
                    off = (hc % 32) * TB
                    eV.op(lambda e, d=dst, o=off: e.tensor_scalar(
                        d[:, o:o+TB], b.T2("scr4")[:], 127.0, -128.0,
                        ALU.min, ALU.max),
                        waits=[(eV, f"f2.hrnd.{k}")], label=f"f2.hint.{tb}.{hc}")
                for fc in range(16):
                    par = fc % 2
                    k = tb * 16 + fc
                    wts = [(eT, f"f2.mm.{tb}.{fc}"), (eA, f"f2.dq3B.{tb}"),
                           (DQ[f"d_x2R{par}"], f"x2R.{tb}.{fc}")]
                    if k >= 2:
                        pt, pf = prev_kf(tb, fc)
                        wts.append((DQ[f"d_o3W{par}"], f"o3W.{pt}.{pf}"))
                    if tb == 1 and fc <= 1:
                        wts.append((DQ["d_out"], f"out.0.{14+fc}"))
                    eV.op(lambda e, p=par, tb=tb, fc=fc: e.tensor_tensor(
                        b.T2(f"o3t{p}")[:], PS[fc % 2][:, 0:TB], b.T2("dq3B")[:],
                        ALU.mult),
                        waits=wts, label=f"f2.o1.{tb}.{fc}")
                    eV.op(lambda e, p=par, fc=fc: e.tensor_scalar(
                        b.T2(f"o3t{p}")[:], b.T2(f"o3t{p}")[:],
                        b.ob_t[:, fc:fc+1], None, ALU.add),
                        waits=[(eV, f"f2.o1.{tb}.{fc}")],
                        label=f"f2.o2.{tb}.{fc}")
                    eV.op(lambda e, p=par: e.tensor_tensor(
                        b.T2(f"o3t{p}")[:], b.T2(f"o3t{p}")[:],
                        b.T2(f"x2r{p}")[:], ALU.add),
                        waits=[(eV, f"f2.o2.{tb}.{fc}")],
                        label=f"f2.res.{tb}.{fc}")
                    if fc == 0:
                        eV.op(lambda e, p=par: e.tensor_copy(
                            b.T2("s2sum")[:], b.T2(f"o3t{p}")[:]),
                            waits=[(eV, f"f2.res.{tb}.0")] + (
                                [(eT, "f2.rmm0.0")] if tb else []),
                            label=f"f2.ssum.{tb}.0")
                        eV.op(lambda e: e.tensor_copy(b.T2("s2sq")[:],
                                                      b.T2("scr4")[:]),
                              waits=[(eA, f"f2.sq.{tb}.0")] + (
                                  [(eT, "f2.rmm1.0")] if tb else []),
                              label=f"f2.ssq.{tb}.0")
                    else:
                        eV.op(lambda e, p=par: e.tensor_tensor(
                            b.T2("s2sum")[:], b.T2("s2sum")[:],
                            b.T2(f"o3t{p}")[:], ALU.add),
                            waits=[(eV, f"f2.res.{tb}.{fc}"),
                                   (eV, f"f2.ssum.{tb}.{fc-1}")],
                            label=f"f2.ssum.{tb}.{fc}")
                        eV.op(lambda e: e.tensor_tensor(
                            b.T2("s2sq")[:], b.T2("s2sq")[:], b.T2("scr4")[:],
                            ALU.add),
                            waits=[(eA, f"f2.sq.{tb}.{fc}"),
                                   (eV, f"f2.ssq.{tb}.{fc-1}")],
                            label=f"f2.ssq.{tb}.{fc}")
                eV.op(lambda e: e.tensor_scalar(b.T2("m2B")[0:1, 0:TB], b.T2("m2B")[0:1, 0:TB], 1.0 / D, None,
                                                ALU.mult),
                      waits=[(eA, f"f2.rsum.{tb}")] + (
                          [(eT, "f2.bc0.0"), (eT, "f2.bc1.0")] if tb else []),
                      label=f"f2.m2.{tb}")
                eV.op(lambda e: e.tensor_tensor(
                    b.T2("scr4")[0:1, 0:TB], b.T2("m2B")[0:1, 0:TB],
                    b.T2("m2B")[0:1, 0:TB], ALU.mult),
                      waits=[(eV, f"f2.m2.{tb}"), (eV, f"f2.ssq.{tb}.15"),
                             (eA, f"f2.sq.{tb}.15")],
                      label=f"f2.msq.{tb}")
                eV.op(lambda e: e.tensor_scalar(b.T2("r2B")[0:1, 0:TB], b.T2("r2B")[0:1, 0:TB], 1.0 / D, None,
                                                ALU.mult),
                      waits=[(eA, f"f2.rsq.{tb}")], label=f"f2.e2.{tb}")
                eV.op(lambda e: e.tensor_tensor(
                    b.T2("r2B")[0:1, 0:TB], b.T2("r2B")[0:1, 0:TB],
                    b.T2("scr4")[0:1, 0:TB], ALU.subtract),
                      waits=[(eV, f"f2.msq.{tb}"), (eV, f"f2.e2.{tb}")],
                      label=f"f2.var.{tb}")
                eV.op(lambda e: e.tensor_scalar(b.T2("r2B")[0:1, 0:TB], b.T2("r2B")[0:1, 0:TB], LN_EPS, None,
                                                ALU.add),
                      waits=[(eV, f"f2.var.{tb}")], label=f"f2.vare.{tb}")
                eV.op(lambda e: e.reciprocal(b.T2("r2B")[0:1, 0:TB], b.T2("r2B")[0:1, 0:TB]),
                      waits=[(eA, f"f2.sqrt.{tb}")], label=f"f2.rstd.{tb}")
                for fc in range(16):
                    par = fc % 2
                    k = tb * 16 + fc
                    wts = [(DQ[f"d_o3R{par}"], f"o3R.{tb}.{fc}"),
                           (eA, f"f2.embc.{tb}"), (eA, f"f2.erbc.{tb}")]
                    if k >= 2:
                        pt, pf = prev_kf(tb, fc)
                        wts.append((DQ["d_out"], f"out.{pt}.{pf}"))
                    eV.op(lambda e, p=par: e.tensor_tensor(
                        b.T2(f"o3t{p}")[:], b.T2(f"hrd{p}")[:], b.T2("m2B")[:],
                        ALU.subtract), waits=wts, label=f"f3.s.{tb}.{fc}")
                    eV.op(lambda e, p=par: e.tensor_tensor(
                        b.T2(f"o3t{p}")[:], b.T2(f"o3t{p}")[:], b.T2("r2B")[:],
                        ALU.mult),
                        waits=[(eV, f"f3.s.{tb}.{fc}")], label=f"f3.o.{tb}.{fc}")

        def sync2(E):
            for tb in [tb_sel]:
                for fc in range(16):
                    par = fc % 2
                    k = tb * 16 + fc
                    wts = [(DQ["d_o3W0"], f"o3W.{tb}.14"),
                           (DQ["d_o3W1"], f"o3W.{tb}.15"),
                           (eV, f"f2.hq1.{tb*HCH + HCH - 1}")]
                    if k >= 2:
                        pt, pf = prev_kf(tb, fc)
                        wts.append((eV, f"f3.s.{pt}.{pf}"))
                    DQ[f"d_o3R{par}"].op(lambda e, fc=fc, tb=tb, p=par: e.dma_start(
                        out=b.T2(f"hrd{p}")[:],
                        in_=b.o3_sp[fc*128:(fc+1)*128, tb*TB:(tb+1)*TB]),
                        waits=wts, label=f"o3R.{tb}.{fc}")

        def scalarD2(E):
            for tb in [tb_sel]:
                for fc in range(16):
                    par = fc % 2
                    wts = [(eV, f"f3.o.{tb}.{fc}")]
                    if tb * 16 + fc >= 1:
                        pt, pf = (tb, fc - 1) if fc >= 1 else (tb - 1, 15)
                        wts.append((DQ["d_out"], f"out.{pt}.{pf}"))
                    DQ["d_out"].op(lambda e, fc=fc, tb=tb, p=par: e.dma_start(
                        out=b.OUT[fc*128:(fc+1)*128, tb*TB:(tb+1)*TB],
                        in_=b.T2(f"o3t{p}")[:]),
                        waits=wts, label=f"out.{tb}.{fc}")

        return {"tensor": pe, "scalar": act, "sync": sync,
                "vector": vec, "sync2": sync2, "scalarD2": scalarD2}

    def ph_fin(self):
        b = self
        DQ = b.DQ

        def sync(E):
            for tp in b.taps:
                if tp == "A":
                    DQ["d_tap"].op(lambda e: e.dma_start(out=b.tap_outs["A"],
                                                         in_=b.acol2[:]),
                                   waits=[(b.eA, "cc.acol2")], label="tap.A")
                elif tp == "gvals":
                    DQ["d_tap"].op(lambda e: e.dma_start(out=b.tap_outs["gvals"],
                                                         in_=b.gvals[:]),
                                   waits=[(b.eV, "cc.ginv")], label="tap.gvals")
                elif tp == "x2":
                    DQ["d_tap"].op(lambda e: e.dma_start(out=b.tap_outs["x2"],
                                                         in_=b.x2_sp),
                                   waits=[(b.DQ["d_x2W1"], "x2W.31")],
                                   label="tap.x2")
                elif tp == "x2pre":
                    DQ["d_tap"].op(lambda e: e.dma_start(out=b.tap_outs["x2pre"],
                                                         in_=b.x2pre_sp),
                                   waits=[(b.DQ["d_x2preW1"], "x2preW.31")],
                                   label="tap.x2pre")
                elif tp == "h":
                    DQ["d_tap"].op(lambda e: e.dma_start(
                        out=b.tap_outs["h"],
                        in_=b.h_sp.rearrange("a p t -> (a p) t")),
                        waits=[(b.DQ["d_hW1"], "hW.127")], label="tap.h")
            E.wait_val(DQ["d_out"], 16 * 32)
            if b.taps:
                E.wait_val(DQ["d_tap"], 16 * len(b.taps))

        return {"sync": sync}

    def run_phases(self, block=None):
        order1 = [("p0", self.ph_p0), ("p1", self.ph_p1), ("p1b", self.ph_p1b),
                  ("pcc", self.ph_pcc), ("p2", self.ph_p2), ("ao", self.ph_ao),
                  ("ln1", self.ph_ln1), ("ln1b", self.ph_ln1b),
                  ("f1", self.ph_f1), ("f1b", self.ph_f1b)]
        order2 = [("f2t0", lambda: self.ph_f2(0)),
                  ("f2t1", lambda: self.ph_f2(1)), ("fin", self.ph_fin)]
        sect_engine = {"sync": "sync", "gpsimd": "gpsimd", "tensor": "tensor",
                       "scalar": "scalar", "vector": "vector",
                       "scalarD": "scalar", "sync2": "sync",
                       "scalarD2": "scalar", "gpsimd2": "gpsimd"}

        def emit(order):
            for pname, pfn in order:
                secs = pfn()
                for sec, fn in secs.items():
                    engname = sect_engine[sec]
                    if self.mode == "count":
                        fn(_FakeE())
                    else:
                        def runner(engobj, fn=fn):
                            for e in ([self.eT, self.eV, self.eA, self.eP]
                                      + list(self.DQ.values())):
                                e.real = engobj
                            fn(_SecE(self, engobj))
                        getattr(block, engname)(runner)

        emit(order1)
        if self.mode == "emit":
            self.era1.close()
            self.open_era2()
        else:
            for n in ["owq0", "owq1", "sHB", "dq3B", "scr4", "hrd0", "hrd1",
                      "x2r0", "x2r1", "o3t0", "o3t1", "s2sum", "s2sq",
                      "m2B", "r2B"]:
                self.t2[n] = self.arena
        emit(order2)


def build_nc(taps=()):
    b = Builder(taps=taps)
    b.open_era1()
    b.mode = "count"
    b.run_phases()
    counts = {e.name: e.cnt for e in [b.eT, b.eV, b.eA, b.eP] + list(b.DQ.values())}
    for e in [b.eT, b.eV, b.eA, b.eP] + list(b.DQ.values()):
        e.cnt = 0
    b.mode = "emit"
    for e in [b.eT, b.eV, b.eA, b.eP] + list(b.DQ.values()):
        cm = b.nc.semaphore(e.name)
        e.sem = cm.__enter__()
        b.stack.callback(cm.__exit__, None, None, None)
    with b.nc.Block() as block:
        b.run_phases(block)
    for e in [b.eT, b.eV, b.eA, b.eP] + list(b.DQ.values()):
        assert e.cnt == counts[e.name], (e.name, e.cnt, counts[e.name])
    b.era2.close()
    return b.nc



def _prep_shared(inp):
    f32 = np.float32
    W_in = np.asarray(inp["L0_attn_W_in"], f32)
    W_out = np.asarray(inp["L0_attn_W_out"], f32)
    gate_w = np.asarray(inp["L0_ffn_gate_w"], f32)
    val_w = np.asarray(inp["L0_ffn_val_w"], f32)
    out_w = np.asarray(inp["L0_ffn_out_w"], f32)

    def blocks_of_T(w):
        wt = np.ascontiguousarray(w.T)
        E = wt.shape[1]
        return np.ascontiguousarray(
            wt.reshape(D, E // 128, 128).transpose(1, 0, 2))

    d = {}
    d["WinT"] = blocks_of_T(W_in)
    d["WoT"] = blocks_of_T(W_out)
    d["GwT"] = blocks_of_T(gate_w)
    d["VwT"] = blocks_of_T(val_w)
    owt = np.ascontiguousarray(out_w.T)
    ow_blocks = np.empty((64, D, 128), f32)
    for fc in range(16):
        for jj in range(4):
            ow_blocks[4 * fc + jj] = owt[2048*jj:2048*(jj+1), 128*fc:128*(fc+1)]
    d["OwT"] = np.ascontiguousarray(ow_blocks)

    def bias_T(v, k):
        return np.ascontiguousarray(np.asarray(v, f32).reshape(k, 128).T)

    d["b_inT"] = bias_T(inp["L0_attn_b_in"], 48)
    d["b_outT"] = bias_T(inp["L0_attn_b_out"], 16)
    d["gate_bT"] = bias_T(inp["L0_ffn_gate_b"], 64)
    d["val_bT"] = bias_T(inp["L0_ffn_val_b"], 64)
    d["out_bT"] = bias_T(inp["L0_ffn_out_b"], 16)

    inv_freq = (1.0 / 10000.0 ** (np.arange(0, 1024, 2, dtype=f32)
                                  / 1024.0)).astype(f32)
    invfF = np.repeat(inv_freq, 2)
    d["invfF"] = np.ascontiguousarray(invfF.reshape(8, 128).T)
    d["onesR"] = np.ones((1, 128), f32)
    d["onesC"] = np.ones((128, 1), f32)
    d["eye"] = np.eye(128, dtype=f32)
    P = np.zeros((128, 128), f32)
    for j in range(64):
        P[2*j+1, 2*j] = -1.0
        P[2*j, 2*j+1] = 1.0
    d["perm"] = P
    d["zrowD"] = np.zeros((1, T), f32)
    return d


def _prep_core(inp, shared, core):
    f32 = np.float32
    X = np.asarray(inp["L0_attn_X"], f32)
    bsel = np.zeros((128, 2), f32)
    bdx = core // 4
    q = core % 4
    bsel[:, bdx] = 1.0
    dcore = dict(shared)
    dcore["XT"] = np.ascontiguousarray(X[bdx, q*T:(q+1)*T, :].T)
    dcore["ramp"] = (np.arange(T, dtype=f32) + q * T)[None, :]
    dcore["bsel"] = bsel
    gate_w = np.asarray(inp["L0_ffn_gate_w"], f32)
    val_w = np.asarray(inp["L0_ffn_val_w"], f32)
    out_w = np.asarray(inp["L0_ffn_out_w"], f32)

    def slice8(wT, c):
        ncols = wT.shape[1] // 8
        sl = wT[:, c*ncols:(c+1)*ncols]
        rows = sl.shape[0]
        npieces = rows * ncols // (D * 128)
        return np.ascontiguousarray(
            sl.reshape(rows // D, D, ncols // 128, 128)
            .transpose(0, 2, 1, 3).reshape(npieces, D, 128))

    dcore["gW8"] = slice8(np.ascontiguousarray(gate_w.T), core)
    dcore["vW8"] = slice8(np.ascontiguousarray(val_w.T), core)
    ow8 = np.ascontiguousarray(out_w.T)
    sl = ow8[:, core*256:(core+1)*256]
    dcore["oW8"] = np.ascontiguousarray(
        sl.reshape(4, D, 2, 128).transpose(0, 2, 1, 3).reshape(8, D, 128))
    return dcore


def kernel(**inputs):
    shared = _prep_shared(inputs)
    in_maps = [_prep_core(inputs, shared, c) for c in range(8)]
    nc = build_nc()
    res = run_bass_kernel_spmd(nc, in_maps, core_ids=list(range(8)))
    out = np.empty((2, 8192, D), np.float32)
    for c in range(8):
        bdx, q = c // 4, c % 4
        out[bdx, q*T:(q+1)*T, :] = res.results[c]["OUT"].T
    return out
